# version 01
# baseline (speedup 1.0000x reference)
import contextlib
import numpy as np
import concourse.bass as bass
import concourse.mybir as mybir
from concourse.bass_utils import run_bass_kernel_spmd

F32 = mybir.dt.float32
F32R = mybir.dt.float32r
BF16 = mybir.dt.bfloat16
I32 = mybir.dt.int32
U32 = mybir.dt.uint32
AF = mybir.ActivationFunctionType
ALU = mybir.AluOpType

C = 1024
T = 4096
NCH = 8
H = 16
DH = 64
LNEPS = 1e-5
MAGIC = 1.5 * 2.0 ** 23


class Prog:
    QUEUES = ("pe", "dve", "act", "pool", "sp")

    def __init__(self, name="prog"):
        self.nc = bass.Bass("TRN2", target_bir_lowering=False)
        self.es = contextlib.ExitStack()
        self.items = {q: [] for q in self.QUEUES}
        self.sems = {}
        self.cnt = {}
        self.waited = {q: {} for q in self.QUEUES}
        self._final = []
        self.selfsync_default = True

    def sbuf(self, name, shape, dt):
        return self.es.enter_context(self.nc.sbuf_tensor(name, list(shape), dt))

    def psum(self, name, shape, dt=F32):
        return self.es.enter_context(self.nc.psum_tensor(name, list(shape), dt))

    def din(self, name, shape, dt=F32):
        return self.nc.dram_tensor(name, list(shape), dt, kind="ExternalInput").ap()

    def dout(self, name, shape, dt=F32):
        return self.nc.dram_tensor(name, list(shape), dt, kind="ExternalOutput").ap()

    def _sem(self, key):
        if key not in self.sems:
            self.sems[key] = self.es.enter_context(self.nc.semaphore("sem_" + key))
            self.cnt[key] = 0
        return self.sems[key]

    def op(self, q, fn, deps=(), tick=True, selfsync=None):
        key = q
        sem = self._sem(key)
        waits = self._mk_waits(q, deps)
        if selfsync is None:
            selfsync = getattr(self, "selfsync_default", False)
        if selfsync and self.cnt.get(key, 0) > 0 and tick:
            waits = [(sem, self.cnt[key])] + waits
        if tick:
            self.cnt[key] += 1
            val = self.cnt[key]

            def clo(eng, fn=fn, waits=waits, sem=sem):
                for w in waits:
                    eng.wait_ge(*w)
                ins = fn(eng)
                if ins is None:
                    ins = eng.nop()
                ins.then_inc(sem, 1)
            self.items[q].append(clo)
            return (key, val, 1)
        else:
            def clo(eng, fn=fn, waits=waits):
                for w in waits:
                    eng.wait_ge(*w)
                fn(eng)
            self.items[q].append(clo)
            return None

    def tick(self, q, deps=()):
        return self.op(q, lambda e: e.nop(), deps=deps)

    def dma(self, q, out, in_, deps=(), **kw):
        key = q + "d"
        sem = self._sem(key)
        waits = self._mk_waits(q, deps)
        self.cnt[key] += 16
        val = self.cnt[key]

        def clo(eng, out=out, in_=in_, waits=waits, sem=sem, kw=kw):
            for w in waits:
                eng.wait_ge(*w)
            eng.dma_start(out=out, in_=in_, **kw).then_inc(sem, 16)
        self.items[q].append(clo)
        return (key, val, 16)

    def _mk_waits(self, q, deps):
        waits = []
        for d in deps:
            if d is None:
                continue
            key, val, scale = d
            if key == q:
                continue
            if self.waited[q].get(key, 0) >= val:
                continue
            self.waited[q][key] = val
            waits.append((self.sems[key], val))
        return waits

    def last(self, q):
        key = q
        return (key, self.cnt.get(key, 0), 1)

    def lastd(self, q):
        key = q + "d"
        return (key, self.cnt.get(key, 0), 16)

    def finalize(self, deps):
        self._final = list(deps)

    def build(self):
        block = self.es.enter_context(self.nc.Block())
        prog = self

        for d in self._final:
            if d is None:
                continue
            key, val, scale = d
            sem = self.sems[key]
            self.items["pool"].append(lambda eng, sem=sem, val=val: eng.wait_ge(sem, val))

        @block.tensor
        def _(eng):
            for f in prog.items["pe"]:
                f(eng)

        @block.vector
        def _(eng):
            for f in prog.items["dve"]:
                f(eng)

        @block.scalar
        def _(eng):
            for f in prog.items["act"]:
                f(eng)

        @block.gpsimd
        def _(eng):
            for f in prog.items["pool"]:
                f(eng)

        @block.sync
        def _(eng):
            for f in prog.items["sp"]:
                f(eng)

        return self.nc


def chunked(ap128x, a):
    return ap128x[128 * a:128 * (a + 1), :]


def rearr(ap, p=128):
    return ap.rearrange("(a p) w -> p a w", p=p)



def emit_layernorm(pr, W, xt, xn, sqr, rows, ones_f, onesrow_f, psStat, psB, ps,
                   xdep, gamma=None, beta=None):
    rowA, rowB, rowC = rows
    b0, b1 = psStat
    b2, b3 = psB
    hrot = [pr.last("pe"), pr.last("pe")]
    for a in range(NCH):
        sl = a % 2
        hsq = pr.op("act", lambda s, a=a, sl=sl: s.activation(sqr[:, sl, :W], xt[:, a, :W], AF.Square,
                                                              bias=0.0, scale=1.0), deps=[xdep, hrot[sl]])
        pr.op("pe", lambda t, a=a: t.matmul(ps.t[b0][0:1, :W], ones_f[:], xt[:, a, :W],
                                            start=(a == 0), stop=(a == NCH - 1)),
              deps=[xdep, ps.free[b0]], tick=False)
        hpe = pr.op("pe", lambda t, a=a, sl=sl: t.matmul(ps.t[b1][0:1, :W], ones_f[:], sqr[:, sl, :W],
                                                         start=(a == 0), stop=(a == NCH - 1)),
                    deps=[hsq, ps.free[b1]])
        hrot[sl] = hpe
    hmm = pr.last("pe")
    ps.data[b0] = hmm
    ps.data[b1] = hmm
    pr.op("dve", lambda v: v.tensor_scalar(rowA[:, :W], ps.t[b0][0:1, :W], 1.0 / C, None, ALU.mult),
          deps=[hmm], tick=False)
    h1 = pr.op("dve", lambda v: v.tensor_scalar(rowB[:, :W], ps.t[b1][0:1, :W], 1.0 / C, None, ALU.mult))
    ps.free[b0] = h1
    ps.free[b1] = h1
    h2 = pr.op("act", lambda s: s.activation(rowC[:, :W], rowA[:, :W], AF.Square, bias=0.0, scale=1.0), deps=[h1])
    pr.op("dve", lambda v: v.tensor_tensor(rowB[:, :W], rowB[:, :W], rowC[:, :W], ALU.subtract), deps=[h2], tick=False)
    pr.op("dve", lambda v: v.tensor_scalar(rowB[:, :W], rowB[:, :W], LNEPS, None, ALU.add), tick=False)
    h3 = pr.op("dve", lambda v: v.reciprocal(rowB[:, :W], rowB[:, :W]))
    h4 = pr.op("act", lambda s: s.activation(rowC[:, :W], rowB[:, :W], AF.Sqrt, bias=0.0, scale=1.0), deps=[h3])
    pr.op("pe", lambda t: t.matmul(ps.t[b2][:, :W], onesrow_f[:], rowA[:, :W], start=True, stop=True),
          deps=[h4, ps.free[b2]], tick=False)
    h6 = pr.op("pe", lambda t: t.matmul(ps.t[b3][:, :W], onesrow_f[:], rowC[:, :W], start=True, stop=True),
               deps=[ps.free[b3]])
    ps.data[b2] = h6
    ps.data[b3] = h6
    for a in range(NCH):
        pr.op("dve", lambda v, a=a: v.tensor_tensor(xn[:, a, :W], xt[:, a, :W], ps.t[b2][:, :W], ALU.subtract),
              deps=[h6], tick=False)
        hxn = pr.op("dve", lambda v, a=a: v.tensor_tensor(xn[:, a, :W], xn[:, a, :W], ps.t[b3][:, :W], ALU.mult),
                    tick=(a == NCH - 1 and gamma is None))
        if gamma is not None:
            hxn = pr.op("dve", lambda v, a=a: v.tensor_scalar(
                xn[:, a, :W], xn[:, a, :W], gamma[:, a, 0:1], beta[:, a, 0:1], ALU.mult, ALU.add),
                tick=(a == NCH - 1))
    ps.free[b2] = hxn
    ps.free[b3] = hxn
    return hxn


class Banks:

    def __init__(self, tensors):
        self.t = tensors
        self.data = [None] * len(tensors)
        self.free = [None] * len(tensors)



def build_launch1(TW=256, phases=99):
    NT = T // TW
    pr = Prog("l1")
    xT = pr.din("xT", [C, T])
    maa = pr.din("maa", [C, 4])
    w1 = pr.din("w1", [C, 96])
    w2s = pr.din("w2s", [96, C])
    Wr = pr.din("Wr", [C, C])
    Wk = pr.din("Wk", [C, C])
    Wvg = pr.din("Wvg", [C, 6 * C])
    cones = pr.din("cones", [128, 1])

    o_xn = pr.dout("xnT", [C, T])
    o_r = pr.dout("rT", [C, T])
    o_w = pr.dout("wT", [C, T])
    o_v = pr.dout("vT", [C, T])
    o_ffn = pr.dout("ffnT", [2 * C, T // 2], U32)
    o_dbg = pr.dout("dbg", [4, T])

    with pr.es:
        w1b = pr.sbuf("w1b", [128, NCH, 96], BF16)
        w2b = pr.sbuf("w2b", [96, C], BF16)
        Wrb = pr.sbuf("Wrb", [128, NCH, C], BF16)
        Wkb = pr.sbuf("Wkb", [128, NCH, C], BF16)
        Wvgb = pr.sbuf("Wvgb", [128, NCH, 6 * C], BF16)
        maas = pr.sbuf("maas", [128, NCH, 4], F32)
        ones_f = pr.sbuf("ones_f", [128, 1], F32)
        onesrow_f = pr.sbuf("onesrow_f", [1, 128], F32)
        onesrow_r = pr.sbuf("onesrow_r", [1, 128], F32R)

        xt = pr.sbuf("xt", [128, NCH, TW], F32)
        xn = pr.sbuf("xn", [128, NCH, TW], F32)
        xx = pr.sbuf("xx", [128, NCH, TW], F32)
        sqr = pr.sbuf("sqr", [128, 2, TW], F32)
        carry = pr.sbuf("carry", [128, NCH, 1], F32)
        rowA = pr.sbuf("rowA", [1, TW], F32)
        rowB = pr.sbuf("rowB", [1, TW], F32)
        rowC = pr.sbuf("rowC", [1, TW], F32)
        rowAr = pr.sbuf("rowAr", [1, TW], F32R)
        rowBr = pr.sbuf("rowBr", [1, TW], F32R)
        xxxb = pr.sbuf("xxxb", [128, NCH, TW], BF16)
        mtb = pr.sbuf("mtb", [96, TW], BF16)
        tmp1 = pr.sbuf("tmp1", [128, TW], F32)
        tmp2 = pr.sbuf("tmp2", [128, TW], F32)
        xkb = pr.sbuf("xkb", [128, NCH, TW], BF16)
        xvb = pr.sbuf("xvb", [128, NCH, TW], BF16)
        xrb = pr.sbuf("xrb", [128, NCH, TW], BF16)
        gsil = pr.sbuf("gsil", [128, 2, TW], F32)
        evac = pr.sbuf("evac", [128, 4, TW], F32)
        evb = pr.sbuf("evb", [128, 2, TW], BF16)

        ps = Banks([pr.psum(f"ps{i}", [128, 512]) for i in range(8)])

        hw = []
        hw.append(pr.dma("pool", w1b[:], w1.rearrange("(a p) n -> p a n", p=128)))
        hw.append(pr.dma("pool", w2b[:], w2s))
        hw.append(pr.dma("pool", Wrb[:], Wr.rearrange("(a p) n -> p a n", p=128)))
        hw.append(pr.dma("pool", Wkb[:], Wk.rearrange("(a p) n -> p a n", p=128)))
        hw.append(pr.dma("pool", Wvgb[:], Wvg.rearrange("(a p) n -> p a n", p=128)))
        hw.append(pr.dma("pool", maas[:], maa.rearrange("(a p) n -> p a n", p=128)))
        hw.append(pr.dma("pool", ones_f[:], cones))
        hwl = hw[-1]
        pr.op("dve", lambda v: v.memset(onesrow_f[:], 1.0), deps=[hwl], tick=False)
        pr.op("dve", lambda v: v.tensor_copy(onesrow_r[:], onesrow_f[:]), tick=False)
        pr.op("dve", lambda v: v.memset(carry[:], 0.0), tick=False)
        hpre = pr.tick("dve", deps=hw)

        hxstore = None
        hmixfree = [hpre] * NCH
        hxnfree = hpre
        hxxfree = hpre

        for k in range(NT):
            t0 = k * TW
            hx = pr.dma("sp", xt[:], xT[:, t0:t0 + TW].rearrange("(a p) w -> p a w", p=128),
                        deps=[hxnfree])
            hrot = [pr.last("pe"), pr.last("pe")]
            for a in range(NCH):
                sl = a % 2
                hsq = pr.op("act", lambda s, a=a, sl=sl: s.activation(sqr[:, sl, :], xt[:, a, :], AF.Square,
                                                                      bias=0.0, scale=1.0), deps=[hx, hrot[sl]])
                pr.op("pe", lambda t, a=a: t.matmul(ps.t[0][0:1, :TW], ones_f[:], xt[:, a, :],
                                                    start=(a == 0), stop=(a == NCH - 1)),
                      deps=[hx, ps.free[0]], tick=False)
                hpe = pr.op("pe", lambda t, a=a, sl=sl: t.matmul(ps.t[1][0:1, :TW], ones_f[:], sqr[:, sl, :],
                                                                 start=(a == 0), stop=(a == NCH - 1)),
                            deps=[hsq, ps.free[1]])
                hrot[sl] = hpe
            hmm = pr.last("pe")
            ps.data[0] = hmm
            ps.data[1] = hmm
            pr.op("dve", lambda v: v.tensor_scalar(rowA[:], ps.t[0][0:1, :TW], 1.0 / C, None, ALU.mult),
                  deps=[hmm], tick=False)
            h1 = pr.op("dve", lambda v: v.tensor_scalar(rowB[:], ps.t[1][0:1, :TW], 1.0 / C, None, ALU.mult))
            ps.free[0] = h1
            ps.free[1] = h1
            h2 = pr.op("act", lambda s: s.activation(rowC[:], rowA[:], AF.Square, bias=0.0, scale=1.0), deps=[h1])
            pr.op("dve", lambda v: v.tensor_tensor(rowB[:], rowB[:], rowC[:], ALU.subtract), deps=[h2], tick=False)
            pr.op("dve", lambda v: v.tensor_scalar(rowB[:], rowB[:], LNEPS, None, ALU.add), tick=False)
            h3 = pr.op("dve", lambda v: v.reciprocal(rowB[:], rowB[:]))
            h4 = pr.op("act", lambda s: s.activation(rowC[:], rowB[:], AF.Sqrt, bias=0.0, scale=1.0), deps=[h3])
            h5 = pr.tick("dve", deps=[h4])
            pr.dma("sp", o_dbg[0:1, t0:t0 + TW], rowA[:], deps=[h5])
            pr.dma("sp", o_dbg[1:2, t0:t0 + TW], rowB[:], deps=[h5])
            pr.dma("sp", o_dbg[2:3, t0:t0 + TW], rowC[:], deps=[h5])
            pr.op("pe", lambda t: t.matmul(ps.t[2][:, :TW], onesrow_f[:], rowA[:], start=True, stop=True),
                  deps=[h4, ps.free[2]], tick=False)
            h6 = pr.op("pe", lambda t: t.matmul(ps.t[3][:, :TW], onesrow_f[:], rowC[:], start=True, stop=True),
                       deps=[ps.free[3]])
            ps.data[2] = h6
            ps.data[3] = h6
            for a in range(NCH):
                pr.op("dve", lambda v, a=a: v.tensor_tensor(xn[:, a, :], xt[:, a, :], ps.t[2][:, :TW], ALU.subtract),
                      deps=[h6, hxnfree], tick=False)
                hxn = pr.op("dve", lambda v, a=a: v.tensor_tensor(xn[:, a, :], xn[:, a, :], ps.t[3][:, :TW], ALU.mult),
                            tick=(a == NCH - 1))
            ps.free[2] = hxn
            ps.free[3] = hxn
            hxnst = pr.dma("sp", o_xn[:, t0:t0 + TW].rearrange("(a p) w -> p a w", p=128), xn[:], deps=[hxn])
            if phases < 2:
                hguard = pr.tick("dve", deps=[pr.lastd("sp"), pr.last("pe"), pr.last("act")])
                hxnfree = hguard
                hxxfree = hguard
                hmixfree = [hguard] * NCH
                continue
            for a in range(NCH):
                pr.op("dve", lambda v, a=a: v.tensor_tensor(xx[:, a, 1:TW], xn[:, a, 0:TW - 1], xn[:, a, 1:TW],
                                                            ALU.subtract), deps=[hxxfree], tick=False)
                hxx = pr.op("dve", lambda v, a=a: v.tensor_tensor(xx[:, a, 0:1], carry[:, a, :], xn[:, a, 0:1],
                                                                  ALU.subtract), tick=(a == NCH - 1))
            for a in range(NCH):
                pr.op("dve", lambda v, a=a: v.tensor_copy(carry[:, a, :], xn[:, a, TW - 1:TW]), tick=False)
            for a in range(NCH):
                hxxx = pr.op("dve", lambda v, a=a: v.scalar_tensor_tensor(xxxb[:, a, :], xx[:, a, :], maas[:, a, 0:1],
                                                                          xn[:, a, :], ALU.mult, ALU.add),
                             tick=(a == NCH - 1))
            if phases < 3:
                hguard = pr.tick("dve", deps=[pr.lastd("sp"), pr.last("pe"), pr.last("act")])
                hxnfree = hguard
                hxxfree = hguard
                hmixfree = [hguard] * NCH
                continue
            for a in range(NCH):
                hm1 = pr.op("pe", lambda t, a=a: t.matmul(ps.t[4][0:96, :TW], w1b[:, a, :], xxxb[:, a, :],
                                                          start=(a == 0), stop=(a == NCH - 1)),
                            deps=[hxxx, ps.free[4]], tick=(a == NCH - 1))
            ps.data[4] = hm1
            htanh = pr.op("act", lambda s: s.activation(mtb[:, :], ps.t[4][0:96, :TW], AF.Tanh, bias=0.0, scale=1.0),
                          deps=[hm1])
            ps.free[4] = htanh
            for a in range(NCH):
                hk = pr.op("pe", lambda t, a=a: t.matmul(ps.t[5][:, :TW], w2b[0:32, 128 * a:128 * (a + 1)],
                                                         mtb[0:32, :], start=True, stop=True),
                           deps=[htanh, ps.free[5], hmixfree[a]])
                hv = pr.op("pe", lambda t, a=a: t.matmul(ps.t[6][:, :TW], w2b[32:64, 128 * a:128 * (a + 1)],
                                                         mtb[32:64, :], start=True, stop=True), deps=[ps.free[6]])
                hr = pr.op("pe", lambda t, a=a: t.matmul(ps.t[7][:, :TW], w2b[64:96, 128 * a:128 * (a + 1)],
                                                         mtb[64:96, :], start=True, stop=True), deps=[ps.free[7]])
                for l, (dst, hsrc, bank, mc) in enumerate(((xkb, hk, 5, 1), (xvb, hv, 6, 2), (xrb, hr, 7, 3))):
                    pr.op("dve", lambda v, a=a, dst=dst, bank=bank, mc=mc:
                          v.tensor_scalar(tmp1[:], ps.t[bank][:, :TW], maas[:, a, mc:mc + 1], None, ALU.add),
                          deps=[hsrc, hxx], tick=False)
                    pr.op("dve", lambda v, a=a: v.tensor_tensor(tmp2[:], tmp1[:], xx[:, a, :], ALU.mult), tick=False)
                    hdone = pr.op("dve", lambda v, a=a, dst=dst: v.tensor_tensor(dst[:, a, :], tmp2[:],
                                                                                 xn[:, a, :], ALU.add))
                    ps.free[bank] = hdone
            hmixed = pr.last("dve")
            def proj(dst_dram, Wb, src, banks, tag):
                hst = []
                for m in range(NCH):
                    bk = banks[m % 2]
                    deps = [hmixed, ps.free[bk]]
                    for a in range(NCH):
                        hpe = pr.op("pe", lambda t, m=m, a=a, bk=bk, Wb=Wb, src=src:
                                    t.matmul(ps.t[bk][:, :TW], Wb[:, a, 128 * m:128 * (m + 1)], src[:, a, :],
                                             start=(a == 0), stop=(a == NCH - 1)),
                                    deps=deps if a == 0 else (), tick=(a == NCH - 1))
                    ps.data[bk] = hpe
                    sl = m % 4
                    hev = pr.op("dve", lambda v, bk=bk, sl=sl: v.tensor_copy(evac[:, sl, :], ps.t[bk][:, :TW]),
                                deps=[hpe, pr.stfree.get(("e", sl))])
                    ps.free[bk] = hev
                    hst.append(pr.dma("sp", dst_dram[128 * m:128 * (m + 1), t0:t0 + TW], evac[:, sl, :], deps=[hev]))
                    pr.stfree[("e", sl)] = hst[-1]
                return hst

            pr.stfree = getattr(pr, "stfree", {})
            if phases < 4:
                hguard = pr.tick("dve", deps=[pr.lastd("sp"), pr.last("pe"), pr.last("act")])
                hxnfree = hguard
                hxxfree = hguard
                hmixfree = [hguard] * NCH
                continue
            proj(o_r, Wrb, xrb, (0, 1), "r")
            proj(o_w, Wkb, xkb, (2, 3), "w")
            for m in range(24 if phases >= 5 else 0):
                bg = 4 + (m % 2)
                bvf = 6 + (m % 2)
                for a in range(NCH):
                    hg = pr.op("pe", lambda t, m=m, a=a, bg=bg: t.matmul(
                        ps.t[bg][:, :TW], Wvgb[:, a, 128 * (24 + m):128 * (25 + m)], xvb[:, a, :],
                        start=(a == 0), stop=(a == NCH - 1)),
                        deps=[hmixed, ps.free[bg]] if a == 0 else (), tick=(a == NCH - 1))
                ps.data[bg] = hg
                gs = m % 2
                hsil = pr.op("act", lambda s, gs=gs, bg=bg: s.activation(gsil[:, gs, :], ps.t[bg][:, :TW], AF.Silu,
                                                                         bias=0.0, scale=1.0),
                             deps=[hg, pr.stfree.get(("g", gs))])
                ps.free[bg] = hsil
                for a in range(NCH):
                    hvf = pr.op("pe", lambda t, m=m, a=a, bvf=bvf: t.matmul(
                        ps.t[bvf][:, :TW], Wvgb[:, a, 128 * m:128 * (m + 1)], xvb[:, a, :],
                        start=(a == 0), stop=(a == NCH - 1)),
                        deps=[ps.free[bvf]] if a == 0 else (), tick=(a == NCH - 1))
                ps.data[bvf] = hvf
                if m < 8:
                    sl = m % 4
                    hev = pr.op("dve", lambda v, bvf=bvf, gs=gs, sl=sl: v.tensor_tensor(
                        evac[:, sl, :], ps.t[bvf][:, :TW], gsil[:, gs, :], ALU.mult),
                        deps=[hvf, hsil, pr.stfree.get(("e", sl))])
                    ps.free[bvf] = hev
                    pr.stfree[("g", gs)] = hev
                    hst = pr.dma("sp", o_v[128 * m:128 * (m + 1), t0:t0 + TW], evac[:, sl, :], deps=[hev])
                    pr.stfree[("e", sl)] = hst
                else:
                    sl = m % 2
                    hev = pr.op("dve", lambda v, bvf=bvf, gs=gs, sl=sl: v.tensor_tensor(
                        evb[:, sl, :], ps.t[bvf][:, :TW], gsil[:, gs, :], ALU.mult),
                        deps=[hvf, hsil, pr.stfree.get(("f", sl))])
                    ps.free[bvf] = hev
                    pr.stfree[("g", gs)] = hev
                    hst = pr.dma("sp", o_ffn[128 * (m - 8):128 * (m - 7), t0 // 2:(t0 + TW) // 2],
                                 evb[:, sl, :].bitcast(U32), deps=[hev])
                    pr.stfree[("f", sl)] = hst
            hguard = pr.tick("dve", deps=[pr.lastd("sp"), pr.last("pe"), pr.last("act")])
            hxnfree = hguard
            hxxfree = hguard
            hmixfree = [hguard] * NCH

        pr.finalize([pr.lastd("sp"), pr.lastd("pool"), pr.last("dve"), pr.last("pe"), pr.last("act")])
    return pr



def build_launch2():
    TSEG = 1024
    NSEG = T // TSEG
    L = 64
    NCHK = TSEG // L
    pr = Prog("l2")
    rT = pr.din("rT", [C, T])
    wT = pr.din("wT", [C, T])
    vT = pr.din("vT", [C, T])
    ffnT = pr.din("ffnT", [2 * C, T // 2], U32)
    Wo = pr.din("Wo", [3 * C, C])
    uvec = pr.din("uvec", [C, 1])
    lng = pr.din("lng", [C, 1])
    lnb = pr.din("lnb", [C, 1])
    cones = pr.din("cones", [128, 1])
    cmask = pr.din("cmask", [128, TSEG])
    ctri = pr.din("ctri", [128, 64])
    cb2 = pr.din("cb2", [128, 128])
    cones2 = pr.din("cones2", [128, 64])
    cid = pr.din("cid", [128, 128])
    o_s = pr.dout("sT", [C, T])
    o_oscr = pr.dout("o_scr", [C, T])
    o_dsb = pr.dout("o_dsb", [128, T])
    o_ruk = pr.dout("o_ruk", [128, 1024])
    o_us = pr.dout("o_us", [128, NCH])

    with pr.es:
        Wob = pr.sbuf("Wob", [128, 24, C], BF16)
        us = pr.sbuf("us", [128, NCH, 1], F32)
        lngs = pr.sbuf("lngs", [128, NCH, 1], F32)
        lnbs = pr.sbuf("lnbs", [128, NCH, 1], F32)
        ones_f = pr.sbuf("ones_f", [128, 1], F32)
        onesrow_f = pr.sbuf("onesrow_f", [1, 128], F32)
        ones64b = pr.sbuf("ones64b", [1, 64], BF16)
        maskseg = pr.sbuf("maskseg", [128, TSEG], F32)
        trib = pr.sbuf("trib", [128, 64], F32)
        b2b = pr.sbuf("b2b", [128, 128], BF16)
        ones2b = pr.sbuf("ones2b", [128, 64], BF16)
        idb = pr.sbuf("idb", [128, 128], BF16)

        wt = pr.sbuf("wt", [128, TSEG], F32)
        ew = pr.sbuf("ew", [128, TSEG], F32)
        cc = pr.sbuf("cc", [128, TSEG], F32)
        kk = pr.sbuf("kk", [128, TSEG], F32)
        tmpa = pr.sbuf("tmpa", [128, TSEG], F32)
        tmpb = pr.sbuf("tmpb", [128, TSEG], F32)
        rr = pr.sbuf("rr", [128, TSEG], F32)
        vv = pr.sbuf("vv", [128, TSEG], F32)
        rtb = pr.sbuf("rtb", [128, TSEG], BF16)
        kxb = pr.sbuf("kxb", [128, TSEG], BF16)
        rt2b = pr.sbuf("rt2b", [128, TSEG], BF16)
        kx2b = pr.sbuf("kx2b", [128, TSEG], BF16)
        tmpc = pr.sbuf("tmpc", [128, TSEG], F32)
        khb = pr.sbuf("khb", [128, TSEG], BF16)
        vb = pr.sbuf("vb", [128, TSEG], BF16)
        vtm = pr.sbuf("vtm", [128, 2, TSEG // 128, 64], BF16)
        khtm = pr.sbuf("khtm", [128, 2, TSEG // 128, 64], BF16)
        dec = pr.sbuf("dec", [128, NCHK], F32)
        amb = pr.sbuf("amb", [128, 2, 64], BF16)
        S2 = pr.sbuf("S2", [128, 64], F32)
        S2b = pr.sbuf("S2b", [128, 2, 64], BF16)
        dsb = pr.sbuf("dsb", [128, TSEG], F32)
        osb = pr.sbuf("osb", [128, TSEG], F32)
        ot = pr.sbuf("ot", [128, NCH, 512], F32)
        on = pr.sbuf("on", [128, NCH, 512], F32)
        onb = pr.sbuf("onb", [128, NCH, 512], BF16)
        ffb = pr.sbuf("ffb", [128, 16, 512], BF16)
        sqr = pr.sbuf("sqr", [128, 2, 512], F32)
        rowA = pr.sbuf("rowA", [1, 512], F32)
        rowB = pr.sbuf("rowB", [1, 512], F32)
        rowC = pr.sbuf("rowC", [1, 512], F32)
        evac = pr.sbuf("evac", [128, 4, 512], F32)

        ps = Banks([pr.psum(f"ps{i}", [128, 512]) for i in range(7)])
        psT = pr.psum("psT", [128, 64], BF16)

        hc = []
        hc.append(pr.dma("pool", Wob[:], Wo.rearrange("(a p) n -> p a n", p=128)))
        hc.append(pr.dma("pool", us[:], uvec.rearrange("(a p) n -> p a n", p=128), allow_slow_non_contiguous=True))
        hc.append(pr.dma("pool", lngs[:], lng.rearrange("(a p) n -> p a n", p=128), allow_slow_non_contiguous=True))
        hc.append(pr.dma("pool", lnbs[:], lnb.rearrange("(a p) n -> p a n", p=128), allow_slow_non_contiguous=True))
        hc.append(pr.dma("pool", ones_f[:], cones))
        hc.append(pr.dma("pool", maskseg[:], cmask))
        hc.append(pr.dma("pool", trib[:], ctri))
        hc.append(pr.dma("pool", b2b[:], cb2))
        hc.append(pr.dma("pool", ones2b[:], cones2))
        hc.append(pr.dma("pool", idb[:], cid))
        hcl = hc[-1]
        pr.op("dve", lambda v: v.memset(onesrow_f[:], 1.0), deps=[hcl], tick=False)
        hpre = pr.op("dve", lambda v: v.memset(ones64b[:], 1.0))
        pr.dma("sp", o_us, us[:].rearrange("p a n -> p (a n)"), deps=[hpre])

        for hp in range(NCH):
            r0 = 128 * hp
            hS = pr.op("dve", lambda v: v.memset(S2[:], 0.0), deps=[hpre])
            hosb_free = hpre
            ambfree = {0: hpre, 1: hpre}
            sbfree = {}
            for sg in range(NSEG):
                t0 = sg * TSEG
                segdeps = [pr.last("act"), pr.last("dve"), pr.last("pe")]
                hw_ = pr.dma("sp", wt[:], wT[r0:r0 + 128, t0:t0 + TSEG], deps=segdeps)
                hr_ = pr.dma("sp", rr[:], rT[r0:r0 + 128, t0:t0 + TSEG], deps=segdeps)
                hv_ = pr.dma("sp", vv[:], vT[r0:r0 + 128, t0:t0 + TSEG], deps=segdeps)
                hew = pr.op("act", lambda s: s.activation(ew[:], wt[:], AF.Exp, bias=0.0, scale=1.0), deps=[hw_])
                hcc = pr.op("dve", lambda v: v.tensor_tensor_scan(cc[:], maskseg[:], ew[:], 0.0,
                                                                  ALU.mult, ALU.subtract), deps=[hew])
                he2 = pr.op("act", lambda s: s.activation(tmpa[:], ew[:], AF.Exp, bias=0.0, scale=-1.0), deps=[hew])
                hkk = pr.op("dve", lambda v: v.tensor_scalar(kk[:], tmpa[:], -1.0, 1.0, ALU.mult, ALU.add),
                            deps=[he2])
                cc3 = cc[:].rearrange("p (c l) -> p c l", l=64)
                midv = bass.broadcast_tensor_aps(cc3, cc3[:, :, 31:32])[1]
                hta = pr.op("dve", lambda v: v.tensor_tensor(tmpa[:], cc[:], ew[:], ALU.add), deps=[hcc, he2])
                hmm1 = pr.op("dve", lambda v, midv=midv: v.tensor_tensor(
                    tmpc[:].rearrange("p (c l) -> p c l", l=64), tmpa[:].rearrange("p (c l) -> p c l", l=64),
                    midv, ALU.subtract), deps=[hta])
                hmm2 = pr.op("act", lambda s: s.activation(tmpc[:], tmpc[:], AF.Exp, bias=0.0, scale=1.0), deps=[hmm1])
                hrt2 = pr.op("dve", lambda v: v.tensor_tensor(rt2b[:], tmpc[:], rr[:], ALU.mult), deps=[hmm2, hr_])
                hta2 = pr.op("act", lambda s: s.activation(tmpa[:], tmpa[:], AF.Exp, bias=0.0, scale=1.0),
                             deps=[hta, hmm1])
                hrt = pr.op("dve", lambda v: v.tensor_tensor(rtb[:], tmpa[:], rr[:], ALU.mult), deps=[hta2, hr_])
                htb = pr.op("dve", lambda v: v.tensor_scalar(tmpb[:], cc[:], -1.0, 60.0, ALU.mult, ALU.min),
                            deps=[hcc, hta2])
                htb2 = pr.op("act", lambda s: s.activation(tmpb[:], tmpb[:], AF.Exp, bias=0.0, scale=1.0), deps=[htb])
                hkx = pr.op("dve", lambda v: v.tensor_tensor(kxb[:], tmpb[:], kk[:], ALU.mult), deps=[htb2, hkk])
                hkm1 = pr.op("dve", lambda v, midv=midv: v.tensor_tensor(
                    tmpc[:].rearrange("p (c l) -> p c l", l=64), midv, cc3, ALU.subtract), deps=[hmm2, hkx])
                pr.op("dve", lambda v: v.tensor_scalar(tmpc[:], tmpc[:], 60.0, None, ALU.min), tick=False)
                hkm2 = pr.op("act", lambda s: s.activation(tmpc[:], tmpc[:], AF.Exp, bias=0.0, scale=1.0), deps=[hkm1])
                hkx2 = pr.op("dve", lambda v: v.tensor_tensor(kx2b[:], tmpc[:], kk[:], ALU.mult), deps=[hkm2, hkk])
                cc3 = cc[:].rearrange("p (c l) -> p c l", l=64)
                cLv = bass.broadcast_tensor_aps(cc3, cc3[:, :, 63:64])[1]
                hkh1 = pr.op("dve", lambda v, cLv=cLv, cc3=cc3: v.tensor_tensor(
                    tmpb[:].rearrange("p (c l) -> p c l", l=64), cLv, cc3, ALU.subtract), deps=[hkx])
                hkh2 = pr.op("act", lambda s: s.activation(tmpb[:], tmpb[:], AF.Exp, bias=0.0, scale=1.0), deps=[hkh1])
                hkh = pr.op("dve", lambda v: v.tensor_tensor(khb[:], tmpb[:], kk[:], ALU.mult), deps=[hkh2])
                hdec = pr.op("act", lambda s, cc3=cc3: s.activation(
                    dec[:].rearrange("p (c o) -> p c o", o=1), cc3[:, :, 63:64], AF.Exp, bias=0.0, scale=1.0),
                    deps=[hcc, pr.last("dve")])
                hvb = pr.op("dve", lambda v: v.tensor_copy(vb[:], vv[:]), deps=[hv_])
                psTfree = psTfree if sg + hp else hpre
                for src_, dstt, hsrc in ((vb, vtm, hvb), (khb, khtm, hkh)):
                    for hh in range(2):
                        for tc in range(TSEG // 128):
                            hT = pr.op("pe", lambda t, src_=src_, hh=hh, tc=tc: t.transpose(
                                psT[:, :], src_[64 * hh:64 * (hh + 1), 128 * tc:128 * (tc + 1)],
                                idb[64 * hh:64 * (hh + 1), 64 * hh:64 * (hh + 1)]),
                                deps=[hsrc, psTfree])
                            hE = pr.op("dve", lambda v, dstt=dstt, hh=hh, tc=tc: v.tensor_copy(
                                dstt[:, hh, tc, :], psT[:, :]), deps=[hT])
                            psTfree = hE
                for j in range(NCHK):
                    c0 = j * L
                    jp = 64 * (j % 2)
                    for hh in range(2):
                        bO = hh
                        bA = 2 + hh
                        bS = 4 + hh
                        rh = 64 * hh
                        hSb = pr.op("dve", lambda v, rh=rh, j=j: v.tensor_copy(
                            S2b[rh:rh + 64, j % 2, :], S2[rh:rh + 64, :]),
                            deps=[hS, sbfree.get((j % 2, hh))])
                        hM1 = pr.op("pe", lambda t, rh=rh, j=j, c0=c0, bO=bO: t.matmul(
                            ps.t[bO][rh:rh + 64, 0:64], S2b[rh:rh + 64, j % 2, :], rtb[rh:rh + 64, c0:c0 + L],
                            start=True, stop=False), deps=[hSb, hrt, ps.free[bO]])
                        sbfree[(j % 2, hh)] = hM1
                        pr.op("pe", lambda t, rh=rh, c0=c0, bA=bA, jp=jp: t.matmul(
                            ps.t[bA][jp:jp + 64, 0:32], kxb[rh:rh + 64, c0:c0 + L],
                            rtb[rh:rh + 64, c0:c0 + 32], start=True, stop=True),
                            deps=[hkx, hrt, ps.free[bA]], tick=False)
                        hA = pr.op("pe", lambda t, rh=rh, c0=c0, bA=bA, jp=jp: t.matmul(
                            ps.t[bA][jp:jp + 64, 32:64], kx2b[rh:rh + 64, c0:c0 + L],
                            rt2b[rh:rh + 64, c0 + 32:c0 + 64], start=True, stop=True),
                            deps=[hkx2, hrt2])
                        ps.data[bA] = hA
                        hAm = pr.op("dve", lambda v, hh=hh, bA=bA, jp=jp: v.tensor_tensor(
                            amb[jp:jp + 64, hh, :], ps.t[bA][jp:jp + 64, 0:64], trib[jp:jp + 64, :], ALU.mult),
                            deps=[hA, ambfree[hh]])
                        ps.free[bA] = hAm
                        hO = pr.op("pe", lambda t, hh=hh, j=j, bO=bO, jp=jp, rh=rh: t.matmul(
                            ps.t[bO][rh:rh + 64, 0:64], vtm[jp:jp + 64, hh, j // 2, :],
                            amb[jp:jp + 64, hh, :], start=False, stop=True), deps=[hAm])
                        ps.data[bO] = hO
                        ambfree[hh] = hO
                        hSmm = pr.op("pe", lambda t, hh=hh, j=j, bS=bS, jp=jp, rh=rh: t.matmul(
                            ps.t[bS][rh:rh + 64, 0:64], khtm[jp:jp + 64, hh, j // 2, :],
                            vtm[jp:jp + 64, hh, j // 2, :], start=True, stop=True),
                            deps=[ps.free[bS]])
                        ps.data[bS] = hSmm
                        pr.op("dve", lambda v, rh=rh, j=j: v.tensor_scalar(
                            S2[rh:rh + 64, :], S2[rh:rh + 64, :], dec[rh:rh + 64, j:j + 1], None, ALU.mult),
                            deps=[hdec, hSb, hSmm], tick=False)
                        hS = pr.op("dve", lambda v, rh=rh, bS=bS: v.tensor_tensor(
                            S2[rh:rh + 64, :], S2[rh:rh + 64, :], ps.t[bS][rh:rh + 64, 0:64], ALU.add))
                        ps.free[bS] = hS
                        hEv = pr.op("dve", lambda v, rh=rh, c0=c0, bO=bO: v.tensor_copy(
                            osb[rh:rh + 64, c0:c0 + L], ps.t[bO][rh:rh + 64, 0:64]),
                            deps=[hO, hosb_free])
                        ps.free[bO] = hEv
                hd1 = pr.op("dve", lambda v, hp=hp: v.tensor_scalar(tmpa[:], rr[:], us[:, hp, 0:1], None, ALU.mult),
                            deps=[pr.last("act"), pr.last("pe")])
                hd2 = pr.op("dve", lambda v: v.tensor_tensor(tmpb[:], tmpa[:], kk[:], ALU.mult), deps=[hd1])
                hd3 = pr.op("dve", lambda v: v.tensor_copy(kxb[:], tmpb[:]), deps=[hd2])
                for half in range(2):
                    bd = 2 + half
                    hdm = pr.op("pe", lambda t, half=half, bd=bd: t.matmul(
                        ps.t[bd][:, 0:512], b2b[:], kxb[:, 512 * half:512 * (half + 1)],
                        start=True, stop=True), deps=[hd3, ps.free[bd]])
                    ps.data[bd] = hdm
                    pr.op("dve", lambda v, half=half, bd=bd: v.tensor_copy(
                        dsb[0:1, 512 * half:512 * (half + 1)], ps.t[bd][0:1, 0:512]), deps=[hdm], tick=False)
                    hdc = pr.op("dve", lambda v, half=half, bd=bd: v.tensor_copy(
                        dsb[64:65, 512 * half:512 * (half + 1)], ps.t[bd][64:65, 0:512]))
                    ps.free[bd] = hdc
                pr.op("dve", lambda v: v.tensor_copy(khb[0:1, :], dsb[0:1, :]), deps=[pr.last("pe")], tick=False)
                hdsb = pr.op("dve", lambda v: v.tensor_copy(khb[64:65, :], dsb[64:65, :]))
                for hh in range(2):
                    rh = 64 * hh
                    for half in range(2):
                        bb = 6
                        hB = pr.op("pe", lambda t, rh=rh, half=half, bb=bb: t.matmul(
                            ps.t[bb][rh:rh + 64, 0:512], ones2b[rh:rh + 1, :],
                            khb[rh:rh + 1, 512 * half:512 * (half + 1)],
                            start=True, stop=True), deps=[hdsb, ps.free[bb]])
                        ps.data[bb] = hB
                        pr.op("dve", lambda v, rh=rh, half=half, bb=bb: v.tensor_tensor(
                            tmpa[rh:rh + 64, 512 * half:512 * (half + 1)], ps.t[bb][rh:rh + 64, 0:512],
                            vv[rh:rh + 64, 512 * half:512 * (half + 1)], ALU.mult),
                            deps=[hB], tick=False)
                        hAdd = pr.op("dve", lambda v, rh=rh, half=half: v.tensor_tensor(
                            osb[rh:rh + 64, 512 * half:512 * (half + 1)],
                            osb[rh:rh + 64, 512 * half:512 * (half + 1)],
                            tmpa[rh:rh + 64, 512 * half:512 * (half + 1)], ALU.add))
                        ps.free[bb] = hAdd
                hosb_free = pr.dma("sp", o_oscr[r0:r0 + 128, t0:t0 + TSEG], osb[:], deps=[pr.last("dve")])
                if hp == 0:
                    pr.dma("sp", o_dsb[0:1, t0:t0 + TSEG], dsb[0:1, :], deps=[pr.last("dve")])
                    pr.dma("sp", o_dsb[64:65, t0:t0 + TSEG], dsb[64:65, :], deps=[pr.last("dve")])
                    if sg == 0:
                        pr.dma("sp", o_ruk[:, 0:TSEG], tmpb[:], deps=[hd3])

        hwoph = pr.tick("dve", deps=[pr.lastd("sp")])
        pr.stfree = {}
        for k in range(8):
            t0 = k * 512
            hod = pr.dma("sp", ot[:], o_oscr[:, t0:t0 + 512].rearrange("(a p) w -> p a w", p=128),
                         deps=[hwoph, pr.last("dve"), pr.last("pe"), pr.last("act")])
            hln = emit_layernorm(pr, 512, ot, on, sqr, (rowA, rowB, rowC), ones_f, onesrow_f,
                                 (0, 1), (2, 3), ps, hod, gamma=lngs, beta=lnbs)
            for a in range(NCH):
                pr.op("dve", lambda v, a=a: v.tensor_copy(onb[:, a, :], on[:, a, :]),
                      tick=False)
            hcb = pr.tick("dve")
            hfb = pr.dma("sp", ffb[:], ffnT[:, t0 // 2:(t0 + 512) // 2].bitcast(BF16)
                         .rearrange("(a p) w -> p a w", p=128), deps=[pr.last("pe")])
            for m in range(NCH):
                bk = 4 + m % 2
                for a in range(24):
                    srcb = onb[:, a, :] if a < 8 else ffb[:, a - 8, :]
                    pr.op("pe", lambda t, m=m, a=a, bk=bk, srcb=srcb: t.matmul(
                        ps.t[bk][:, 0:512], Wob[:, a, 128 * m:128 * (m + 1)], srcb,
                        start=(a == 0), stop=(a == 23)),
                        deps=[hcb, hfb, ps.free[bk]] if a == 0 else (), tick=(a == 23))
                hpe = pr.last("pe")
                ps.data[bk] = hpe
                sl = m % 4
                hev = pr.op("dve", lambda v, bk=bk, sl=sl: v.tensor_copy(evac[:, sl, :], ps.t[bk][:, 0:512]),
                            deps=[hpe, pr.stfree.get(("e", sl))])
                ps.free[bk] = hev
                hst = pr.dma("sp", o_s[128 * m:128 * (m + 1), t0:t0 + 512], evac[:, sl, :], deps=[hev])
                pr.stfree[("e", sl)] = hst

        pr.finalize([pr.lastd("sp"), pr.lastd("pool"), pr.last("dve"), pr.last("pe"), pr.last("act")])
    return pr




def build_launch3(phases=9, nexp=8, noacc=False, niq=None):
    pr = Prog("l3")
    xT = pr.din("xT", [C, T])
    sT = pr.din("sT", [C, T])
    xnT = pr.din("xnT", [C, T])
    rg = pr.din("rg", [C, 8])
    gw = pr.din("gw", [8, C, 4 * C])
    uw = pr.din("uw", [8, C, 4 * C])
    dw = pr.din("dw", [8, 4 * C, C])
    cones = pr.din("cones", [128, 1])
    cid = pr.din("cid", [128, 128])
    csel = pr.din("csel", [8, 8, 128])
    o_x1 = pr.dout("x1T", [C, T])
    o_acc = pr.dout("accT", [C, T])
    o_rdbg = pr.dout("rdbg", [128, 32, 36])

    NTT = 8
    NIQ = 8
    NIQ = niq if niq else NIQ
    ICQ = 4

    with pr.es:
        xnb = pr.sbuf("xnb", [128, NCH, T // 2], BF16)
        acc = pr.sbuf("acc", [128, NCH, T // 2], F32)
        routb = pr.sbuf("routb", [128, NCH, 8], F32)
        xnstrip = pr.sbuf("xnstrip", [128, 2, NCH, 128], F32)
        idb = pr.sbuf("idb", [128, 128], F32)
        ones_f = pr.sbuf("ones_f", [128, 1], F32)
        Gt = pr.sbuf("Gt", [128, 32, 8], F32)
        GTsb = pr.sbuf("GTsb", [8, T], F32)
        gB = pr.sbuf("gB", [128, T // 2], F32)
        selb = pr.sbuf("selb", [8, 8, 128], F32)
        probs = pr.sbuf("probs", [128, 8], F32)
        m8 = pr.sbuf("m8", [128, 8], F32)
        negmax = pr.sbuf("negmax", [128, 1], F32)
        sumexp = pr.sbuf("sumexp", [128, 1], F32)
        w1c = pr.sbuf("w1c", [128, 1], F32)
        w2c = pr.sbuf("w2c", [128, 1], F32)
        t2m = pr.sbuf("t2m", [128, 1], F32)
        scr = pr.sbuf("scr", [128, 8], F32)
        scr2 = pr.sbuf("scr2", [128, 8], F32)
        gsl = pr.sbuf("gsl", [128, ICQ, NCH, 128], BF16)
        usl = pr.sbuf("usl", [128, ICQ, NCH, 128], BF16)
        dsl = pr.sbuf("dsl", [128, ICQ, C], BF16)
        hbq = pr.sbuf("hbq", [128, ICQ, 4, 512], BF16)
        sil = pr.sbuf("sil", [128, 512], F32)
        stage = pr.sbuf("stage", [128, NTT, 512], F32)
        xa = pr.sbuf("xa", [128, 1, 512], F32)
        xb = pr.sbuf("xb", [128, 1, 512], F32)
        xc = pr.sbuf("xc", [128, 1, 512], F32)
        xg = pr.sbuf("xg", [128, 1, 512], F32)

        ps = Banks([pr.psum(f"ps{i}", [128, 512]) for i in range(8)])

        hx = pr.dma("pool", routb[:], rg.rearrange("(a p) n -> p a n", p=128))
        hid = pr.dma("pool", idb[:], cid)
        ho = pr.dma("pool", ones_f[:], cones)
        hsel = pr.dma("pool", selb[:], csel)

        hstrip_free = {}
        for tt in range(32):
            c0 = tt * 128
            slt = tt % 2
            hsl_ = pr.dma("sp", xnstrip[:, slt, :, :],
                          xnT[:, c0:c0 + 128].rearrange("(a p) w -> p a w", p=128),
                          deps=[hstrip_free.get(slt)])
            for a in range(NCH):
                hl = pr.op("pe", lambda t, slt=slt, a=a: t.matmul(
                    ps.t[0][:, 0:8], xnstrip[:, slt, a, :], routb[:, a, :],
                    start=(a == 0), stop=(a == NCH - 1)),
                    deps=[hsl_, hx, ps.free[0]] if a == 0 else (), tick=(a == NCH - 1))
            hstrip_free[slt] = hl
            ps.data[0] = hl
            hlg = pr.op("dve", lambda v: v.tensor_copy(scr2[:], ps.t[0][:, 0:8]),
                        deps=[hl, pr.stfree3.get("dbg") if hasattr(pr, "stfree3") else None])
            pr.dma("sp", o_rdbg[:, tt, 16:24], scr2[:], deps=[hlg, pr.stfree3.get("s2") if hasattr(pr, "stfree3") else None])
            pr.stfree3 = getattr(pr, "stfree3", {})
            pr.stfree3["s2"] = h3 if False else None
            h1 = pr.op("dve", lambda v: v.max(scr[:], scr2[:]), deps=[hlg])
            h1 = pr.op("dve", lambda v: v.tensor_copy(negmax[:], scr[:, 0:1]), deps=[h1])
            ps.free[0] = h1
            h2 = pr.op("dve", lambda v: v.tensor_scalar(negmax[:], negmax[:], -1.0, None, ALU.mult), deps=[h1])
            h3 = pr.op("act", lambda s: s.activation(probs[:], scr2[:], AF.Exp, bias=negmax[:], scale=1.0),
                       deps=[h2])
            hdx = pr.dma("sp", o_rdbg[:, tt, 24:32], probs[:], deps=[h3])
            h3 = pr.op("dve", lambda v: v.tensor_reduce(sumexp[:], probs[:], mybir.AxisListType.X, ALU.add),
                       deps=[h3])
            hds = pr.dma("sp", o_rdbg[:, tt, 33:34], sumexp[:], deps=[h3], allow_slow_non_contiguous=True)
            h4 = pr.op("dve", lambda v: v.reciprocal(sumexp[:], sumexp[:]), deps=[h3, hds])
            h4 = pr.op("dve", lambda v: v.tensor_scalar(probs[:], probs[:], sumexp[:], None, ALU.mult),
                       deps=[h4, hdx]) if False else h4
            h5 = pr.op("dve", lambda v: v.tensor_scalar(probs[:], probs[:], sumexp[:], None, ALU.mult), deps=[h4])
            h6 = pr.op("dve", lambda v: v.max(m8[:], probs[:]), deps=[h5])
            pr.op("dve", lambda v: v.tensor_tensor(w1c[:], m8[:, 0:1], m8[:, 1:2], ALU.add), deps=[h6])
            pr.op("dve", lambda v: v.tensor_scalar(w1c[:], w1c[:], 1e-9, None, ALU.add))
            h7 = pr.op("dve", lambda v: v.reciprocal(w1c[:], w1c[:]))
            pr.op("dve", lambda v: v.tensor_tensor(w2c[:], m8[:, 1:2], w1c[:], ALU.mult))
            pr.op("dve", lambda v: v.tensor_tensor(w1c[:], m8[:, 0:1], w1c[:], ALU.mult))
            pr.op("dve", lambda v: v.tensor_tensor(t2m[:], m8[:, 1:2], m8[:, 0:1], ALU.is_lt))
            h8 = pr.op("dve", lambda v: v.tensor_tensor(w2c[:], w2c[:], t2m[:], ALU.mult))
            bm1 = bass.broadcast_tensor_aps(probs[:], m8[:, 0:1])[1]
            bm2 = bass.broadcast_tensor_aps(probs[:], m8[:, 1:2])[1]
            bw1 = bass.broadcast_tensor_aps(probs[:], w1c[:])[1]
            bw2 = bass.broadcast_tensor_aps(probs[:], w2c[:])[1]
            pr.op("dve", lambda v, tt=tt, bm1=bm1, bw1=bw1: v.tensor_tensor(scr[:], probs[:], bm1, ALU.is_equal),
                  deps=[h8])
            pr.op("dve", lambda v, tt=tt, bw1=bw1: v.tensor_tensor(Gt[:, tt, :], scr[:], bw1, ALU.mult))
            pr.op("dve", lambda v, bm2=bm2: v.tensor_tensor(scr[:], probs[:], bm2, ALU.is_equal))
            pr.op("dve", lambda v, bw2=bw2: v.tensor_tensor(scr[:], scr[:], bw2, ALU.mult))
            h9 = pr.op("dve", lambda v, tt=tt: v.tensor_tensor(Gt[:, tt, :], Gt[:, tt, :], scr[:], ALU.add))
            pr.dma("sp", o_rdbg[:, tt, 0:8], probs[:], deps=[h9])
            hdbg = pr.dma("sp", o_rdbg[:, tt, 8:16], m8[:], deps=[h9])
            pr.stfree3 = getattr(pr, "stfree3", {})
            pr.stfree3["dbg"] = hdbg
            hT = pr.op("pe", lambda t, tt=tt: t.transpose(ps.t[0][0:8, 0:128], Gt[:, tt, :], idb[:]),
                       deps=[h9, ps.free[0], hid])
            ps.data[0] = hT
            hG = pr.op("dve", lambda v, tt=tt: v.tensor_copy(GTsb[:, 128 * tt:128 * (tt + 1)], ps.t[0][0:8, 0:128]),
                       deps=[hT])
            ps.free[0] = hG
        hrouter = pr.last("dve")
        pr.stfree3 = getattr(pr, "stfree3", {})
        pr.dma("sp", o_acc[0:8, 0:T], GTsb[:], deps=[hrouter])
        if phases < 2:
            pr.finalize([pr.lastd("sp"), pr.lastd("pool"), pr.last("dve"), pr.last("pe"), pr.last("act")])
            return pr

        wsl = {}
        for th in range(2):
            toff = th * (T // 2)
            NTT2 = (T // 2) // 512
            hxnl2 = []
            for a in range(NCH):
                hxnl2.append(pr.dma("pool", xnb[:, a, 0:T // 2], xnT[128 * a:128 * (a + 1), toff:toff + T // 2],
                                    deps=[pr.last("pe"), pr.last("dve")] if a == 0 else ()))
            for a in range(NCH):
                pr.op("dve", lambda v, a=a: v.memset(acc[:, a, :], 0.0),
                      deps=[hxnl2[-1], pr.lastd("sp")], tick=(a == NCH - 1))
            hacc0 = pr.last("dve")
            for e in range(nexp):
                hgbev = []
                for q in range(NTT2):
                    hq = pr.op("pe", lambda t, e=e, q=q, toff=toff: t.matmul(
                        ps.t[7][:, 0:512], selb[:, e, :], GTsb[:, toff + 512 * q:toff + 512 * (q + 1)],
                        start=True, stop=True), deps=[hrouter, hsel, ps.free[7]])
                    ps.data[7] = hq
                    hev = pr.op("dve", lambda v, q=q: v.tensor_copy(gB[:, 512 * q:512 * (q + 1)], ps.t[7][:, 0:512]),
                                deps=[hq, pr.stfree3.get("gBfree")])
                    ps.free[7] = hev
                    hgbev.append(hev)
                hgb = hgbev[-1]
                for iq in range(NIQ):
                    hws = []
                    for ic in range(ICQ):
                        i0_ = (iq * ICQ + ic) * 128
                        wdep = wsl.get(ic)
                        hg_ = pr.dma("pool", gsl[:, ic, :, :], gw[e, :, i0_:i0_ + 128]
                                     .rearrange("(a p) n -> p a n", p=128),
                                     deps=[wdep] if wdep else ())
                        hu_ = pr.dma("pool", usl[:, ic, :, :], uw[e, :, i0_:i0_ + 128]
                                     .rearrange("(a p) n -> p a n", p=128), deps=())
                        hd_ = pr.dma("pool", dsl[:, ic, :], dw[e, i0_:i0_ + 128, :], deps=())
                        hws.append((hg_, hu_, hd_))
                    silfree = pr.last("dve")
                    for ic in range(ICQ):
                        hg_, hu_, hd_ = hws[ic]
                        for tt in range(NTT2):
                            bg = 1 + (tt % 2)
                            bu = 3 + (tt % 2)
                            for a in range(NCH):
                                hgm = pr.op("pe", lambda t, ic=ic, tt=tt, a=a, bg=bg: t.matmul(
                                    ps.t[bg][:, 0:512], gsl[:, ic, a, :], xnb[:, a, 512 * tt:512 * (tt + 1)],
                                    start=(a == 0), stop=(a == NCH - 1)),
                                    deps=[hg_, hacc0, ps.free[bg]] if a == 0 else (), tick=(a == NCH - 1))
                            ps.data[bg] = hgm
                            for a in range(NCH):
                                hum = pr.op("pe", lambda t, ic=ic, tt=tt, a=a, bu=bu: t.matmul(
                                    ps.t[bu][:, 0:512], usl[:, ic, a, :], xnb[:, a, 512 * tt:512 * (tt + 1)],
                                    start=(a == 0), stop=(a == NCH - 1)),
                                    deps=[hu_, ps.free[bu]] if a == 0 else (), tick=(a == NCH - 1))
                            ps.data[bu] = hum
                            hsil = pr.op("act", lambda s, bg=bg: s.activation(sil[:], ps.t[bg][:, 0:512], AF.Silu,
                                                                              bias=0.0, scale=1.0),
                                         deps=[hgm, silfree], selfsync=False)
                            ps.free[bg] = hsil
                            hhb = pr.op("dve", lambda v, ic=ic, tt=tt, bu=bu: v.tensor_tensor(
                                hbq[:, ic, tt, :], ps.t[bu][:, 0:512], sil[:], ALU.mult), deps=[hum, hsil])
                            ps.free[bu] = hhb
                            silfree = hhb
                        wsl[ic] = pr.last("pe")
                    hhb_all = pr.last("dve")
                    for co in range(NCH):
                        for tt in range(NTT2):
                            bk = 5 + (tt % 2)
                            for ic in range(ICQ):
                                hym = pr.op("pe", lambda t, co=co, tt=tt, ic=ic, bk=bk: t.matmul(
                                    ps.t[bk][:, 0:512], dsl[:, ic, 128 * co:128 * (co + 1)], hbq[:, ic, tt, :],
                                    start=(ic == 0), stop=(ic == ICQ - 1)),
                                    deps=([hhb_all, hws[ic][2], ps.free[bk]] if ic == 0 else [hws[ic][2]]),
                                    tick=(ic == ICQ - 1))
                            ps.data[bk] = hym
                            pr.op("dve", lambda v, tt=tt, bk=bk: v.tensor_tensor(
                                stage[:, 0, :], ps.t[bk][:, 0:512], gB[:, 512 * tt:512 * (tt + 1)], ALU.mult),
                                deps=[hym, hgb], tick=False)
                            hst = pr.op("dve", lambda v, co=co, tt=tt: v.tensor_tensor(
                                acc[:, co, 512 * tt:512 * (tt + 1)], acc[:, co, 512 * tt:512 * (tt + 1)],
                                stage[:, 0, :], ALU.add))
                            ps.free[bk] = hst
                    for ic2 in range(ICQ):
                        wsl[ic2] = pr.last("pe")
                pr.stfree3["gBfree"] = pr.last("dve")
            hacc_all = pr.last("dve")
            pr.dma("sp", o_acc[:, toff:toff + T // 2].rearrange("(a p) w -> p a w", p=128), acc[:],
                   deps=[hacc_all])
            for a in range(NCH):
                for tt in range(NTT2):
                    gd = pr.stfree3.get("cmb")
                    ha_ = pr.dma("sp", xb[:, 0, :], xT[128 * a:128 * (a + 1), toff + 512 * tt:toff + 512 * (tt + 1)],
                                 deps=[gd])
                    hc_ = pr.dma("sp", xc[:, 0, :], sT[128 * a:128 * (a + 1), toff + 512 * tt:toff + 512 * (tt + 1)],
                                 deps=[gd])
                    hsl2 = pr.op("act", lambda s, a=a, tt=tt: s.activation(
                        xg[:, 0, :], acc[:, a, 512 * tt:512 * (tt + 1)], AF.Silu, bias=0.0, scale=1.0),
                        deps=[hacc_all, gd], selfsync=False)
                    pr.op("dve", lambda v: v.tensor_tensor(xg[:, 0, :], xg[:, 0, :], xc[:, 0, :], ALU.mult),
                          deps=[hsl2, hc_], tick=False)
                    hx1 = pr.op("dve", lambda v: v.tensor_tensor(xg[:, 0, :], xg[:, 0, :], xb[:, 0, :], ALU.add),
                                deps=[ha_])
                    hstx = pr.dma("sp", o_x1[128 * a:128 * (a + 1), toff + 512 * tt:toff + 512 * (tt + 1)],
                                  xg[:, 0, :], deps=[hx1])
                    pr.stfree3["cmb"] = hstx

        pr.finalize([pr.lastd("sp"), pr.lastd("pool"), pr.last("dve"), pr.last("pe"), pr.last("act")])
    return pr



def build_launch4():
    pr = Prog("l4")
    x1T = pr.din("x1T", [C, T])
    WqT = pr.din("WqT", [C, 3 * C])
    qkvb = pr.din("qkvb", [3 * C, 1])
    cones = pr.din("cones", [128, 1])
    cid = pr.din("cid", [128, 128])
    o_xn1 = pr.dout("xn1T", [C, T])
    o_q = pr.dout("qTb", [C, T // 2], U32)
    o_ktm = pr.dout("Ktm", [H, T, DH // 2], U32)
    o_vtm = pr.dout("Vtm", [H, T, DH // 2], U32)
    o_qtm = pr.dout("Qtm", [H, T, DH // 2], U32)
    o_vt = pr.dout("vT", [C, T])

    TW2 = 512
    with pr.es:
        Wqb = pr.sbuf("Wqb", [128, NCH, 3 * C], BF16)
        qbs = pr.sbuf("qbs", [128, 24, 1], F32)
        ones_f = pr.sbuf("ones_f", [128, 1], F32)
        onesrow_f = pr.sbuf("onesrow_f", [1, 128], F32)
        idb = pr.sbuf("idb", [128, 128], BF16)
        xt = pr.sbuf("xt", [128, NCH, TW2], F32)
        xn = pr.sbuf("xn", [128, NCH, TW2], F32)
        xnb = pr.sbuf("xnb", [128, NCH, TW2], BF16)
        sqr = pr.sbuf("sqr", [128, 2, TW2], F32)
        rowA = pr.sbuf("rowA", [1, TW2], F32)
        rowB = pr.sbuf("rowB", [1, TW2], F32)
        rowC = pr.sbuf("rowC", [1, TW2], F32)
        qevb = pr.sbuf("qevb", [128, 2, TW2], BF16)
        kvt = pr.sbuf("kvt", [128, 3, NCH, TW2], BF16)
        vev = pr.sbuf("vev", [128, 2, TW2], F32)
        tev = pr.sbuf("tev", [128, 2, 64], BF16)
        ps = Banks([pr.psum(f"ps{i}", [128, 512]) for i in range(7)])
        psT = pr.psum("psT", [128, 64], BF16)

        hw = pr.dma("pool", Wqb[:], WqT.rearrange("(a p) n -> p a n", p=128))
        hb = pr.dma("pool", qbs[:], qkvb.rearrange("(a p) n -> p a n", p=128), allow_slow_non_contiguous=True)
        ho = pr.dma("pool", ones_f[:], cones)
        hi = pr.dma("pool", idb[:], cid)
        h0 = pr.op("dve", lambda v: v.memset(onesrow_f[:], 1.0), deps=[hi])
        pr.stf = {}

        for k in range(8):
            t0 = k * TW2
            hx = pr.dma("sp", xt[:], x1T[:, t0:t0 + TW2].rearrange("(a p) w -> p a w", p=128),
                        deps=[pr.stf.get("xn"), h0])
            hln = emit_layernorm(pr, TW2, xt, xn, sqr, (rowA, rowB, rowC), ones_f, onesrow_f,
                                 (0, 1), (2, 3), ps, hx)
            hst = pr.dma("sp", o_xn1[:, t0:t0 + TW2].rearrange("(a p) w -> p a w", p=128), xn[:], deps=[hln])
            pr.stf["xn"] = hst
            for a in range(NCH):
                pr.op("dve", lambda v, a=a: v.tensor_copy(xnb[:, a, :], xn[:, a, :]), tick=(a == NCH - 1))
            hcb = pr.last("dve")
            for m in range(24):
                bk = 4 + m % 2
                for a in range(NCH):
                    hmm = pr.op("pe", lambda t, m=m, a=a, bk=bk: t.matmul(
                        ps.t[bk][:, 0:TW2], Wqb[:, a, 128 * m:128 * (m + 1)], xnb[:, a, :],
                        start=(a == 0), stop=(a == NCH - 1)),
                        deps=[hcb, hw, ps.free[bk]] if a == 0 else (), tick=(a == NCH - 1))
                ps.data[bk] = hmm
                if m < 8:
                    sl = m % 2
                    hev = pr.op("dve", lambda v, m=m, bk=bk, sl=sl: v.tensor_scalar(
                        qevb[:, sl, :], ps.t[bk][:, 0:TW2], qbs[:, m, 0:1], None, ALU.add),
                        deps=[hmm, hb, pr.stf.get(("q", sl))])
                    hev2 = pr.op("dve", lambda v, m=m, sl=sl: v.tensor_copy(kvt[:, 2, m, :], qevb[:, sl, :]),
                                 deps=[pr.stf.get(("kv", 2, m))])
                    ps.free[bk] = hev2
                    pr.stf[("kvh", 2, m)] = hev2
                    hq = pr.dma("sp", o_q[128 * m:128 * (m + 1), t0 // 2:(t0 + TW2) // 2],
                                qevb[:, sl, :].bitcast(U32), deps=[hev])
                    pr.stf[("q", sl)] = hq
                elif m < 16:
                    a2 = m - 8
                    hev = pr.op("dve", lambda v, m=m, bk=bk, a2=a2: v.tensor_scalar(
                        kvt[:, 0, a2, :], ps.t[bk][:, 0:TW2], qbs[:, m, 0:1], None, ALU.add),
                        deps=[hmm, pr.stf.get(("kv", 0, a2))])
                    ps.free[bk] = hev
                    pr.stf[("kvh", 0, a2)] = hev
                else:
                    a2 = m - 16
                    hev = pr.op("dve", lambda v, m=m, bk=bk, a2=a2: v.tensor_scalar(
                        kvt[:, 1, a2, :], ps.t[bk][:, 0:TW2], qbs[:, m, 0:1], None, ALU.add),
                        deps=[hmm, pr.stf.get(("kv", 1, a2))])
                    hv2 = pr.op("dve", lambda v, m=m, bk=bk, a2=a2: v.tensor_scalar(
                        vev[:, a2 % 2, :], ps.t[bk][:, 0:TW2], qbs[:, m, 0:1], None, ALU.add),
                        deps=[pr.stf.get(("vf", a2 % 2))])
                    ps.free[bk] = hv2
                    pr.stf[("kvh", 1, a2)] = hev
                    hvs = pr.dma("sp", o_vt[128 * a2:128 * (a2 + 1), t0:t0 + TW2], vev[:, a2 % 2, :], deps=[hv2])
                    pr.stf[("vf", a2 % 2)] = hvs
            for kv in range(3):
                dst = (o_ktm, o_vtm, o_qtm)[kv]
                for h in range(H):
                    a2 = h // 2
                    rb = 64 * (h % 2)
                    for tc in range(TW2 // 128):
                        hT = pr.op("pe", lambda t, kv=kv, a2=a2, rb=rb, tc=tc: t.transpose(
                            psT[:, :], kvt[rb:rb + 64, kv, a2, 128 * tc:128 * (tc + 1)],
                            idb[rb:rb + 64, rb:rb + 64]),
                            deps=[pr.stf.get(("kvh", kv, a2)), pr.stf.get("psT")])
                        hE = pr.op("dve", lambda v, sl=tc % 2: v.tensor_copy(tev[:, sl, :], psT[:, :]),
                                   deps=[hT, pr.stf.get(("tev", tc % 2))])
                        pr.stf["psT"] = hE
                        hD = pr.dma("sp", dst[h, t0 + 128 * tc:t0 + 128 * (tc + 1), :],
                                    tev[:, tc % 2, :].bitcast(U32), deps=[hE])
                        pr.stf[("tev", tc % 2)] = hD
            for kv in range(3):
                for a2 in range(NCH):
                    pr.stf[("kv", kv, a2)] = pr.last("pe")

        pr.finalize([pr.lastd("sp"), pr.lastd("pool"), pr.last("dve"), pr.last("pe"), pr.last("act")])
    return pr



def build_launch6(nolora=False):
    pr = Prog("l6")
    x1T = pr.din("x1T", [C, T])
    xn1T = pr.din("xn1T", [C, T])
    ygT = pr.din("ygT", [C, T])
    vT = pr.din("vT", [C, T])
    dwc = pr.din("dwc", [C, 4])
    PwT = pr.din("PwT", [C, C])
    pb = pr.din("pb", [C, 1])
    Fk = pr.din("Fk", [C, C])
    fb = pr.din("fb", [C, 1])
    fa = pr.din("fa", [C, 16])
    fbb = pr.din("fbb", [128, C])
    cones = pr.din("cones", [128, 1])
    o_out = pr.dout("outT", [C, T])
    o_qd = pr.dout("qdbg", [128, NCH, C])
    o_s1 = pr.dout("s1T", [C, T])
    o_f = pr.dout("fT", [C, T])
    o_yl = pr.dout("ylT", [C, T])

    TW2 = 512
    with pr.es:
        PwTb = pr.sbuf("PwTb", [128, NCH, C], BF16)
        dwcs = pr.sbuf("dwcs", [128, NCH, 4], F32)
        pbs = pr.sbuf("pbs", [128, NCH, 1], F32)
        fbs = pr.sbuf("fbs", [128, NCH, 1], F32)
        fab = pr.sbuf("fab", [128, NCH, 16], BF16)
        fbbb = pr.sbuf("fbbb", [128, C], BF16)
        Fkf = pr.sbuf("Fkf", [128, NCH, C], F32)
        Fkq = pr.sbuf("Fkq", [128, NCH, C], BF16)
        amax = pr.sbuf("amax", [128, 1], F32)
        arow = pr.sbuf("arow", [1, 128], F32)
        ascl = pr.sbuf("ascl", [128, 1], F32)
        sscl = pr.sbuf("sscl", [128, 1], F32)
        ones_f = pr.sbuf("ones_f", [128, 1], F32)
        onesrow_f = pr.sbuf("onesrow_f", [1, 128], F32)
        idb32 = pr.sbuf("idb32", [128, 128], F32)
        vt = pr.sbuf("vt", [128, NCH, TW2 + 2], F32)
        yg = pr.sbuf("yg", [128, NCH, TW2], F32)
        ygb = pr.sbuf("ygb", [128, NCH, TW2], BF16)
        xnt = pr.sbuf("xnt", [128, NCH, TW2], F32)
        xnb = pr.sbuf("xnb", [128, NCH, TW2], BF16)
        x1t = pr.sbuf("x1t", [128, 2, TW2], F32)
        sev = pr.sbuf("sev", [128, 2, TW2], F32)
        fev = pr.sbuf("fev", [128, 2, TW2], F32)
        lorab = pr.sbuf("lorab", [128, TW2], BF16)
        ps = Banks([pr.psum(f"ps{i}", [128, 512]) for i in range(8)])

        hc1 = pr.dma("pool", PwTb[:], PwT.rearrange("(a p) n -> p a n", p=128))
        hc2 = pr.dma("pool", dwcs[:], dwc.rearrange("(a p) n -> p a n", p=128))
        hc3 = pr.dma("pool", pbs[:], pb.rearrange("(a p) n -> p a n", p=128), allow_slow_non_contiguous=True)
        hc4 = pr.dma("pool", fbs[:], fb.rearrange("(a p) n -> p a n", p=128), allow_slow_non_contiguous=True)
        hc5 = pr.dma("pool", fab[:], fa.rearrange("(a p) n -> p a n", p=128))
        hc6 = pr.dma("pool", fbbb[:], fbb)
        hc7 = pr.dma("sp", Fkf[:], Fk.rearrange("(a p) n -> p a n", p=128))
        hc8 = pr.dma("pool", ones_f[:], cones)
        h0 = pr.op("dve", lambda v: v.memset(onesrow_f[:], 1.0), deps=[hc8])
        pr.op("dve", lambda v: v.memset(lorab[:], 0.0), tick=False)

        hq1 = pr.op("dve", lambda v: v.tensor_reduce(amax[:], Fkf[:].rearrange("p a n -> p (a n)"),
                                                     mybir.AxisListType.X, ALU.max, apply_absolute_value=True),
                    deps=[hc7])
        hi32 = pr.dma("pool", idb32[:], pr.din("cid", [128, 128]))
        hq2 = pr.op("pe", lambda t: t.transpose(ps.t[0][0:1, 0:128], amax[:], idb32[:]),
                    deps=[hq1, hi32])
        hq3 = pr.op("dve", lambda v: v.tensor_copy(arow[:], ps.t[0][0:1, 0:128]), deps=[hq2])
        ps.free[0] = hq3
        w = 128
        hqq = hq3
        while w > 1:
            w //= 2
            hqq = pr.op("dve", lambda v, w=w: v.tensor_tensor(arow[:, 0:w], arow[:, 0:w], arow[:, w:2 * w],
                                                              ALU.max), deps=[hqq])
        hq4 = pr.op("pe", lambda t: t.matmul(ps.t[0][:, 0:2], onesrow_f[:], arow[:, 0:2], start=True, stop=True),
                    deps=[hqq, h0, ps.free[0]])
        ps.data[0] = hq4
        hq5 = pr.op("dve", lambda v: v.tensor_copy(ascl[:], ps.t[0][:, 0:1]), deps=[hq4])
        ps.free[0] = hq5
        hq6 = pr.op("dve", lambda v: v.tensor_scalar(sscl[:], ascl[:], 1.0 / 7.0, None, ALU.mult), deps=[hq5])
        hq7 = pr.op("dve", lambda v: v.reciprocal(ascl[:], ascl[:]), deps=[hq6])
        hq8 = pr.op("dve", lambda v: v.tensor_scalar(ascl[:], ascl[:], 7.0, None, ALU.mult), deps=[hq7])
        hq9 = hq8
        for a in range(NCH):
            pr.op("dve", lambda v, a=a: v.tensor_scalar(Fkf[:, a, :], Fkf[:, a, :], ascl[:], None, ALU.mult),
                  deps=[hq9])
            pr.op("dve", lambda v, a=a: v.tensor_scalar(Fkf[:, a, :], Fkf[:, a, :], MAGIC, MAGIC,
                                                        ALU.add, ALU.subtract))
            pr.op("dve", lambda v, a=a: v.tensor_scalar(Fkf[:, a, :], Fkf[:, a, :], -8.0, 7.0,
                                                        ALU.max, ALU.min))
            hq9 = pr.op("dve", lambda v, a=a: v.tensor_copy(Fkq[:, a, :], Fkf[:, a, :]))
        hqdone = hq9
        pr.dma("sp", o_qd[:, 0, 0:1], ascl[:], deps=[hqdone], allow_slow_non_contiguous=True)
        pr.dma("sp", o_qd[:, 1, 0:1], sscl[:], deps=[hqdone], allow_slow_non_contiguous=True)
        for a in range(NCH):
            if a >= 2:
                pr.dma("pool", o_qd[:, a, :], Fkq[:, a, :], deps=[hqdone])
        pr.stf = {}
        for k in range(8):
            t0 = k * TW2
            gv = pr.stf.get("vt")
            lo = max(t0 - 1, 0)
            hi_ = min(t0 + TW2 + 1, T)
            hv = pr.dma("sp", vt[:, :, (1 if t0 == 0 else 0):(1 if t0 == 0 else 0) + (hi_ - lo)]
                        .rearrange("p a w -> p a w"),
                        vT[:, lo:hi_].rearrange("(a p) w -> p a w", p=128), deps=[gv])
            hy = pr.dma("sp", yg[:].rearrange("p a w -> p a w"),
                        ygT[:, t0:t0 + TW2].rearrange("(a p) w -> p a w", p=128), deps=[pr.stf.get("yg")])
            hxn_ = pr.dma("sp", xnt[:], xn1T[:, t0:t0 + TW2].rearrange("(a p) w -> p a w", p=128),
                          deps=[pr.stf.get("xn")])
            if t0 == 0:
                pr.op("dve", lambda v: v.memset(vt[:, :, 0:1], 0.0), deps=[hv], tick=False)
            for a in range(NCH):
                if t0 + TW2 + 1 > T:
                    pr.op("dve", lambda v, a=a: v.memset(vt[:, a, TW2 + 1:TW2 + 2], 0.0), deps=[hv], tick=False)
                pr.op("dve", lambda v, a=a: v.scalar_tensor_tensor(
                    yg[:, a, :], vt[:, a, 0:TW2], dwcs[:, a, 0:1], yg[:, a, :], ALU.mult, ALU.add),
                    deps=[hv, hy], tick=False)
                pr.op("dve", lambda v, a=a: v.scalar_tensor_tensor(
                    yg[:, a, :], vt[:, a, 1:TW2 + 1], dwcs[:, a, 1:2], yg[:, a, :], ALU.mult, ALU.add), tick=False)
                pr.op("dve", lambda v, a=a: v.scalar_tensor_tensor(
                    yg[:, a, :], vt[:, a, 2:TW2 + 2], dwcs[:, a, 2:3], yg[:, a, :], ALU.mult, ALU.add), tick=False)
                pr.op("dve", lambda v, a=a: v.tensor_scalar(
                    yg[:, a, :], yg[:, a, :], dwcs[:, a, 3:4], None, ALU.add), tick=False)
                pr.op("dve", lambda v, a=a: v.tensor_copy(ygb[:, a, :], yg[:, a, :]), tick=(a == NCH - 1))
            hyg = pr.last("dve")
            hyd = pr.dma("sp", o_yl[:, t0:t0 + TW2].rearrange("(a p) w -> p a w", p=128), yg[:], deps=[hyg])
            hyg = hyd
            pr.stf["yg"] = hyg
            pr.stf["vt"] = hyg
            for a in range(NCH):
                pr.op("dve", lambda v, a=a: v.tensor_copy(xnb[:, a, :], xnt[:, a, :]),
                      deps=[hxn_], tick=(a == NCH - 1))
            hxnb = pr.last("dve")
            pr.stf["xn"] = hxnb
            for a in range(NCH):
                hl1 = pr.op("pe", lambda t, a=a: t.matmul(ps.t[6][0:16, 0:TW2], fab[:, a, :], xnb[:, a, :],
                                                          start=(a == 0), stop=(a == NCH - 1)),
                            deps=[hxnb, hc5, ps.free[6]] if a == 0 else (), tick=(a == NCH - 1))
            ps.data[6] = hl1
            pr.op("dve", lambda v: v.tensor_copy(lorab[0:16, :], ps.t[6][0:16, 0:TW2]), deps=[hl1], tick=False)
            hl2 = pr.op("dve", lambda v: v.tensor_scalar(lorab[0:16, :], lorab[0:16, :], ascl[0:16, :],
                                                         None, ALU.mult))
            ps.free[6] = hl2
            for m in range(NCH):
                bs = m % 2
                bf = 2 + m % 2
                for a in range(NCH):
                    hpm = pr.op("pe", lambda t, m=m, a=a, bs=bs: t.matmul(
                        ps.t[bs][:, 0:TW2], PwTb[:, a, 128 * m:128 * (m + 1)], ygb[:, a, :],
                        start=(a == 0), stop=(a == NCH - 1)),
                        deps=[hyg, hc1, ps.free[bs]] if a == 0 else (), tick=(a == NCH - 1))
                ps.data[bs] = hpm
                for a in range(NCH):
                    hfm = pr.op("pe", lambda t, m=m, a=a, bf=bf, nolora=nolora: t.matmul(
                        ps.t[bf][:, 0:TW2], Fkq[:, a, 128 * m:128 * (m + 1)], xnb[:, a, :],
                        start=(a == 0), stop=(nolora and a == NCH - 1)),
                        deps=[hqdone, hxnb, ps.free[bf]] if a == 0 else (), tick=(nolora and a == NCH - 1))
                if not nolora:
                    hfm = pr.op("pe", lambda t, m=m, bf=bf: t.matmul(
                        ps.t[bf][:, 0:TW2], fbbb[:, 128 * m:128 * (m + 1)], lorab[:, :],
                        start=False, stop=True), deps=[hl2, hc6])
                ps.data[bf] = hfm
                sl = m % 2
                hsev = pr.op("dve", lambda v, m=m, bs=bs, sl=sl: v.tensor_scalar(
                    sev[:, sl, :], ps.t[bs][:, 0:TW2], pbs[:, m, 0:1], None, ALU.add),
                    deps=[hpm, hc3, pr.stf.get(("sev", sl))])
                pr.stf[("sev", sl)] = pr.dma("sp", o_s1[128 * m:128 * (m + 1), t0:t0 + TW2], sev[:, sl, :],
                                             deps=[hsev])
                ps.free[bs] = hsev
                pr.stf[("sev", sl)] = hsev
                hfev = pr.op("dve", lambda v, m=m, bf=bf, sl=sl: v.tensor_scalar(
                    fev[:, sl, :], ps.t[bf][:, 0:TW2], sscl[:, 0:1], fbs[:, m, 0:1], ALU.mult, ALU.add),
                    deps=[hfm, hc4, hq6, pr.stf.get(("fev", sl))])
                hfd = pr.dma("sp", o_f[128 * m:128 * (m + 1), t0:t0 + TW2], fev[:, sl, :], deps=[hfev])
                ps.free[bf] = hfev
                pr.stf[("fev", sl)] = hfev
                hx1_ = pr.dma("sp", x1t[:, sl, :], x1T[128 * m:128 * (m + 1), t0:t0 + TW2],
                              deps=[pr.stf.get(("x1", sl))])
                hsl_ = pr.op("act", lambda s, sl=sl: s.activation(fev[:, sl, :], fev[:, sl, :], AF.Silu,
                                                                  bias=0.0, scale=1.0), deps=[hfev, hfd],
                             selfsync=False)
                pr.op("dve", lambda v, sl=sl: v.tensor_tensor(fev[:, sl, :], fev[:, sl, :], sev[:, sl, :],
                                                              ALU.mult), deps=[hsl_], tick=False)
                hout = pr.op("dve", lambda v, sl=sl: v.tensor_tensor(fev[:, sl, :], fev[:, sl, :], x1t[:, sl, :],
                                                                     ALU.add), deps=[hx1_])
                hos = pr.dma("sp", o_out[128 * m:128 * (m + 1), t0:t0 + TW2], fev[:, sl, :], deps=[hout])
                pr.stf[("x1", sl)] = hos
                pr.stf[("fev", sl)] = hos
        pr.finalize([pr.lastd("sp"), pr.lastd("pool"), pr.last("dve"), pr.last("pe"), pr.last("act")])
    return pr



def build_launch5(h0, h1, NJT=32, NI=None):
    pr = Prog("l5")
    qTb = pr.din("qTb", [C, T // 2], U32)
    Ktm = pr.din("Ktm", [H, T, DH // 2], U32)
    Qtm = pr.din("Qtm", [H, T, DH // 2], U32)
    Vtm = pr.din("Vtm", [H, T, DH // 2], U32)
    cid = pr.din("cid", [128, 128])
    o_yg = pr.dout("ygT", [C, T])
    FI = 1024
    NI = NI or (T // FI)
    LAM = 4.0
    ACT_D = 14

    with pr.es:
        onesrow_f = pr.sbuf("onesrow_f", [1, 128], F32)
        onesrow_b = pr.sbuf("onesrow_b", [1, 128], BF16)
        idb = pr.sbuf("idb", [128, 128], BF16)
        idb32 = pr.sbuf("idb32", [128, 128], F32)
        qrow = pr.sbuf("qrow", [1, 2, FI], BF16)
        qreps = pr.sbuf("qreps", [128, 64, FI], BF16)
        Km = pr.sbuf("Km", [128, NJT, DH], BF16)
        K32 = pr.sbuf("K32", [128, NJT, DH], F32)
        K32n = pr.sbuf("K32n", [128, NJT, DH], F32)
        Qm = pr.sbuf("Qm", [128, NJT, DH], BF16)
        Vm = pr.sbuf("Vm", [128, NJT, DH + 1], BF16)
        Sk = pr.sbuf("Sk", [128, NJT, 32], F32)
        nSk = pr.sbuf("nSk", [128, NJT], F32)
        Sq = pr.sbuf("Sq", [128, NJT, 32], F32)
        epsb = pr.sbuf("epsb", [128, NJT], F32)
        relu_t = pr.sbuf("relu_t", [128, 4, FI], BF16)
        kernb = pr.sbuf("kernb", [128, 2, FI], BF16)
        outi = pr.sbuf("outi", [128, 8, DH + 1], F32)
        den = pr.sbuf("den", [128, 1], F32)
        lnepsb = pr.sbuf("lnepsb", [128, 1], F32)
        ytile = pr.sbuf("ytile", [128, 2, DH], F32)
        tev = pr.sbuf("tev", [64, 2, 128], F32)
        ps = Banks([pr.psum(f"ps{i}", [128, 512]) for i in range(4)])
        psA = pr.psum("psA", [128, 2048])

        hcc = pr.dma("pool", idb[:], cid)
        hc2 = pr.dma("sp", idb32[:], cid)
        h0_ = pr.op("dve", lambda v: v.memset(onesrow_f[:], 1.0), deps=[hcc, hc2])
        h0b = pr.op("dve", lambda v: v.tensor_copy(onesrow_b[:], onesrow_f[:]))
        pr.op("dve", lambda v: v.memset(lnepsb[:], float(np.log(1e-6))), tick=False)
        pr.stf = {}
        hjoin = h0b

        for h in range(h0, h1):
            hK = pr.dma("pool", Km[:], Ktm[h].bitcast(BF16).rearrange("(a p) d -> p a d", p=128), deps=[hjoin])
            hK2 = pr.op("dve", lambda v: v.tensor_copy(K32[:].rearrange("p a d -> p (a d)"),
                                                       Km[:].rearrange("p a d -> p (a d)")), deps=[hK, hjoin])
            hK3 = pr.op("dve", lambda v: v.tensor_scalar(K32n[:].rearrange("p a d -> p (a d)"),
                                                         K32[:].rearrange("p a d -> p (a d)"),
                                                         -1.0, None, ALU.mult), deps=[hK2])
            hQm = pr.dma("pool", Qm[:], Qtm[h].bitcast(BF16).rearrange("(a p) d -> p a d", p=128), deps=[hjoin])
            hV = pr.dma("pool", Vm[:, :, 0:DH], Vtm[h].bitcast(BF16).rearrange("(a p) d -> p a d", p=128),
                        deps=[hjoin])
            hV1 = pr.op("dve", lambda v: v.memset(Vm[:, :, DH:DH + 1], 1.0), deps=[hV])
            hs = pr.op("dve", lambda v: v.tensor_tensor(Sk[:], K32[:, :, 0:32], K32[:, :, 32:64], ALU.add),
                       deps=[hK2])
            w = 32
            while w > 1:
                w //= 2
                hs = pr.op("dve", lambda v, w=w: v.tensor_tensor(Sk[:, :, 0:w], Sk[:, :, 0:w], Sk[:, :, w:2 * w],
                                                                 ALU.add), deps=[hs])
            hsk = pr.op("dve", lambda v: v.tensor_scalar(nSk[:], Sk[:, :, 0], -LAM / DH, None, ALU.mult),
                        deps=[hs])
            hq32 = pr.op("dve", lambda v: v.tensor_copy(Sq[:, :, 0:32], Qm[:, :, 0:32]), deps=[hQm])
            hq3 = pr.op("dve", lambda v: v.tensor_tensor(Sq[:, :, 0:32], Sq[:, :, 0:32], Qm[:, :, 32:64], ALU.add),
                        deps=[hq32])
            w = 32
            while w > 1:
                w //= 2
                hq3 = pr.op("dve", lambda v, w=w: v.tensor_tensor(Sq[:, :, 0:w], Sq[:, :, 0:w], Sq[:, :, w:2 * w],
                                                                  ALU.add), deps=[hq3])
            heps = pr.op("act", lambda s: s.activation(epsb[:], Sq[:, :, 0], AF.Exp,
                                                       bias=lnepsb[:], scale=LAM / DH), deps=[hq3])
            for ic in range(NI):
                i0 = ic * FI
                for d in range(DH):
                    sl = d % 2
                    bq = 2 + sl
                    hr = pr.dma("sp", qrow[:, sl, :],
                                qTb[64 * h + d:64 * h + d + 1, i0 // 2:(i0 + FI) // 2].bitcast(BF16),
                                deps=[pr.stf.get(("qrow", sl))])
                    for half in range(2):
                        hm = pr.op("pe", lambda t, sl=sl, bq=bq, half=half: t.matmul(
                            ps.t[bq][:, 0:512], onesrow_b[:], qrow[0:1, sl, 512 * half:512 * (half + 1)],
                            start=True, stop=True),
                            deps=[hr, h0b, ps.free[bq], pr.stf.get("qrepG")])
                        ps.data[bq] = hm
                        hev = pr.op("dve", lambda v, d=d, bq=bq, half=half: v.tensor_copy(
                            qreps[:, d, 512 * half:512 * (half + 1)], ps.t[bq][:, 0:512]),
                            deps=[hm])
                        ps.free[bq] = hev
                    pr.stf[("qrow", sl)] = hev
                hqrall = pr.last("dve")
                for jt in range(NJT):
                    accslot = jt % 2
                    for d in range(DH):
                        rs = d % 4
                        if d % 64 < ACT_D:
                            hrl = pr.op("act", lambda s, d=d, jt=jt, rs=rs: s.activation(
                                relu_t[:, rs, :], qreps[:, d, :], AF.Relu,
                                bias=K32n[:, jt, d:d + 1], scale=1.0),
                                deps=[hqrall, hK3, pr.stf.get(("rt", rs))], selfsync=False)
                        else:
                            hrl = pr.op("dve", lambda v, d=d, jt=jt, rs=rs: v.tensor_scalar(
                                relu_t[:, rs, :], qreps[:, d, :], K32[:, jt, d:d + 1], 0.0,
                                ALU.subtract, ALU.max),
                                deps=[hqrall, pr.stf.get(("rt", rs))], selfsync=False)
                        for half in range(2):
                            hax = pr.op("pe", lambda t, d=d, rs=rs, half=half, accslot=accslot: t.matmul(
                                psA[:, 512 * (2 * accslot + half):512 * (2 * accslot + half) + 512],
                                idb[:], relu_t[:, rs, 512 * half:512 * (half + 1)],
                                start=(d == 0), stop=(d == DH - 1)),
                                deps=([hrl] + ([pr.stf.get(("acc", accslot))] if d == 0 else [])),
                                tick=(half == 1), selfsync=False)
                        pr.stf[("rt", rs)] = hax
                    hacc = pr.last("pe")
                    ks = jt % 2
                    hkn = []
                    for half in range(2):
                        hkn.append(pr.op("act", lambda s, jt=jt, ks=ks, half=half, accslot=accslot: s.activation(
                            kernb[:, ks, 512 * half:512 * (half + 1)],
                            psA[:, 512 * (2 * accslot + half):512 * (2 * accslot + half) + 512],
                            AF.Exp, bias=nSk[:, jt:jt + 1], scale=-2.0 * LAM / DH),
                            deps=[hacc, hsk, pr.stf.get(("kern", ks))], selfsync=False))
                    pr.stf[("acc", accslot)] = hkn[-1]
                    for isl in range(8):
                        bo = isl % 2
                        him = pr.op("pe", lambda t, jt=jt, ks=ks, isl=isl, bo=bo: t.matmul(
                            ps.t[bo][:, 0:DH + 1], kernb[:, ks, 128 * isl:128 * (isl + 1)],
                            Vm[:, jt, :], start=True, stop=True),
                            deps=[hkn[isl // 4], hV1, ps.free[bo]], selfsync=False)
                        ps.data[bo] = him
                        if jt == 0:
                            hadd = pr.op("dve", lambda v, isl=isl, bo=bo: v.tensor_copy(
                                outi[:, isl, :], ps.t[bo][:, 0:DH + 1]),
                                deps=[him, pr.stf.get(("outi", isl))])
                        else:
                            hadd = pr.op("dve", lambda v, isl=isl, bo=bo: v.tensor_tensor(
                                outi[:, isl, :], outi[:, isl, :], ps.t[bo][:, 0:DH + 1], ALU.add),
                                deps=[him])
                        ps.free[bo] = hadd
                        if isl == 7:
                            pr.stf[("kern", ks)] = hadd
                for isl in range(8):
                    gi = ic * 8 + isl
                    hd1 = pr.op("dve", lambda v, isl=isl, gi=gi: v.tensor_tensor(
                        den[:], outi[:, isl, DH:DH + 1], epsb[:, gi:gi + 1], ALU.add), deps=[heps])
                    hd2 = pr.op("dve", lambda v: v.reciprocal(den[:], den[:]), deps=[hd1])
                    hd3 = pr.op("dve", lambda v, isl=isl: v.tensor_scalar(
                        ytile[:, isl % 2, :], outi[:, isl, 0:DH], den[:], None, ALU.mult),
                        deps=[hd2, pr.stf.get(("yt", isl % 2))])
                    pr.stf[("outi", isl)] = hd3
                    hT = pr.op("pe", lambda t, isl=isl: t.transpose(ps.t[0][0:64, 0:128], ytile[:, isl % 2, :],
                                                                   idb32[:]),
                               deps=[hd3, ps.free[0], hc2])
                    ps.data[0] = hT
                    hE = pr.op("dve", lambda v, isl=isl: v.tensor_copy(tev[:, isl % 2, :], ps.t[0][0:64, 0:128]),
                               deps=[hT, pr.stf.get(("tev", isl % 2))])
                    ps.free[0] = hE
                    pr.stf[("yt", isl % 2)] = hE
                    hD = pr.dma("sp", o_yg[64 * h:64 * (h + 1), i0 + 128 * isl:i0 + 128 * (isl + 1)],
                                tev[:, isl % 2, :], deps=[hE])
                    pr.stf[("tev", isl % 2)] = hD
                pr.stf["qrepG"] = pr.tick("dve", deps=[pr.last("pe"), pr.last("act")])
            hjoin = pr.tick("dve", deps=[pr.last("pe"), pr.last("act"), pr.lastd("sp"), pr.lastd("pool")])

        pr.finalize([pr.lastd("sp"), pr.lastd("pool"), pr.last("dve"), pr.last("pe"), pr.last("act")])
    return pr



_PROGS = {}


def _progs():
    if not _PROGS:
        specs = {
            "l1": build_launch1, "l2": build_launch2, "l3": build_launch3,
            "l4": build_launch4, "l5a": lambda: build_launch5(0, 8),
            "l5b": lambda: build_launch5(8, 16), "l6": build_launch6,
        }
        for k, f in specs.items():
            pr = f()
            _PROGS[k] = pr.build()
    return _PROGS


def kernel(**inp):
    f32 = np.float32
    progs = _progs()
    NCORES = 8
    x = np.asarray(inp["L0_attn_x"], f32)

    eye = np.eye(128, dtype=f32)
    ones = np.ones((128, 1), f32)
    maa = np.ascontiguousarray(np.stack([np.asarray(inp[f"L0_attn_maa_{c}"], f32).reshape(-1)
                                         for c in "xkvr"], 1))
    w2s = np.ascontiguousarray(np.asarray(inp["L0_attn_w2"], f32).reshape(96, 1024))
    cmask = np.ones((128, 1024), f32)
    cmask[:, ::64] = 0.0
    tri = (np.arange(64)[:, None] < np.arange(64)[None, :]).astype(f32)
    ctri = np.ascontiguousarray(np.concatenate([tri, tri], 0))
    cb2 = np.zeros((128, 128), f32)
    cb2[0:64, 0] = 1.0
    cb2[64:128, 64] = 1.0
    csel = np.ascontiguousarray(np.stack(
        [np.tile((np.arange(8) == e)[:, None].astype(f32), (1, 128)) for e in range(8)], 1))
    cones2 = np.ones((128, 64), f32)
    WqT = np.ascontiguousarray(np.asarray(inp["L1_attn_qkv_w"], f32).T)
    PwT = np.ascontiguousarray(np.asarray(inp["L1_attn_proj_w"], f32).T)
    dwcw = np.asarray(inp["L1_attn_dwc_w"], f32)
    dwcc = np.ascontiguousarray(np.concatenate(
        [dwcw[:, 0, :], np.asarray(inp["L1_attn_dwc_b"], f32)[:, None]], 1))
    fbbp = np.ascontiguousarray(np.concatenate(
        [np.asarray(inp["L1_ffn_b"], f32), np.zeros((112, 1024), f32)], 0))

    def run(key, maps):
        res = run_bass_kernel_spmd(progs[key], maps, core_ids=list(range(NCORES)))
        return res.results

    m1 = [{
        "xT": np.ascontiguousarray(x[b].T),
        "maa": maa, "w1": np.asarray(inp["L0_attn_w1"], f32), "w2s": w2s,
        "Wr": np.asarray(inp["L0_attn_Wr"], f32), "Wk": np.asarray(inp["L0_attn_Wk"], f32),
        "Wvg": np.asarray(inp["L0_attn_Wvg"], f32), "cones": ones,
    } for b in range(NCORES)]
    r1 = run("l1", m1)

    m2 = [{
        "rT": r1[b]["rT"], "wT": r1[b]["wT"], "vT": r1[b]["vT"], "ffnT": r1[b]["ffnT"],
        "Wo": np.asarray(inp["L0_attn_Wo"], f32),
        "uvec": np.asarray(inp["L0_attn_u"], f32).reshape(1024, 1),
        "lng": np.asarray(inp["L0_attn_ln_g"], f32).reshape(1024, 1),
        "lnb": np.asarray(inp["L0_attn_ln_b"], f32).reshape(1024, 1),
        "cones": ones, "cmask": cmask, "ctri": ctri, "cb2": cb2, "cones2": cones2, "cid": eye,
    } for b in range(NCORES)]
    r2 = run("l2", m2)

    m3 = [{
        "xT": m1[b]["xT"], "sT": r2[b]["sT"], "xnT": r1[b]["xnT"],
        "rg": np.asarray(inp["L0_ffn_router_gate"], f32),
        "gw": np.asarray(inp["L0_ffn_gate_w"], f32), "uw": np.asarray(inp["L0_ffn_up_w"], f32),
        "dw": np.asarray(inp["L0_ffn_down_w"], f32),
        "cones": ones, "cid": eye, "csel": csel,
    } for b in range(NCORES)]
    r3 = run("l3", m3)

    m4 = [{
        "x1T": r3[b]["x1T"], "WqT": WqT,
        "qkvb": np.asarray(inp["L1_attn_qkv_b"], f32).reshape(-1, 1),
        "cones": ones, "cid": eye,
    } for b in range(NCORES)]
    r4 = run("l4", m4)

    m5 = [{
        "qTb": r4[b]["qTb"], "Ktm": r4[b]["Ktm"], "Qtm": r4[b]["Qtm"], "Vtm": r4[b]["Vtm"],
        "cid": eye,
    } for b in range(NCORES)]
    r5a = run("l5a", m5)
    r5b = run("l5b", m5)

    m6 = [{
        "x1T": r3[b]["x1T"], "xn1T": r4[b]["xn1T"],
        "ygT": np.ascontiguousarray(np.concatenate([np.asarray(r5a[b]["ygT"])[0:512],
                                                    np.asarray(r5b[b]["ygT"])[512:1024]], 0)),
        "vT": r4[b]["vT"], "dwc": dwcc, "PwT": PwT,
        "pb": np.asarray(inp["L1_attn_proj_b"], f32).reshape(-1, 1),
        "Fk": np.asarray(inp["L1_ffn_kernel"], f32),
        "fb": np.asarray(inp["L1_ffn_bias"], f32).reshape(-1, 1),
        "fa": np.asarray(inp["L1_ffn_a"], f32), "fbb": fbbp,
        "cones": ones, "cid": eye,
    } for b in range(NCORES)]
    r6 = run("l6", m6)

    out = np.empty((NCORES, T, C), f32)
    for b in range(NCORES):
        out[b] = np.asarray(r6[b]["outT"], f32).T
    return out


# version 02 (final)
# speedup vs baseline: 1.0159x; 1.0159x over previous
import contextlib
import numpy as np
import concourse.bass as bass
import concourse.mybir as mybir
from concourse.bass_utils import run_bass_kernel_spmd

F32 = mybir.dt.float32
F32R = mybir.dt.float32r
BF16 = mybir.dt.bfloat16
I32 = mybir.dt.int32
U32 = mybir.dt.uint32
AF = mybir.ActivationFunctionType
ALU = mybir.AluOpType

C = 1024
T = 4096
NCH = 8
H = 16
DH = 64
LNEPS = 1e-5
MAGIC = 1.5 * 2.0 ** 23


class Prog:
    QUEUES = ("pe", "dve", "act", "pool", "sp")

    def __init__(self, name="prog"):
        self.nc = bass.Bass("TRN2", target_bir_lowering=False)
        self.es = contextlib.ExitStack()
        self.items = {q: [] for q in self.QUEUES}
        self.sems = {}
        self.cnt = {}
        self.waited = {q: {} for q in self.QUEUES}
        self._final = []
        self.selfsync_default = True
        self.prefix = ""
        self._scopes = []
        self.io = None

    def scope(self, prefix):
        prog = self

        class _S:
            def __enter__(s):
                prog.prefix = prefix
                es = contextlib.ExitStack()
                prog._scopes.append(es)
                return es

            def __exit__(s, *a):
                prog._scopes.pop().close()
                prog.prefix = ""
                return False
        return _S()

    def barrier(self):
        deps = [self.last(q) for q in self.QUEUES] + [self.lastd(q) for q in self.QUEUES
                                                      if (q + "d") in self.cnt]
        hs = []
        for q in self.QUEUES:
            hs.append(self.tick(q, deps=deps))
        for q in self.QUEUES:
            self.tick(q, deps=hs)

    def sbuf(self, name, shape, dt):
        es = self._scopes[-1] if self._scopes else self.es
        return es.enter_context(self.nc.sbuf_tensor(self.prefix + name, list(shape), dt))

    def psum(self, name, shape, dt=F32):
        es = self._scopes[-1] if self._scopes else self.es
        return es.enter_context(self.nc.psum_tensor(self.prefix + name, list(shape), dt))

    def din(self, name, shape, dt=F32):
        if self.io is not None:
            if name not in self.io:
                self.io[name] = self.nc.dram_tensor(name, list(shape), dt, kind="ExternalInput").ap()
            return self.io[name]
        return self.nc.dram_tensor(name, list(shape), dt, kind="ExternalInput").ap()

    def dout(self, name, shape, dt=F32, final=False):
        if self.io is not None:
            if name not in self.io:
                kind = "ExternalOutput" if final else "Internal"
                self.io[name] = self.nc.dram_tensor(name, list(shape), dt, kind=kind).ap()
            return self.io[name]
        return self.nc.dram_tensor(name, list(shape), dt, kind="ExternalOutput").ap()

    def _sem(self, key):
        if key not in self.sems:
            self.sems[key] = self.es.enter_context(self.nc.semaphore("sem_" + key))
            self.cnt[key] = 0
        return self.sems[key]

    def op(self, q, fn, deps=(), tick=True, selfsync=None):
        key = q
        sem = self._sem(key)
        waits = self._mk_waits(q, deps)
        if selfsync is None:
            selfsync = getattr(self, "selfsync_default", False)
        if selfsync and self.cnt.get(key, 0) > 0 and tick:
            waits = [(sem, self.cnt[key])] + waits
        if tick:
            self.cnt[key] += 1
            val = self.cnt[key]

            def clo(eng, fn=fn, waits=waits, sem=sem):
                for w in waits:
                    eng.wait_ge(*w)
                ins = fn(eng)
                if ins is None:
                    ins = eng.nop()
                ins.then_inc(sem, 1)
            self.items[q].append(clo)
            return (key, val, 1)
        else:
            def clo(eng, fn=fn, waits=waits):
                for w in waits:
                    eng.wait_ge(*w)
                fn(eng)
            self.items[q].append(clo)
            return None

    def tick(self, q, deps=()):
        return self.op(q, lambda e: e.nop(), deps=deps)

    def dma(self, q, out, in_, deps=(), **kw):
        key = q + "d"
        sem = self._sem(key)
        waits = self._mk_waits(q, deps)
        self.cnt[key] += 16
        val = self.cnt[key]

        def clo(eng, out=out, in_=in_, waits=waits, sem=sem, kw=kw):
            for w in waits:
                eng.wait_ge(*w)
            eng.dma_start(out=out, in_=in_, **kw).then_inc(sem, 16)
        self.items[q].append(clo)
        return (key, val, 16)

    def _mk_waits(self, q, deps):
        waits = []
        for d in deps:
            if d is None:
                continue
            key, val, scale = d
            if key == q:
                continue
            if self.waited[q].get(key, 0) >= val:
                continue
            self.waited[q][key] = val
            waits.append((self.sems[key], val))
        return waits

    def last(self, q):
        key = q
        return (key, self.cnt.get(key, 0), 1)

    def lastd(self, q):
        key = q + "d"
        return (key, self.cnt.get(key, 0), 16)

    def finalize(self, deps):
        self._final = list(deps)

    def build(self):
        block = self.es.enter_context(self.nc.Block())
        prog = self

        for d in self._final:
            if d is None:
                continue
            key, val, scale = d
            sem = self.sems[key]
            self.items["pool"].append(lambda eng, sem=sem, val=val: eng.wait_ge(sem, val))

        @block.tensor
        def _(eng):
            for f in prog.items["pe"]:
                f(eng)

        @block.vector
        def _(eng):
            for f in prog.items["dve"]:
                f(eng)

        @block.scalar
        def _(eng):
            for f in prog.items["act"]:
                f(eng)

        @block.gpsimd
        def _(eng):
            for f in prog.items["pool"]:
                f(eng)

        @block.sync
        def _(eng):
            for f in prog.items["sp"]:
                f(eng)

        return self.nc


def chunked(ap128x, a):
    return ap128x[128 * a:128 * (a + 1), :]


def rearr(ap, p=128):
    return ap.rearrange("(a p) w -> p a w", p=p)



def emit_layernorm(pr, W, xt, xn, sqr, rows, ones_f, onesrow_f, psStat, psB, ps,
                   xdep, gamma=None, beta=None):
    rowA, rowB, rowC = rows
    b0, b1 = psStat
    b2, b3 = psB
    hrot = [pr.last("pe"), pr.last("pe")]
    for a in range(NCH):
        sl = a % 2
        hsq = pr.op("act", lambda s, a=a, sl=sl: s.activation(sqr[:, sl, :W], xt[:, a, :W], AF.Square,
                                                              bias=0.0, scale=1.0), deps=[xdep, hrot[sl]])
        pr.op("pe", lambda t, a=a: t.matmul(ps.t[b0][0:1, :W], ones_f[:], xt[:, a, :W],
                                            start=(a == 0), stop=(a == NCH - 1)),
              deps=[xdep, ps.free[b0]], tick=False)
        hpe = pr.op("pe", lambda t, a=a, sl=sl: t.matmul(ps.t[b1][0:1, :W], ones_f[:], sqr[:, sl, :W],
                                                         start=(a == 0), stop=(a == NCH - 1)),
                    deps=[hsq, ps.free[b1]])
        hrot[sl] = hpe
    hmm = pr.last("pe")
    ps.data[b0] = hmm
    ps.data[b1] = hmm
    pr.op("dve", lambda v: v.tensor_scalar(rowA[:, :W], ps.t[b0][0:1, :W], 1.0 / C, None, ALU.mult),
          deps=[hmm], tick=False)
    h1 = pr.op("dve", lambda v: v.tensor_scalar(rowB[:, :W], ps.t[b1][0:1, :W], 1.0 / C, None, ALU.mult))
    ps.free[b0] = h1
    ps.free[b1] = h1
    h2 = pr.op("act", lambda s: s.activation(rowC[:, :W], rowA[:, :W], AF.Square, bias=0.0, scale=1.0), deps=[h1])
    pr.op("dve", lambda v: v.tensor_tensor(rowB[:, :W], rowB[:, :W], rowC[:, :W], ALU.subtract), deps=[h2], tick=False)
    pr.op("dve", lambda v: v.tensor_scalar(rowB[:, :W], rowB[:, :W], LNEPS, None, ALU.add), tick=False)
    h3 = pr.op("dve", lambda v: v.reciprocal(rowB[:, :W], rowB[:, :W]))
    h4 = pr.op("act", lambda s: s.activation(rowC[:, :W], rowB[:, :W], AF.Sqrt, bias=0.0, scale=1.0), deps=[h3])
    pr.op("pe", lambda t: t.matmul(ps.t[b2][:, :W], onesrow_f[:], rowA[:, :W], start=True, stop=True),
          deps=[h4, ps.free[b2]], tick=False)
    h6 = pr.op("pe", lambda t: t.matmul(ps.t[b3][:, :W], onesrow_f[:], rowC[:, :W], start=True, stop=True),
               deps=[ps.free[b3]])
    ps.data[b2] = h6
    ps.data[b3] = h6
    for a in range(NCH):
        pr.op("dve", lambda v, a=a: v.tensor_tensor(xn[:, a, :W], xt[:, a, :W], ps.t[b2][:, :W], ALU.subtract),
              deps=[h6], tick=False)
        hxn = pr.op("dve", lambda v, a=a: v.tensor_tensor(xn[:, a, :W], xn[:, a, :W], ps.t[b3][:, :W], ALU.mult),
                    tick=(a == NCH - 1 and gamma is None))
        if gamma is not None:
            hxn = pr.op("dve", lambda v, a=a: v.tensor_scalar(
                xn[:, a, :W], xn[:, a, :W], gamma[:, a, 0:1], beta[:, a, 0:1], ALU.mult, ALU.add),
                tick=(a == NCH - 1))
    ps.free[b2] = hxn
    ps.free[b3] = hxn
    return hxn


class Banks:

    def __init__(self, tensors):
        self.t = tensors
        self.data = [None] * len(tensors)
        self.free = [None] * len(tensors)



def build_launch1(TW=256, phases=99, pr=None):
    NT = T // TW
    standalone = pr is None
    if standalone:
        pr = Prog("l1")
    xT = pr.din("xT", [C, T])
    maa = pr.din("maa", [C, 4])
    w1 = pr.din("w1", [C, 96])
    w2s = pr.din("w2s", [96, C])
    Wr = pr.din("Wr", [C, C])
    Wk = pr.din("Wk", [C, C])
    Wvg = pr.din("Wvg", [C, 6 * C])
    cones = pr.din("cones", [128, 1])

    o_xn = pr.dout("xnT", [C, T])
    o_r = pr.dout("rT", [C, T])
    o_w = pr.dout("wT", [C, T])
    o_v = pr.dout("vT", [C, T])
    o_ffn = pr.dout("ffnT", [2 * C, T // 2], U32)
    o_dbg = pr.dout("dbg", [4, T])

    with (pr.es if standalone else pr.scope("l1_")):
        w1b = pr.sbuf("w1b", [128, NCH, 96], BF16)
        w2b = pr.sbuf("w2b", [96, C], BF16)
        Wrb = pr.sbuf("Wrb", [128, NCH, C], BF16)
        Wkb = pr.sbuf("Wkb", [128, NCH, C], BF16)
        Wvgb = pr.sbuf("Wvgb", [128, NCH, 6 * C], BF16)
        maas = pr.sbuf("maas", [128, NCH, 4], F32)
        ones_f = pr.sbuf("ones_f", [128, 1], F32)
        onesrow_f = pr.sbuf("onesrow_f", [1, 128], F32)
        onesrow_r = pr.sbuf("onesrow_r", [1, 128], F32R)

        xt = pr.sbuf("xt", [128, NCH, TW], F32)
        xn = pr.sbuf("xn", [128, NCH, TW], F32)
        xx = pr.sbuf("xx", [128, NCH, TW], F32)
        sqr = pr.sbuf("sqr", [128, 2, TW], F32)
        carry = pr.sbuf("carry", [128, NCH, 1], F32)
        rowA = pr.sbuf("rowA", [1, TW], F32)
        rowB = pr.sbuf("rowB", [1, TW], F32)
        rowC = pr.sbuf("rowC", [1, TW], F32)
        rowAr = pr.sbuf("rowAr", [1, TW], F32R)
        rowBr = pr.sbuf("rowBr", [1, TW], F32R)
        xxxb = pr.sbuf("xxxb", [128, NCH, TW], BF16)
        mtb = pr.sbuf("mtb", [96, TW], BF16)
        tmp1 = pr.sbuf("tmp1", [128, TW], F32)
        tmp2 = pr.sbuf("tmp2", [128, TW], F32)
        xkb = pr.sbuf("xkb", [128, NCH, TW], BF16)
        xvb = pr.sbuf("xvb", [128, NCH, TW], BF16)
        xrb = pr.sbuf("xrb", [128, NCH, TW], BF16)
        gsil = pr.sbuf("gsil", [128, 2, TW], F32)
        evac = pr.sbuf("evac", [128, 4, TW], F32)
        evb = pr.sbuf("evb", [128, 2, TW], BF16)

        ps = Banks([pr.psum(f"ps{i}", [128, 512]) for i in range(8)])

        hw = []
        hw.append(pr.dma("pool", w1b[:], w1.rearrange("(a p) n -> p a n", p=128)))
        hw.append(pr.dma("pool", w2b[:], w2s))
        hw.append(pr.dma("pool", Wrb[:], Wr.rearrange("(a p) n -> p a n", p=128)))
        hw.append(pr.dma("pool", Wkb[:], Wk.rearrange("(a p) n -> p a n", p=128)))
        hw.append(pr.dma("pool", Wvgb[:], Wvg.rearrange("(a p) n -> p a n", p=128)))
        hw.append(pr.dma("pool", maas[:], maa.rearrange("(a p) n -> p a n", p=128)))
        hw.append(pr.dma("pool", ones_f[:], cones))
        hwl = hw[-1]
        pr.op("dve", lambda v: v.memset(onesrow_f[:], 1.0), deps=[hwl], tick=False)
        pr.op("dve", lambda v: v.tensor_copy(onesrow_r[:], onesrow_f[:]), tick=False)
        pr.op("dve", lambda v: v.memset(carry[:], 0.0), tick=False)
        hpre = pr.tick("dve", deps=hw)

        hxstore = None
        hmixfree = [hpre] * NCH
        hxnfree = hpre
        hxxfree = hpre

        for k in range(NT):
            t0 = k * TW
            hx = pr.dma("sp", xt[:], xT[:, t0:t0 + TW].rearrange("(a p) w -> p a w", p=128),
                        deps=[hxnfree])
            hrot = [pr.last("pe"), pr.last("pe")]
            for a in range(NCH):
                sl = a % 2
                hsq = pr.op("act", lambda s, a=a, sl=sl: s.activation(sqr[:, sl, :], xt[:, a, :], AF.Square,
                                                                      bias=0.0, scale=1.0), deps=[hx, hrot[sl]])
                pr.op("pe", lambda t, a=a: t.matmul(ps.t[0][0:1, :TW], ones_f[:], xt[:, a, :],
                                                    start=(a == 0), stop=(a == NCH - 1)),
                      deps=[hx, ps.free[0]], tick=False)
                hpe = pr.op("pe", lambda t, a=a, sl=sl: t.matmul(ps.t[1][0:1, :TW], ones_f[:], sqr[:, sl, :],
                                                                 start=(a == 0), stop=(a == NCH - 1)),
                            deps=[hsq, ps.free[1]])
                hrot[sl] = hpe
            hmm = pr.last("pe")
            ps.data[0] = hmm
            ps.data[1] = hmm
            pr.op("dve", lambda v: v.tensor_scalar(rowA[:], ps.t[0][0:1, :TW], 1.0 / C, None, ALU.mult),
                  deps=[hmm], tick=False)
            h1 = pr.op("dve", lambda v: v.tensor_scalar(rowB[:], ps.t[1][0:1, :TW], 1.0 / C, None, ALU.mult))
            ps.free[0] = h1
            ps.free[1] = h1
            h2 = pr.op("act", lambda s: s.activation(rowC[:], rowA[:], AF.Square, bias=0.0, scale=1.0), deps=[h1])
            pr.op("dve", lambda v: v.tensor_tensor(rowB[:], rowB[:], rowC[:], ALU.subtract), deps=[h2], tick=False)
            pr.op("dve", lambda v: v.tensor_scalar(rowB[:], rowB[:], LNEPS, None, ALU.add), tick=False)
            h3 = pr.op("dve", lambda v: v.reciprocal(rowB[:], rowB[:]))
            h4 = pr.op("act", lambda s: s.activation(rowC[:], rowB[:], AF.Sqrt, bias=0.0, scale=1.0), deps=[h3])
            h5 = pr.tick("dve", deps=[h4])
            pr.dma("sp", o_dbg[0:1, t0:t0 + TW], rowA[:], deps=[h5])
            pr.dma("sp", o_dbg[1:2, t0:t0 + TW], rowB[:], deps=[h5])
            pr.dma("sp", o_dbg[2:3, t0:t0 + TW], rowC[:], deps=[h5])
            pr.op("pe", lambda t: t.matmul(ps.t[2][:, :TW], onesrow_f[:], rowA[:], start=True, stop=True),
                  deps=[h4, ps.free[2]], tick=False)
            h6 = pr.op("pe", lambda t: t.matmul(ps.t[3][:, :TW], onesrow_f[:], rowC[:], start=True, stop=True),
                       deps=[ps.free[3]])
            ps.data[2] = h6
            ps.data[3] = h6
            for a in range(NCH):
                pr.op("dve", lambda v, a=a: v.tensor_tensor(xn[:, a, :], xt[:, a, :], ps.t[2][:, :TW], ALU.subtract),
                      deps=[h6, hxnfree], tick=False)
                hxn = pr.op("dve", lambda v, a=a: v.tensor_tensor(xn[:, a, :], xn[:, a, :], ps.t[3][:, :TW], ALU.mult),
                            tick=(a == NCH - 1))
            ps.free[2] = hxn
            ps.free[3] = hxn
            hxnst = pr.dma("sp", o_xn[:, t0:t0 + TW].rearrange("(a p) w -> p a w", p=128), xn[:], deps=[hxn])
            if phases < 2:
                hguard = pr.tick("dve", deps=[pr.lastd("sp"), pr.last("pe"), pr.last("act")])
                hxnfree = hguard
                hxxfree = hguard
                hmixfree = [hguard] * NCH
                continue
            for a in range(NCH):
                pr.op("dve", lambda v, a=a: v.tensor_tensor(xx[:, a, 1:TW], xn[:, a, 0:TW - 1], xn[:, a, 1:TW],
                                                            ALU.subtract), deps=[hxxfree], tick=False)
                hxx = pr.op("dve", lambda v, a=a: v.tensor_tensor(xx[:, a, 0:1], carry[:, a, :], xn[:, a, 0:1],
                                                                  ALU.subtract), tick=(a == NCH - 1))
            for a in range(NCH):
                pr.op("dve", lambda v, a=a: v.tensor_copy(carry[:, a, :], xn[:, a, TW - 1:TW]), tick=False)
            for a in range(NCH):
                hxxx = pr.op("dve", lambda v, a=a: v.scalar_tensor_tensor(xxxb[:, a, :], xx[:, a, :], maas[:, a, 0:1],
                                                                          xn[:, a, :], ALU.mult, ALU.add),
                             tick=(a == NCH - 1))
            if phases < 3:
                hguard = pr.tick("dve", deps=[pr.lastd("sp"), pr.last("pe"), pr.last("act")])
                hxnfree = hguard
                hxxfree = hguard
                hmixfree = [hguard] * NCH
                continue
            for a in range(NCH):
                hm1 = pr.op("pe", lambda t, a=a: t.matmul(ps.t[4][0:96, :TW], w1b[:, a, :], xxxb[:, a, :],
                                                          start=(a == 0), stop=(a == NCH - 1)),
                            deps=[hxxx, ps.free[4]], tick=(a == NCH - 1))
            ps.data[4] = hm1
            htanh = pr.op("act", lambda s: s.activation(mtb[:, :], ps.t[4][0:96, :TW], AF.Tanh, bias=0.0, scale=1.0),
                          deps=[hm1])
            ps.free[4] = htanh
            for a in range(NCH):
                hk = pr.op("pe", lambda t, a=a: t.matmul(ps.t[5][:, :TW], w2b[0:32, 128 * a:128 * (a + 1)],
                                                         mtb[0:32, :], start=True, stop=True),
                           deps=[htanh, ps.free[5], hmixfree[a]])
                hv = pr.op("pe", lambda t, a=a: t.matmul(ps.t[6][:, :TW], w2b[32:64, 128 * a:128 * (a + 1)],
                                                         mtb[32:64, :], start=True, stop=True), deps=[ps.free[6]])
                hr = pr.op("pe", lambda t, a=a: t.matmul(ps.t[7][:, :TW], w2b[64:96, 128 * a:128 * (a + 1)],
                                                         mtb[64:96, :], start=True, stop=True), deps=[ps.free[7]])
                for l, (dst, hsrc, bank, mc) in enumerate(((xkb, hk, 5, 1), (xvb, hv, 6, 2), (xrb, hr, 7, 3))):
                    pr.op("dve", lambda v, a=a, dst=dst, bank=bank, mc=mc:
                          v.tensor_scalar(tmp1[:], ps.t[bank][:, :TW], maas[:, a, mc:mc + 1], None, ALU.add),
                          deps=[hsrc, hxx], tick=False)
                    pr.op("dve", lambda v, a=a: v.tensor_tensor(tmp2[:], tmp1[:], xx[:, a, :], ALU.mult), tick=False)
                    hdone = pr.op("dve", lambda v, a=a, dst=dst: v.tensor_tensor(dst[:, a, :], tmp2[:],
                                                                                 xn[:, a, :], ALU.add))
                    ps.free[bank] = hdone
            hmixed = pr.last("dve")
            def proj(dst_dram, Wb, src, banks, tag):
                hst = []
                for m in range(NCH):
                    bk = banks[m % 2]
                    deps = [hmixed, ps.free[bk]]
                    for a in range(NCH):
                        hpe = pr.op("pe", lambda t, m=m, a=a, bk=bk, Wb=Wb, src=src:
                                    t.matmul(ps.t[bk][:, :TW], Wb[:, a, 128 * m:128 * (m + 1)], src[:, a, :],
                                             start=(a == 0), stop=(a == NCH - 1)),
                                    deps=deps if a == 0 else (), tick=(a == NCH - 1))
                    ps.data[bk] = hpe
                    sl = m % 4
                    hev = pr.op("dve", lambda v, bk=bk, sl=sl: v.tensor_copy(evac[:, sl, :], ps.t[bk][:, :TW]),
                                deps=[hpe, pr.stfree.get(("e", sl))])
                    ps.free[bk] = hev
                    hst.append(pr.dma("sp", dst_dram[128 * m:128 * (m + 1), t0:t0 + TW], evac[:, sl, :], deps=[hev]))
                    pr.stfree[("e", sl)] = hst[-1]
                return hst

            pr.stfree = getattr(pr, "stfree", {})
            if phases < 4:
                hguard = pr.tick("dve", deps=[pr.lastd("sp"), pr.last("pe"), pr.last("act")])
                hxnfree = hguard
                hxxfree = hguard
                hmixfree = [hguard] * NCH
                continue
            proj(o_r, Wrb, xrb, (0, 1), "r")
            proj(o_w, Wkb, xkb, (2, 3), "w")
            for m in range(24 if phases >= 5 else 0):
                bg = 4 + (m % 2)
                bvf = 6 + (m % 2)
                for a in range(NCH):
                    hg = pr.op("pe", lambda t, m=m, a=a, bg=bg: t.matmul(
                        ps.t[bg][:, :TW], Wvgb[:, a, 128 * (24 + m):128 * (25 + m)], xvb[:, a, :],
                        start=(a == 0), stop=(a == NCH - 1)),
                        deps=[hmixed, ps.free[bg]] if a == 0 else (), tick=(a == NCH - 1))
                ps.data[bg] = hg
                gs = m % 2
                hsil = pr.op("act", lambda s, gs=gs, bg=bg: s.activation(gsil[:, gs, :], ps.t[bg][:, :TW], AF.Silu,
                                                                         bias=0.0, scale=1.0),
                             deps=[hg, pr.stfree.get(("g", gs))])
                ps.free[bg] = hsil
                for a in range(NCH):
                    hvf = pr.op("pe", lambda t, m=m, a=a, bvf=bvf: t.matmul(
                        ps.t[bvf][:, :TW], Wvgb[:, a, 128 * m:128 * (m + 1)], xvb[:, a, :],
                        start=(a == 0), stop=(a == NCH - 1)),
                        deps=[ps.free[bvf]] if a == 0 else (), tick=(a == NCH - 1))
                ps.data[bvf] = hvf
                if m < 8:
                    sl = m % 4
                    hev = pr.op("dve", lambda v, bvf=bvf, gs=gs, sl=sl: v.tensor_tensor(
                        evac[:, sl, :], ps.t[bvf][:, :TW], gsil[:, gs, :], ALU.mult),
                        deps=[hvf, hsil, pr.stfree.get(("e", sl))])
                    ps.free[bvf] = hev
                    pr.stfree[("g", gs)] = hev
                    hst = pr.dma("sp", o_v[128 * m:128 * (m + 1), t0:t0 + TW], evac[:, sl, :], deps=[hev])
                    pr.stfree[("e", sl)] = hst
                else:
                    sl = m % 2
                    hev = pr.op("dve", lambda v, bvf=bvf, gs=gs, sl=sl: v.tensor_tensor(
                        evb[:, sl, :], ps.t[bvf][:, :TW], gsil[:, gs, :], ALU.mult),
                        deps=[hvf, hsil, pr.stfree.get(("f", sl))])
                    ps.free[bvf] = hev
                    pr.stfree[("g", gs)] = hev
                    hst = pr.dma("sp", o_ffn[128 * (m - 8):128 * (m - 7), t0 // 2:(t0 + TW) // 2],
                                 evb[:, sl, :].bitcast(U32), deps=[hev])
                    pr.stfree[("f", sl)] = hst
            hguard = pr.tick("dve", deps=[pr.lastd("sp"), pr.last("pe"), pr.last("act")])
            hxnfree = hguard
            hxxfree = hguard
            hmixfree = [hguard] * NCH

        if standalone:
            pr.finalize([pr.lastd("sp"), pr.lastd("pool"), pr.last("dve"), pr.last("pe"), pr.last("act")])
    return pr



def build_launch2(pr=None):
    TSEG = 1024
    NSEG = T // TSEG
    L = 64
    NCHK = TSEG // L
    standalone = pr is None
    if standalone:
        pr = Prog("l2")
    rT = pr.din("rT", [C, T])
    wT = pr.din("wT", [C, T])
    vT = pr.din("vT", [C, T])
    ffnT = pr.din("ffnT", [2 * C, T // 2], U32)
    Wo = pr.din("Wo", [3 * C, C])
    uvec = pr.din("uvec", [C, 1])
    lng = pr.din("lng", [C, 1])
    lnb = pr.din("lnb", [C, 1])
    cones = pr.din("cones", [128, 1])
    cmask = pr.din("cmask", [128, TSEG])
    ctri = pr.din("ctri", [128, 64])
    cb2 = pr.din("cb2", [128, 128])
    cones2 = pr.din("cones2", [128, 64])
    cid = pr.din("cid", [128, 128])
    o_s = pr.dout("sT", [C, T])
    o_oscr = pr.dout("o_scr", [C, T])
    o_dsb = pr.dout("o_dsb", [128, T])
    o_ruk = pr.dout("o_ruk", [128, 1024])
    o_us = pr.dout("o_us", [128, NCH])

    with (pr.es if standalone else pr.scope("l2_")):
        Wob = pr.sbuf("Wob", [128, 24, C], BF16)
        us = pr.sbuf("us", [128, NCH, 1], F32)
        lngs = pr.sbuf("lngs", [128, NCH, 1], F32)
        lnbs = pr.sbuf("lnbs", [128, NCH, 1], F32)
        ones_f = pr.sbuf("ones_f", [128, 1], F32)
        onesrow_f = pr.sbuf("onesrow_f", [1, 128], F32)
        ones64b = pr.sbuf("ones64b", [1, 64], BF16)
        maskseg = pr.sbuf("maskseg", [128, TSEG], F32)
        trib = pr.sbuf("trib", [128, 64], F32)
        b2b = pr.sbuf("b2b", [128, 128], BF16)
        ones2b = pr.sbuf("ones2b", [128, 64], BF16)
        idb = pr.sbuf("idb", [128, 128], BF16)

        wt = pr.sbuf("wt", [128, TSEG], F32)
        ew = pr.sbuf("ew", [128, TSEG], F32)
        cc = pr.sbuf("cc", [128, TSEG], F32)
        kk = pr.sbuf("kk", [128, TSEG], F32)
        tmpa = pr.sbuf("tmpa", [128, TSEG], F32)
        tmpb = pr.sbuf("tmpb", [128, TSEG], F32)
        rr = pr.sbuf("rr", [128, TSEG], F32)
        vv = pr.sbuf("vv", [128, TSEG], F32)
        rtb = pr.sbuf("rtb", [128, TSEG], BF16)
        kxb = pr.sbuf("kxb", [128, TSEG], BF16)
        rt2b = pr.sbuf("rt2b", [128, TSEG], BF16)
        kx2b = pr.sbuf("kx2b", [128, TSEG], BF16)
        tmpc = pr.sbuf("tmpc", [128, TSEG], F32)
        khb = pr.sbuf("khb", [128, TSEG], BF16)
        vb = pr.sbuf("vb", [128, TSEG], BF16)
        vtm = pr.sbuf("vtm", [128, 2, TSEG // 128, 64], BF16)
        khtm = pr.sbuf("khtm", [128, 2, TSEG // 128, 64], BF16)
        dec = pr.sbuf("dec", [128, NCHK], F32)
        amb = pr.sbuf("amb", [128, 2, 64], BF16)
        S2 = pr.sbuf("S2", [128, 64], F32)
        S2b = pr.sbuf("S2b", [128, 2, 64], BF16)
        dsb = pr.sbuf("dsb", [128, TSEG], F32)
        osb = pr.sbuf("osb", [128, TSEG], F32)
        ot = pr.sbuf("ot", [128, NCH, 512], F32)
        on = pr.sbuf("on", [128, NCH, 512], F32)
        onb = pr.sbuf("onb", [128, NCH, 512], BF16)
        ffb = pr.sbuf("ffb", [128, 16, 512], BF16)
        sqr = pr.sbuf("sqr", [128, 2, 512], F32)
        rowA = pr.sbuf("rowA", [1, 512], F32)
        rowB = pr.sbuf("rowB", [1, 512], F32)
        rowC = pr.sbuf("rowC", [1, 512], F32)
        evac = pr.sbuf("evac", [128, 4, 512], F32)

        ps = Banks([pr.psum(f"ps{i}", [128, 512]) for i in range(7)])
        psT = pr.psum("psT", [128, 64], BF16)

        hc = []
        hc.append(pr.dma("pool", Wob[:], Wo.rearrange("(a p) n -> p a n", p=128)))
        hc.append(pr.dma("pool", us[:], uvec.rearrange("(a p) n -> p a n", p=128), allow_slow_non_contiguous=True))
        hc.append(pr.dma("pool", lngs[:], lng.rearrange("(a p) n -> p a n", p=128), allow_slow_non_contiguous=True))
        hc.append(pr.dma("pool", lnbs[:], lnb.rearrange("(a p) n -> p a n", p=128), allow_slow_non_contiguous=True))
        hc.append(pr.dma("pool", ones_f[:], cones))
        hc.append(pr.dma("pool", maskseg[:], cmask))
        hc.append(pr.dma("pool", trib[:], ctri))
        hc.append(pr.dma("pool", b2b[:], cb2))
        hc.append(pr.dma("pool", ones2b[:], cones2))
        hc.append(pr.dma("pool", idb[:], cid))
        hcl = hc[-1]
        pr.op("dve", lambda v: v.memset(onesrow_f[:], 1.0), deps=[hcl], tick=False)
        hpre = pr.op("dve", lambda v: v.memset(ones64b[:], 1.0))
        pr.dma("sp", o_us, us[:].rearrange("p a n -> p (a n)"), deps=[hpre])

        for hp in range(NCH):
            r0 = 128 * hp
            hS = pr.op("dve", lambda v: v.memset(S2[:], 0.0), deps=[hpre])
            hosb_free = hpre
            ambfree = {0: hpre, 1: hpre}
            sbfree = {}
            for sg in range(NSEG):
                t0 = sg * TSEG
                segdeps = [pr.last("act"), pr.last("dve"), pr.last("pe")]
                hw_ = pr.dma("sp", wt[:], wT[r0:r0 + 128, t0:t0 + TSEG], deps=segdeps)
                hr_ = pr.dma("sp", rr[:], rT[r0:r0 + 128, t0:t0 + TSEG], deps=segdeps)
                hv_ = pr.dma("sp", vv[:], vT[r0:r0 + 128, t0:t0 + TSEG], deps=segdeps)
                hew = pr.op("act", lambda s: s.activation(ew[:], wt[:], AF.Exp, bias=0.0, scale=1.0), deps=[hw_])
                hcc = pr.op("dve", lambda v: v.tensor_tensor_scan(cc[:], maskseg[:], ew[:], 0.0,
                                                                  ALU.mult, ALU.subtract), deps=[hew])
                he2 = pr.op("act", lambda s: s.activation(tmpa[:], ew[:], AF.Exp, bias=0.0, scale=-1.0), deps=[hew])
                hkk = pr.op("dve", lambda v: v.tensor_scalar(kk[:], tmpa[:], -1.0, 1.0, ALU.mult, ALU.add),
                            deps=[he2])
                cc3 = cc[:].rearrange("p (c l) -> p c l", l=64)
                midv = bass.broadcast_tensor_aps(cc3, cc3[:, :, 31:32])[1]
                hta = pr.op("dve", lambda v: v.tensor_tensor(tmpa[:], cc[:], ew[:], ALU.add), deps=[hcc, he2])
                hmm1 = pr.op("dve", lambda v, midv=midv: v.tensor_tensor(
                    tmpc[:].rearrange("p (c l) -> p c l", l=64), tmpa[:].rearrange("p (c l) -> p c l", l=64),
                    midv, ALU.subtract), deps=[hta])
                hmm2 = pr.op("act", lambda s: s.activation(tmpc[:], tmpc[:], AF.Exp, bias=0.0, scale=1.0), deps=[hmm1])
                hrt2 = pr.op("dve", lambda v: v.tensor_tensor(rt2b[:], tmpc[:], rr[:], ALU.mult), deps=[hmm2, hr_])
                hta2 = pr.op("act", lambda s: s.activation(tmpa[:], tmpa[:], AF.Exp, bias=0.0, scale=1.0),
                             deps=[hta, hmm1])
                hrt = pr.op("dve", lambda v: v.tensor_tensor(rtb[:], tmpa[:], rr[:], ALU.mult), deps=[hta2, hr_])
                htb = pr.op("dve", lambda v: v.tensor_scalar(tmpb[:], cc[:], -1.0, 60.0, ALU.mult, ALU.min),
                            deps=[hcc, hta2])
                htb2 = pr.op("act", lambda s: s.activation(tmpb[:], tmpb[:], AF.Exp, bias=0.0, scale=1.0), deps=[htb])
                hkx = pr.op("dve", lambda v: v.tensor_tensor(kxb[:], tmpb[:], kk[:], ALU.mult), deps=[htb2, hkk])
                hkm1 = pr.op("dve", lambda v, midv=midv: v.tensor_tensor(
                    tmpc[:].rearrange("p (c l) -> p c l", l=64), midv, cc3, ALU.subtract), deps=[hmm2, hkx])
                pr.op("dve", lambda v: v.tensor_scalar(tmpc[:], tmpc[:], 60.0, None, ALU.min), tick=False)
                hkm2 = pr.op("act", lambda s: s.activation(tmpc[:], tmpc[:], AF.Exp, bias=0.0, scale=1.0), deps=[hkm1])
                hkx2 = pr.op("dve", lambda v: v.tensor_tensor(kx2b[:], tmpc[:], kk[:], ALU.mult), deps=[hkm2, hkk])
                cc3 = cc[:].rearrange("p (c l) -> p c l", l=64)
                cLv = bass.broadcast_tensor_aps(cc3, cc3[:, :, 63:64])[1]
                hkh1 = pr.op("dve", lambda v, cLv=cLv, cc3=cc3: v.tensor_tensor(
                    tmpb[:].rearrange("p (c l) -> p c l", l=64), cLv, cc3, ALU.subtract), deps=[hkx])
                hkh2 = pr.op("act", lambda s: s.activation(tmpb[:], tmpb[:], AF.Exp, bias=0.0, scale=1.0), deps=[hkh1])
                hkh = pr.op("dve", lambda v: v.tensor_tensor(khb[:], tmpb[:], kk[:], ALU.mult), deps=[hkh2])
                hdec = pr.op("act", lambda s, cc3=cc3: s.activation(
                    dec[:].rearrange("p (c o) -> p c o", o=1), cc3[:, :, 63:64], AF.Exp, bias=0.0, scale=1.0),
                    deps=[hcc, pr.last("dve")])
                hvb = pr.op("dve", lambda v: v.tensor_copy(vb[:], vv[:]), deps=[hv_])
                psTfree = psTfree if sg + hp else hpre
                for src_, dstt, hsrc in ((vb, vtm, hvb), (khb, khtm, hkh)):
                    for hh in range(2):
                        for tc in range(TSEG // 128):
                            hT = pr.op("pe", lambda t, src_=src_, hh=hh, tc=tc: t.transpose(
                                psT[:, :], src_[64 * hh:64 * (hh + 1), 128 * tc:128 * (tc + 1)],
                                idb[64 * hh:64 * (hh + 1), 64 * hh:64 * (hh + 1)]),
                                deps=[hsrc, psTfree])
                            hE = pr.op("dve", lambda v, dstt=dstt, hh=hh, tc=tc: v.tensor_copy(
                                dstt[:, hh, tc, :], psT[:, :]), deps=[hT])
                            psTfree = hE
                for j in range(NCHK):
                    c0 = j * L
                    jp = 64 * (j % 2)
                    for hh in range(2):
                        bO = hh
                        bA = 2 + hh
                        bS = 4 + hh
                        rh = 64 * hh
                        hSb = pr.op("dve", lambda v, rh=rh, j=j: v.tensor_copy(
                            S2b[rh:rh + 64, j % 2, :], S2[rh:rh + 64, :]),
                            deps=[hS, sbfree.get((j % 2, hh))])
                        hM1 = pr.op("pe", lambda t, rh=rh, j=j, c0=c0, bO=bO: t.matmul(
                            ps.t[bO][rh:rh + 64, 0:64], S2b[rh:rh + 64, j % 2, :], rtb[rh:rh + 64, c0:c0 + L],
                            start=True, stop=False), deps=[hSb, hrt, ps.free[bO]])
                        sbfree[(j % 2, hh)] = hM1
                        pr.op("pe", lambda t, rh=rh, c0=c0, bA=bA, jp=jp: t.matmul(
                            ps.t[bA][jp:jp + 64, 0:32], kxb[rh:rh + 64, c0:c0 + L],
                            rtb[rh:rh + 64, c0:c0 + 32], start=True, stop=True),
                            deps=[hkx, hrt, ps.free[bA]], tick=False)
                        hA = pr.op("pe", lambda t, rh=rh, c0=c0, bA=bA, jp=jp: t.matmul(
                            ps.t[bA][jp:jp + 64, 32:64], kx2b[rh:rh + 64, c0:c0 + L],
                            rt2b[rh:rh + 64, c0 + 32:c0 + 64], start=True, stop=True),
                            deps=[hkx2, hrt2])
                        ps.data[bA] = hA
                        hAm = pr.op("dve", lambda v, hh=hh, bA=bA, jp=jp: v.tensor_tensor(
                            amb[jp:jp + 64, hh, :], ps.t[bA][jp:jp + 64, 0:64], trib[jp:jp + 64, :], ALU.mult),
                            deps=[hA, ambfree[hh]])
                        ps.free[bA] = hAm
                        hO = pr.op("pe", lambda t, hh=hh, j=j, bO=bO, jp=jp, rh=rh: t.matmul(
                            ps.t[bO][rh:rh + 64, 0:64], vtm[jp:jp + 64, hh, j // 2, :],
                            amb[jp:jp + 64, hh, :], start=False, stop=True), deps=[hAm])
                        ps.data[bO] = hO
                        ambfree[hh] = hO
                        hSmm = pr.op("pe", lambda t, hh=hh, j=j, bS=bS, jp=jp, rh=rh: t.matmul(
                            ps.t[bS][rh:rh + 64, 0:64], khtm[jp:jp + 64, hh, j // 2, :],
                            vtm[jp:jp + 64, hh, j // 2, :], start=True, stop=True),
                            deps=[ps.free[bS]])
                        ps.data[bS] = hSmm
                        pr.op("dve", lambda v, rh=rh, j=j: v.tensor_scalar(
                            S2[rh:rh + 64, :], S2[rh:rh + 64, :], dec[rh:rh + 64, j:j + 1], None, ALU.mult),
                            deps=[hdec, hSb, hSmm], tick=False)
                        hS = pr.op("dve", lambda v, rh=rh, bS=bS: v.tensor_tensor(
                            S2[rh:rh + 64, :], S2[rh:rh + 64, :], ps.t[bS][rh:rh + 64, 0:64], ALU.add))
                        ps.free[bS] = hS
                        hEv = pr.op("dve", lambda v, rh=rh, c0=c0, bO=bO: v.tensor_copy(
                            osb[rh:rh + 64, c0:c0 + L], ps.t[bO][rh:rh + 64, 0:64]),
                            deps=[hO, hosb_free])
                        ps.free[bO] = hEv
                hd1 = pr.op("dve", lambda v, hp=hp: v.tensor_scalar(tmpa[:], rr[:], us[:, hp, 0:1], None, ALU.mult),
                            deps=[pr.last("act"), pr.last("pe")])
                hd2 = pr.op("dve", lambda v: v.tensor_tensor(tmpb[:], tmpa[:], kk[:], ALU.mult), deps=[hd1])
                hd3 = pr.op("dve", lambda v: v.tensor_copy(kxb[:], tmpb[:]), deps=[hd2])
                for half in range(2):
                    bd = 2 + half
                    hdm = pr.op("pe", lambda t, half=half, bd=bd: t.matmul(
                        ps.t[bd][:, 0:512], b2b[:], kxb[:, 512 * half:512 * (half + 1)],
                        start=True, stop=True), deps=[hd3, ps.free[bd]])
                    ps.data[bd] = hdm
                    pr.op("dve", lambda v, half=half, bd=bd: v.tensor_copy(
                        dsb[0:1, 512 * half:512 * (half + 1)], ps.t[bd][0:1, 0:512]), deps=[hdm], tick=False)
                    hdc = pr.op("dve", lambda v, half=half, bd=bd: v.tensor_copy(
                        dsb[64:65, 512 * half:512 * (half + 1)], ps.t[bd][64:65, 0:512]))
                    ps.free[bd] = hdc
                pr.op("dve", lambda v: v.tensor_copy(khb[0:1, :], dsb[0:1, :]), deps=[pr.last("pe")], tick=False)
                hdsb = pr.op("dve", lambda v: v.tensor_copy(khb[64:65, :], dsb[64:65, :]))
                for hh in range(2):
                    rh = 64 * hh
                    for half in range(2):
                        bb = 6
                        hB = pr.op("pe", lambda t, rh=rh, half=half, bb=bb: t.matmul(
                            ps.t[bb][rh:rh + 64, 0:512], ones2b[rh:rh + 1, :],
                            khb[rh:rh + 1, 512 * half:512 * (half + 1)],
                            start=True, stop=True), deps=[hdsb, ps.free[bb]])
                        ps.data[bb] = hB
                        pr.op("dve", lambda v, rh=rh, half=half, bb=bb: v.tensor_tensor(
                            tmpa[rh:rh + 64, 512 * half:512 * (half + 1)], ps.t[bb][rh:rh + 64, 0:512],
                            vv[rh:rh + 64, 512 * half:512 * (half + 1)], ALU.mult),
                            deps=[hB], tick=False)
                        hAdd = pr.op("dve", lambda v, rh=rh, half=half: v.tensor_tensor(
                            osb[rh:rh + 64, 512 * half:512 * (half + 1)],
                            osb[rh:rh + 64, 512 * half:512 * (half + 1)],
                            tmpa[rh:rh + 64, 512 * half:512 * (half + 1)], ALU.add))
                        ps.free[bb] = hAdd
                hosb_free = pr.dma("sp", o_oscr[r0:r0 + 128, t0:t0 + TSEG], osb[:], deps=[pr.last("dve")])
                if hp == 0:
                    pr.dma("sp", o_dsb[0:1, t0:t0 + TSEG], dsb[0:1, :], deps=[pr.last("dve")])
                    pr.dma("sp", o_dsb[64:65, t0:t0 + TSEG], dsb[64:65, :], deps=[pr.last("dve")])
                    if sg == 0:
                        pr.dma("sp", o_ruk[:, 0:TSEG], tmpb[:], deps=[hd3])

        hwoph = pr.tick("dve", deps=[pr.lastd("sp")])
        pr.stfree = {}
        for k in range(8):
            t0 = k * 512
            hod = pr.dma("sp", ot[:], o_oscr[:, t0:t0 + 512].rearrange("(a p) w -> p a w", p=128),
                         deps=[hwoph, pr.last("dve"), pr.last("pe"), pr.last("act")])
            hln = emit_layernorm(pr, 512, ot, on, sqr, (rowA, rowB, rowC), ones_f, onesrow_f,
                                 (0, 1), (2, 3), ps, hod, gamma=lngs, beta=lnbs)
            for a in range(NCH):
                pr.op("dve", lambda v, a=a: v.tensor_copy(onb[:, a, :], on[:, a, :]),
                      tick=False)
            hcb = pr.tick("dve")
            hfb = pr.dma("sp", ffb[:], ffnT[:, t0 // 2:(t0 + 512) // 2].bitcast(BF16)
                         .rearrange("(a p) w -> p a w", p=128), deps=[pr.last("pe")])
            for m in range(NCH):
                bk = 4 + m % 2
                for a in range(24):
                    srcb = onb[:, a, :] if a < 8 else ffb[:, a - 8, :]
                    pr.op("pe", lambda t, m=m, a=a, bk=bk, srcb=srcb: t.matmul(
                        ps.t[bk][:, 0:512], Wob[:, a, 128 * m:128 * (m + 1)], srcb,
                        start=(a == 0), stop=(a == 23)),
                        deps=[hcb, hfb, ps.free[bk]] if a == 0 else (), tick=(a == 23))
                hpe = pr.last("pe")
                ps.data[bk] = hpe
                sl = m % 4
                hev = pr.op("dve", lambda v, bk=bk, sl=sl: v.tensor_copy(evac[:, sl, :], ps.t[bk][:, 0:512]),
                            deps=[hpe, pr.stfree.get(("e", sl))])
                ps.free[bk] = hev
                hst = pr.dma("sp", o_s[128 * m:128 * (m + 1), t0:t0 + 512], evac[:, sl, :], deps=[hev])
                pr.stfree[("e", sl)] = hst

        if standalone:
            pr.finalize([pr.lastd("sp"), pr.lastd("pool"), pr.last("dve"), pr.last("pe"), pr.last("act")])
    return pr




def build_launch3(phases=9, nexp=8, noacc=False, niq=None, pr=None):
    standalone = pr is None
    if standalone:
        pr = Prog("l3")
    xT = pr.din("xT", [C, T])
    sT = pr.din("sT", [C, T])
    xnT = pr.din("xnT", [C, T])
    rg = pr.din("rg", [C, 8])
    gw = pr.din("gw", [8, C, 4 * C])
    uw = pr.din("uw", [8, C, 4 * C])
    dw = pr.din("dw", [8, 4 * C, C])
    cones = pr.din("cones", [128, 1])
    cid = pr.din("cid", [128, 128])
    csel = pr.din("csel", [8, 8, 128])
    o_x1 = pr.dout("x1T", [C, T])
    o_acc = pr.dout("accT", [C, T])
    o_rdbg = pr.dout("rdbg", [128, 32, 36])

    NTT = 8
    NIQ = 8
    NIQ = niq if niq else NIQ
    ICQ = 4

    with (pr.es if standalone else pr.scope("l3_")):
        xnb = pr.sbuf("xnb", [128, NCH, T // 2], BF16)
        acc = pr.sbuf("acc", [128, NCH, T // 2], F32)
        routb = pr.sbuf("routb", [128, NCH, 8], F32)
        xnstrip = pr.sbuf("xnstrip", [128, 2, NCH, 128], F32)
        idb = pr.sbuf("idb", [128, 128], F32)
        ones_f = pr.sbuf("ones_f", [128, 1], F32)
        Gt = pr.sbuf("Gt", [128, 32, 8], F32)
        GTsb = pr.sbuf("GTsb", [8, T], F32)
        gB = pr.sbuf("gB", [128, T // 2], F32)
        selb = pr.sbuf("selb", [8, 8, 128], F32)
        probs = pr.sbuf("probs", [128, 8], F32)
        m8 = pr.sbuf("m8", [128, 8], F32)
        negmax = pr.sbuf("negmax", [128, 1], F32)
        sumexp = pr.sbuf("sumexp", [128, 1], F32)
        w1c = pr.sbuf("w1c", [128, 1], F32)
        w2c = pr.sbuf("w2c", [128, 1], F32)
        t2m = pr.sbuf("t2m", [128, 1], F32)
        scr = pr.sbuf("scr", [128, 8], F32)
        scr2 = pr.sbuf("scr2", [128, 8], F32)
        gsl = pr.sbuf("gsl", [128, ICQ, NCH, 128], BF16)
        usl = pr.sbuf("usl", [128, ICQ, NCH, 128], BF16)
        dsl = pr.sbuf("dsl", [128, ICQ, C], BF16)
        hbq = pr.sbuf("hbq", [128, ICQ, 4, 512], BF16)
        sil = pr.sbuf("sil", [128, 512], F32)
        stage = pr.sbuf("stage", [128, NTT, 512], F32)
        xa = pr.sbuf("xa", [128, 1, 512], F32)
        xb = pr.sbuf("xb", [128, 1, 512], F32)
        xc = pr.sbuf("xc", [128, 1, 512], F32)
        xg = pr.sbuf("xg", [128, 1, 512], F32)

        ps = Banks([pr.psum(f"ps{i}", [128, 512]) for i in range(8)])

        hx = pr.dma("pool", routb[:], rg.rearrange("(a p) n -> p a n", p=128))
        hid = pr.dma("pool", idb[:], cid)
        ho = pr.dma("pool", ones_f[:], cones)
        hsel = pr.dma("pool", selb[:], csel)

        hstrip_free = {}
        for tt in range(32):
            c0 = tt * 128
            slt = tt % 2
            hsl_ = pr.dma("sp", xnstrip[:, slt, :, :],
                          xnT[:, c0:c0 + 128].rearrange("(a p) w -> p a w", p=128),
                          deps=[hstrip_free.get(slt)])
            for a in range(NCH):
                hl = pr.op("pe", lambda t, slt=slt, a=a: t.matmul(
                    ps.t[0][:, 0:8], xnstrip[:, slt, a, :], routb[:, a, :],
                    start=(a == 0), stop=(a == NCH - 1)),
                    deps=[hsl_, hx, ps.free[0]] if a == 0 else (), tick=(a == NCH - 1))
            hstrip_free[slt] = hl
            ps.data[0] = hl
            hlg = pr.op("dve", lambda v: v.tensor_copy(scr2[:], ps.t[0][:, 0:8]),
                        deps=[hl, pr.stfree3.get("dbg") if hasattr(pr, "stfree3") else None])
            pr.dma("sp", o_rdbg[:, tt, 16:24], scr2[:], deps=[hlg, pr.stfree3.get("s2") if hasattr(pr, "stfree3") else None])
            pr.stfree3 = getattr(pr, "stfree3", {})
            pr.stfree3["s2"] = h3 if False else None
            h1 = pr.op("dve", lambda v: v.max(scr[:], scr2[:]), deps=[hlg])
            h1 = pr.op("dve", lambda v: v.tensor_copy(negmax[:], scr[:, 0:1]), deps=[h1])
            ps.free[0] = h1
            h2 = pr.op("dve", lambda v: v.tensor_scalar(negmax[:], negmax[:], -1.0, None, ALU.mult), deps=[h1])
            h3 = pr.op("act", lambda s: s.activation(probs[:], scr2[:], AF.Exp, bias=negmax[:], scale=1.0),
                       deps=[h2])
            hdx = pr.dma("sp", o_rdbg[:, tt, 24:32], probs[:], deps=[h3])
            h3 = pr.op("dve", lambda v: v.tensor_reduce(sumexp[:], probs[:], mybir.AxisListType.X, ALU.add),
                       deps=[h3])
            hds = pr.dma("sp", o_rdbg[:, tt, 33:34], sumexp[:], deps=[h3], allow_slow_non_contiguous=True)
            h4 = pr.op("dve", lambda v: v.reciprocal(sumexp[:], sumexp[:]), deps=[h3, hds])
            h4 = pr.op("dve", lambda v: v.tensor_scalar(probs[:], probs[:], sumexp[:], None, ALU.mult),
                       deps=[h4, hdx]) if False else h4
            h5 = pr.op("dve", lambda v: v.tensor_scalar(probs[:], probs[:], sumexp[:], None, ALU.mult), deps=[h4])
            h6 = pr.op("dve", lambda v: v.max(m8[:], probs[:]), deps=[h5])
            pr.op("dve", lambda v: v.tensor_tensor(w1c[:], m8[:, 0:1], m8[:, 1:2], ALU.add), deps=[h6])
            pr.op("dve", lambda v: v.tensor_scalar(w1c[:], w1c[:], 1e-9, None, ALU.add))
            h7 = pr.op("dve", lambda v: v.reciprocal(w1c[:], w1c[:]))
            pr.op("dve", lambda v: v.tensor_tensor(w2c[:], m8[:, 1:2], w1c[:], ALU.mult))
            pr.op("dve", lambda v: v.tensor_tensor(w1c[:], m8[:, 0:1], w1c[:], ALU.mult))
            pr.op("dve", lambda v: v.tensor_tensor(t2m[:], m8[:, 1:2], m8[:, 0:1], ALU.is_lt))
            h8 = pr.op("dve", lambda v: v.tensor_tensor(w2c[:], w2c[:], t2m[:], ALU.mult))
            bm1 = bass.broadcast_tensor_aps(probs[:], m8[:, 0:1])[1]
            bm2 = bass.broadcast_tensor_aps(probs[:], m8[:, 1:2])[1]
            bw1 = bass.broadcast_tensor_aps(probs[:], w1c[:])[1]
            bw2 = bass.broadcast_tensor_aps(probs[:], w2c[:])[1]
            pr.op("dve", lambda v, tt=tt, bm1=bm1, bw1=bw1: v.tensor_tensor(scr[:], probs[:], bm1, ALU.is_equal),
                  deps=[h8])
            pr.op("dve", lambda v, tt=tt, bw1=bw1: v.tensor_tensor(Gt[:, tt, :], scr[:], bw1, ALU.mult))
            pr.op("dve", lambda v, bm2=bm2: v.tensor_tensor(scr[:], probs[:], bm2, ALU.is_equal))
            pr.op("dve", lambda v, bw2=bw2: v.tensor_tensor(scr[:], scr[:], bw2, ALU.mult))
            h9 = pr.op("dve", lambda v, tt=tt: v.tensor_tensor(Gt[:, tt, :], Gt[:, tt, :], scr[:], ALU.add))
            pr.dma("sp", o_rdbg[:, tt, 0:8], probs[:], deps=[h9])
            hdbg = pr.dma("sp", o_rdbg[:, tt, 8:16], m8[:], deps=[h9])
            pr.stfree3 = getattr(pr, "stfree3", {})
            pr.stfree3["dbg"] = hdbg
            hT = pr.op("pe", lambda t, tt=tt: t.transpose(ps.t[0][0:8, 0:128], Gt[:, tt, :], idb[:]),
                       deps=[h9, ps.free[0], hid])
            ps.data[0] = hT
            hG = pr.op("dve", lambda v, tt=tt: v.tensor_copy(GTsb[:, 128 * tt:128 * (tt + 1)], ps.t[0][0:8, 0:128]),
                       deps=[hT])
            ps.free[0] = hG
        hrouter = pr.last("dve")
        pr.stfree3 = getattr(pr, "stfree3", {})
        pr.dma("sp", o_acc[0:8, 0:T], GTsb[:], deps=[hrouter])
        if phases < 2:
            if standalone:
                pr.finalize([pr.lastd("sp"), pr.lastd("pool"), pr.last("dve"), pr.last("pe"), pr.last("act")])
            return pr

        wsl = {}
        for th in range(2):
            toff = th * (T // 2)
            NTT2 = (T // 2) // 512
            hxnl2 = []
            for a in range(NCH):
                hxnl2.append(pr.dma("pool", xnb[:, a, 0:T // 2], xnT[128 * a:128 * (a + 1), toff:toff + T // 2],
                                    deps=[pr.last("pe"), pr.last("dve")] if a == 0 else ()))
            for a in range(NCH):
                pr.op("dve", lambda v, a=a: v.memset(acc[:, a, :], 0.0),
                      deps=[hxnl2[-1], pr.lastd("sp")], tick=(a == NCH - 1))
            hacc0 = pr.last("dve")
            for e in range(nexp):
                hgbev = []
                for q in range(NTT2):
                    hq = pr.op("pe", lambda t, e=e, q=q, toff=toff: t.matmul(
                        ps.t[7][:, 0:512], selb[:, e, :], GTsb[:, toff + 512 * q:toff + 512 * (q + 1)],
                        start=True, stop=True), deps=[hrouter, hsel, ps.free[7]])
                    ps.data[7] = hq
                    hev = pr.op("dve", lambda v, q=q: v.tensor_copy(gB[:, 512 * q:512 * (q + 1)], ps.t[7][:, 0:512]),
                                deps=[hq, pr.stfree3.get("gBfree")])
                    ps.free[7] = hev
                    hgbev.append(hev)
                hgb = hgbev[-1]
                for iq in range(NIQ):
                    hws = []
                    for ic in range(ICQ):
                        i0_ = (iq * ICQ + ic) * 128
                        wdep = wsl.get(ic)
                        hg_ = pr.dma("pool", gsl[:, ic, :, :], gw[e, :, i0_:i0_ + 128]
                                     .rearrange("(a p) n -> p a n", p=128),
                                     deps=[wdep] if wdep else ())
                        hu_ = pr.dma("pool", usl[:, ic, :, :], uw[e, :, i0_:i0_ + 128]
                                     .rearrange("(a p) n -> p a n", p=128), deps=())
                        hd_ = pr.dma("pool", dsl[:, ic, :], dw[e, i0_:i0_ + 128, :], deps=())
                        hws.append((hg_, hu_, hd_))
                    silfree = pr.last("dve")
                    for ic in range(ICQ):
                        hg_, hu_, hd_ = hws[ic]
                        for tt in range(NTT2):
                            bg = 1 + (tt % 2)
                            bu = 3 + (tt % 2)
                            for a in range(NCH):
                                hgm = pr.op("pe", lambda t, ic=ic, tt=tt, a=a, bg=bg: t.matmul(
                                    ps.t[bg][:, 0:512], gsl[:, ic, a, :], xnb[:, a, 512 * tt:512 * (tt + 1)],
                                    start=(a == 0), stop=(a == NCH - 1)),
                                    deps=[hg_, hacc0, ps.free[bg]] if a == 0 else (), tick=(a == NCH - 1))
                            ps.data[bg] = hgm
                            for a in range(NCH):
                                hum = pr.op("pe", lambda t, ic=ic, tt=tt, a=a, bu=bu: t.matmul(
                                    ps.t[bu][:, 0:512], usl[:, ic, a, :], xnb[:, a, 512 * tt:512 * (tt + 1)],
                                    start=(a == 0), stop=(a == NCH - 1)),
                                    deps=[hu_, ps.free[bu]] if a == 0 else (), tick=(a == NCH - 1))
                            ps.data[bu] = hum
                            hsil = pr.op("act", lambda s, bg=bg: s.activation(sil[:], ps.t[bg][:, 0:512], AF.Silu,
                                                                              bias=0.0, scale=1.0),
                                         deps=[hgm, silfree], selfsync=False)
                            ps.free[bg] = hsil
                            hhb = pr.op("dve", lambda v, ic=ic, tt=tt, bu=bu: v.tensor_tensor(
                                hbq[:, ic, tt, :], ps.t[bu][:, 0:512], sil[:], ALU.mult), deps=[hum, hsil])
                            ps.free[bu] = hhb
                            silfree = hhb
                        wsl[ic] = pr.last("pe")
                    hhb_all = pr.last("dve")
                    for co in range(NCH):
                        for tt in range(NTT2):
                            bk = 5 + (tt % 2)
                            for ic in range(ICQ):
                                hym = pr.op("pe", lambda t, co=co, tt=tt, ic=ic, bk=bk: t.matmul(
                                    ps.t[bk][:, 0:512], dsl[:, ic, 128 * co:128 * (co + 1)], hbq[:, ic, tt, :],
                                    start=(ic == 0), stop=(ic == ICQ - 1)),
                                    deps=([hhb_all, hws[ic][2], ps.free[bk]] if ic == 0 else [hws[ic][2]]),
                                    tick=(ic == ICQ - 1))
                            ps.data[bk] = hym
                            pr.op("dve", lambda v, tt=tt, bk=bk: v.tensor_tensor(
                                stage[:, 0, :], ps.t[bk][:, 0:512], gB[:, 512 * tt:512 * (tt + 1)], ALU.mult),
                                deps=[hym, hgb], tick=False)
                            hst = pr.op("dve", lambda v, co=co, tt=tt: v.tensor_tensor(
                                acc[:, co, 512 * tt:512 * (tt + 1)], acc[:, co, 512 * tt:512 * (tt + 1)],
                                stage[:, 0, :], ALU.add))
                            ps.free[bk] = hst
                    for ic2 in range(ICQ):
                        wsl[ic2] = pr.last("pe")
                pr.stfree3["gBfree"] = pr.last("dve")
            hacc_all = pr.last("dve")
            pr.dma("sp", o_acc[:, toff:toff + T // 2].rearrange("(a p) w -> p a w", p=128), acc[:],
                   deps=[hacc_all])
            for a in range(NCH):
                for tt in range(NTT2):
                    gd = pr.stfree3.get("cmb")
                    ha_ = pr.dma("sp", xb[:, 0, :], xT[128 * a:128 * (a + 1), toff + 512 * tt:toff + 512 * (tt + 1)],
                                 deps=[gd])
                    hc_ = pr.dma("sp", xc[:, 0, :], sT[128 * a:128 * (a + 1), toff + 512 * tt:toff + 512 * (tt + 1)],
                                 deps=[gd])
                    hsl2 = pr.op("act", lambda s, a=a, tt=tt: s.activation(
                        xg[:, 0, :], acc[:, a, 512 * tt:512 * (tt + 1)], AF.Silu, bias=0.0, scale=1.0),
                        deps=[hacc_all, gd], selfsync=False)
                    pr.op("dve", lambda v: v.tensor_tensor(xg[:, 0, :], xg[:, 0, :], xc[:, 0, :], ALU.mult),
                          deps=[hsl2, hc_], tick=False)
                    hx1 = pr.op("dve", lambda v: v.tensor_tensor(xg[:, 0, :], xg[:, 0, :], xb[:, 0, :], ALU.add),
                                deps=[ha_])
                    hstx = pr.dma("sp", o_x1[128 * a:128 * (a + 1), toff + 512 * tt:toff + 512 * (tt + 1)],
                                  xg[:, 0, :], deps=[hx1])
                    pr.stfree3["cmb"] = hstx

        if standalone:
            pr.finalize([pr.lastd("sp"), pr.lastd("pool"), pr.last("dve"), pr.last("pe"), pr.last("act")])
    return pr



def build_launch4(pr=None):
    standalone = pr is None
    if standalone:
        pr = Prog("l4")
    x1T = pr.din("x1T", [C, T])
    WqT = pr.din("WqT", [C, 3 * C])
    qkvb = pr.din("qkvb", [3 * C, 1])
    cones = pr.din("cones", [128, 1])
    cid = pr.din("cid", [128, 128])
    o_xn1 = pr.dout("xn1T", [C, T])
    o_q = pr.dout("qTb", [C, T // 2], U32)
    o_ktm = pr.dout("Ktm", [H, T, DH // 2], U32)
    o_vtm = pr.dout("Vtm", [H, T, DH // 2], U32)
    o_qtm = pr.dout("Qtm", [H, T, DH // 2], U32)
    o_vt = pr.dout("v1T", [C, T])

    TW2 = 512
    with (pr.es if standalone else pr.scope("l4_")):
        Wqb = pr.sbuf("Wqb", [128, NCH, 3 * C], BF16)
        qbs = pr.sbuf("qbs", [128, 24, 1], F32)
        ones_f = pr.sbuf("ones_f", [128, 1], F32)
        onesrow_f = pr.sbuf("onesrow_f", [1, 128], F32)
        idb = pr.sbuf("idb", [128, 128], BF16)
        xt = pr.sbuf("xt", [128, NCH, TW2], F32)
        xn = pr.sbuf("xn", [128, NCH, TW2], F32)
        xnb = pr.sbuf("xnb", [128, NCH, TW2], BF16)
        sqr = pr.sbuf("sqr", [128, 2, TW2], F32)
        rowA = pr.sbuf("rowA", [1, TW2], F32)
        rowB = pr.sbuf("rowB", [1, TW2], F32)
        rowC = pr.sbuf("rowC", [1, TW2], F32)
        qevb = pr.sbuf("qevb", [128, 2, TW2], BF16)
        kvt = pr.sbuf("kvt", [128, 3, NCH, TW2], BF16)
        vev = pr.sbuf("vev", [128, 2, TW2], F32)
        tev = pr.sbuf("tev", [128, 2, 64], BF16)
        ps = Banks([pr.psum(f"ps{i}", [128, 512]) for i in range(7)])
        psT = pr.psum("psT", [128, 64], BF16)

        hw = pr.dma("pool", Wqb[:], WqT.rearrange("(a p) n -> p a n", p=128))
        hb = pr.dma("pool", qbs[:], qkvb.rearrange("(a p) n -> p a n", p=128), allow_slow_non_contiguous=True)
        ho = pr.dma("pool", ones_f[:], cones)
        hi = pr.dma("pool", idb[:], cid)
        h0 = pr.op("dve", lambda v: v.memset(onesrow_f[:], 1.0), deps=[hi])
        pr.stf = {}

        for k in range(8):
            t0 = k * TW2
            hx = pr.dma("sp", xt[:], x1T[:, t0:t0 + TW2].rearrange("(a p) w -> p a w", p=128),
                        deps=[pr.stf.get("xn"), h0])
            hln = emit_layernorm(pr, TW2, xt, xn, sqr, (rowA, rowB, rowC), ones_f, onesrow_f,
                                 (0, 1), (2, 3), ps, hx)
            hst = pr.dma("sp", o_xn1[:, t0:t0 + TW2].rearrange("(a p) w -> p a w", p=128), xn[:], deps=[hln])
            pr.stf["xn"] = hst
            for a in range(NCH):
                pr.op("dve", lambda v, a=a: v.tensor_copy(xnb[:, a, :], xn[:, a, :]), tick=(a == NCH - 1))
            hcb = pr.last("dve")
            for m in range(24):
                bk = 4 + m % 2
                for a in range(NCH):
                    hmm = pr.op("pe", lambda t, m=m, a=a, bk=bk: t.matmul(
                        ps.t[bk][:, 0:TW2], Wqb[:, a, 128 * m:128 * (m + 1)], xnb[:, a, :],
                        start=(a == 0), stop=(a == NCH - 1)),
                        deps=[hcb, hw, ps.free[bk]] if a == 0 else (), tick=(a == NCH - 1))
                ps.data[bk] = hmm
                if m < 8:
                    sl = m % 2
                    hev = pr.op("dve", lambda v, m=m, bk=bk, sl=sl: v.tensor_scalar(
                        qevb[:, sl, :], ps.t[bk][:, 0:TW2], qbs[:, m, 0:1], None, ALU.add),
                        deps=[hmm, hb, pr.stf.get(("q", sl))])
                    hev2 = pr.op("dve", lambda v, m=m, sl=sl: v.tensor_copy(kvt[:, 2, m, :], qevb[:, sl, :]),
                                 deps=[pr.stf.get(("kv", 2, m))])
                    ps.free[bk] = hev2
                    pr.stf[("kvh", 2, m)] = hev2
                    hq = pr.dma("sp", o_q[128 * m:128 * (m + 1), t0 // 2:(t0 + TW2) // 2],
                                qevb[:, sl, :].bitcast(U32), deps=[hev])
                    pr.stf[("q", sl)] = hq
                elif m < 16:
                    a2 = m - 8
                    hev = pr.op("dve", lambda v, m=m, bk=bk, a2=a2: v.tensor_scalar(
                        kvt[:, 0, a2, :], ps.t[bk][:, 0:TW2], qbs[:, m, 0:1], None, ALU.add),
                        deps=[hmm, pr.stf.get(("kv", 0, a2))])
                    ps.free[bk] = hev
                    pr.stf[("kvh", 0, a2)] = hev
                else:
                    a2 = m - 16
                    hev = pr.op("dve", lambda v, m=m, bk=bk, a2=a2: v.tensor_scalar(
                        kvt[:, 1, a2, :], ps.t[bk][:, 0:TW2], qbs[:, m, 0:1], None, ALU.add),
                        deps=[hmm, pr.stf.get(("kv", 1, a2))])
                    hv2 = pr.op("dve", lambda v, m=m, bk=bk, a2=a2: v.tensor_scalar(
                        vev[:, a2 % 2, :], ps.t[bk][:, 0:TW2], qbs[:, m, 0:1], None, ALU.add),
                        deps=[pr.stf.get(("vf", a2 % 2))])
                    ps.free[bk] = hv2
                    pr.stf[("kvh", 1, a2)] = hev
                    hvs = pr.dma("sp", o_vt[128 * a2:128 * (a2 + 1), t0:t0 + TW2], vev[:, a2 % 2, :], deps=[hv2])
                    pr.stf[("vf", a2 % 2)] = hvs
            for kv in range(3):
                dst = (o_ktm, o_vtm, o_qtm)[kv]
                for h in range(H):
                    a2 = h // 2
                    rb = 64 * (h % 2)
                    for tc in range(TW2 // 128):
                        hT = pr.op("pe", lambda t, kv=kv, a2=a2, rb=rb, tc=tc: t.transpose(
                            psT[:, :], kvt[rb:rb + 64, kv, a2, 128 * tc:128 * (tc + 1)],
                            idb[rb:rb + 64, rb:rb + 64]),
                            deps=[pr.stf.get(("kvh", kv, a2)), pr.stf.get("psT")])
                        hE = pr.op("dve", lambda v, sl=tc % 2: v.tensor_copy(tev[:, sl, :], psT[:, :]),
                                   deps=[hT, pr.stf.get(("tev", tc % 2))])
                        pr.stf["psT"] = hE
                        hD = pr.dma("sp", dst[h, t0 + 128 * tc:t0 + 128 * (tc + 1), :],
                                    tev[:, tc % 2, :].bitcast(U32), deps=[hE])
                        pr.stf[("tev", tc % 2)] = hD
            for kv in range(3):
                for a2 in range(NCH):
                    pr.stf[("kv", kv, a2)] = pr.last("pe")

        if standalone:
            pr.finalize([pr.lastd("sp"), pr.lastd("pool"), pr.last("dve"), pr.last("pe"), pr.last("act")])
    return pr



def build_launch6(nolora=False, pr=None):
    standalone = pr is None
    if standalone:
        pr = Prog("l6")
    x1T = pr.din("x1T", [C, T])
    xn1T = pr.din("xn1T", [C, T])
    ygT = pr.din("ygT", [C, T])
    vT = pr.din("v1T", [C, T])
    dwc = pr.din("dwc", [C, 4])
    PwT = pr.din("PwT", [C, C])
    pb = pr.din("pb", [C, 1])
    Fk = pr.din("Fk", [C, C])
    fb = pr.din("fb", [C, 1])
    fa = pr.din("fa", [C, 16])
    fbb = pr.din("fbb", [128, C])
    cones = pr.din("cones", [128, 1])
    o_out = pr.dout("outT", [C, T], final=True)
    o_qd = pr.dout("qdbg", [128, NCH, C])
    o_s1 = pr.dout("s1T", [C, T])
    o_f = pr.dout("fT", [C, T])
    o_yl = pr.dout("ylT", [C, T])

    TW2 = 512
    with (pr.es if standalone else pr.scope("l6_")):
        PwTb = pr.sbuf("PwTb", [128, NCH, C], BF16)
        dwcs = pr.sbuf("dwcs", [128, NCH, 4], F32)
        pbs = pr.sbuf("pbs", [128, NCH, 1], F32)
        fbs = pr.sbuf("fbs", [128, NCH, 1], F32)
        fab = pr.sbuf("fab", [128, NCH, 16], BF16)
        fbbb = pr.sbuf("fbbb", [128, C], BF16)
        Fkf = pr.sbuf("Fkf", [128, NCH, C], F32)
        Fkq = pr.sbuf("Fkq", [128, NCH, C], BF16)
        amax = pr.sbuf("amax", [128, 1], F32)
        arow = pr.sbuf("arow", [1, 128], F32)
        ascl = pr.sbuf("ascl", [128, 1], F32)
        sscl = pr.sbuf("sscl", [128, 1], F32)
        ones_f = pr.sbuf("ones_f", [128, 1], F32)
        onesrow_f = pr.sbuf("onesrow_f", [1, 128], F32)
        idb32 = pr.sbuf("idb32", [128, 128], F32)
        vt = pr.sbuf("vt", [128, NCH, TW2 + 2], F32)
        yg = pr.sbuf("yg", [128, NCH, TW2], F32)
        ygb = pr.sbuf("ygb", [128, NCH, TW2], BF16)
        xnt = pr.sbuf("xnt", [128, NCH, TW2], F32)
        xnb = pr.sbuf("xnb", [128, NCH, TW2], BF16)
        x1t = pr.sbuf("x1t", [128, 2, TW2], F32)
        sev = pr.sbuf("sev", [128, 2, TW2], F32)
        fev = pr.sbuf("fev", [128, 2, TW2], F32)
        lorab = pr.sbuf("lorab", [128, TW2], BF16)
        ps = Banks([pr.psum(f"ps{i}", [128, 512]) for i in range(8)])

        hc1 = pr.dma("pool", PwTb[:], PwT.rearrange("(a p) n -> p a n", p=128))
        hc2 = pr.dma("pool", dwcs[:], dwc.rearrange("(a p) n -> p a n", p=128))
        hc3 = pr.dma("pool", pbs[:], pb.rearrange("(a p) n -> p a n", p=128), allow_slow_non_contiguous=True)
        hc4 = pr.dma("pool", fbs[:], fb.rearrange("(a p) n -> p a n", p=128), allow_slow_non_contiguous=True)
        hc5 = pr.dma("pool", fab[:], fa.rearrange("(a p) n -> p a n", p=128))
        hc6 = pr.dma("pool", fbbb[:], fbb)
        hc7 = pr.dma("sp", Fkf[:], Fk.rearrange("(a p) n -> p a n", p=128))
        hc8 = pr.dma("pool", ones_f[:], cones)
        h0 = pr.op("dve", lambda v: v.memset(onesrow_f[:], 1.0), deps=[hc8])
        pr.op("dve", lambda v: v.memset(lorab[:], 0.0), tick=False)

        hq1 = pr.op("dve", lambda v: v.tensor_reduce(amax[:], Fkf[:].rearrange("p a n -> p (a n)"),
                                                     mybir.AxisListType.X, ALU.max, apply_absolute_value=True),
                    deps=[hc7])
        hi32 = pr.dma("pool", idb32[:], pr.din("cid", [128, 128]))
        hq2 = pr.op("pe", lambda t: t.transpose(ps.t[0][0:1, 0:128], amax[:], idb32[:]),
                    deps=[hq1, hi32])
        hq3 = pr.op("dve", lambda v: v.tensor_copy(arow[:], ps.t[0][0:1, 0:128]), deps=[hq2])
        ps.free[0] = hq3
        w = 128
        hqq = hq3
        while w > 1:
            w //= 2
            hqq = pr.op("dve", lambda v, w=w: v.tensor_tensor(arow[:, 0:w], arow[:, 0:w], arow[:, w:2 * w],
                                                              ALU.max), deps=[hqq])
        hq4 = pr.op("pe", lambda t: t.matmul(ps.t[0][:, 0:2], onesrow_f[:], arow[:, 0:2], start=True, stop=True),
                    deps=[hqq, h0, ps.free[0]])
        ps.data[0] = hq4
        hq5 = pr.op("dve", lambda v: v.tensor_copy(ascl[:], ps.t[0][:, 0:1]), deps=[hq4])
        ps.free[0] = hq5
        hq6 = pr.op("dve", lambda v: v.tensor_scalar(sscl[:], ascl[:], 1.0 / 7.0, None, ALU.mult), deps=[hq5])
        hq7 = pr.op("dve", lambda v: v.reciprocal(ascl[:], ascl[:]), deps=[hq6])
        hq8 = pr.op("dve", lambda v: v.tensor_scalar(ascl[:], ascl[:], 7.0, None, ALU.mult), deps=[hq7])
        hq9 = hq8
        for a in range(NCH):
            pr.op("dve", lambda v, a=a: v.tensor_scalar(Fkf[:, a, :], Fkf[:, a, :], ascl[:], None, ALU.mult),
                  deps=[hq9])
            pr.op("dve", lambda v, a=a: v.tensor_scalar(Fkf[:, a, :], Fkf[:, a, :], MAGIC, MAGIC,
                                                        ALU.add, ALU.subtract))
            pr.op("dve", lambda v, a=a: v.tensor_scalar(Fkf[:, a, :], Fkf[:, a, :], -8.0, 7.0,
                                                        ALU.max, ALU.min))
            hq9 = pr.op("dve", lambda v, a=a: v.tensor_copy(Fkq[:, a, :], Fkf[:, a, :]))
        hqdone = hq9
        pr.dma("sp", o_qd[:, 0, 0:1], ascl[:], deps=[hqdone], allow_slow_non_contiguous=True)
        pr.dma("sp", o_qd[:, 1, 0:1], sscl[:], deps=[hqdone], allow_slow_non_contiguous=True)
        for a in range(NCH):
            if a >= 2:
                pr.dma("pool", o_qd[:, a, :], Fkq[:, a, :], deps=[hqdone])
        pr.stf = {}
        for k in range(8):
            t0 = k * TW2
            gv = pr.stf.get("vt")
            lo = max(t0 - 1, 0)
            hi_ = min(t0 + TW2 + 1, T)
            hv = pr.dma("sp", vt[:, :, (1 if t0 == 0 else 0):(1 if t0 == 0 else 0) + (hi_ - lo)]
                        .rearrange("p a w -> p a w"),
                        vT[:, lo:hi_].rearrange("(a p) w -> p a w", p=128), deps=[gv])
            hy = pr.dma("sp", yg[:].rearrange("p a w -> p a w"),
                        ygT[:, t0:t0 + TW2].rearrange("(a p) w -> p a w", p=128), deps=[pr.stf.get("yg")])
            hxn_ = pr.dma("sp", xnt[:], xn1T[:, t0:t0 + TW2].rearrange("(a p) w -> p a w", p=128),
                          deps=[pr.stf.get("xn")])
            if t0 == 0:
                pr.op("dve", lambda v: v.memset(vt[:, :, 0:1], 0.0), deps=[hv], tick=False)
            for a in range(NCH):
                if t0 + TW2 + 1 > T:
                    pr.op("dve", lambda v, a=a: v.memset(vt[:, a, TW2 + 1:TW2 + 2], 0.0), deps=[hv], tick=False)
                pr.op("dve", lambda v, a=a: v.scalar_tensor_tensor(
                    yg[:, a, :], vt[:, a, 0:TW2], dwcs[:, a, 0:1], yg[:, a, :], ALU.mult, ALU.add),
                    deps=[hv, hy], tick=False)
                pr.op("dve", lambda v, a=a: v.scalar_tensor_tensor(
                    yg[:, a, :], vt[:, a, 1:TW2 + 1], dwcs[:, a, 1:2], yg[:, a, :], ALU.mult, ALU.add), tick=False)
                pr.op("dve", lambda v, a=a: v.scalar_tensor_tensor(
                    yg[:, a, :], vt[:, a, 2:TW2 + 2], dwcs[:, a, 2:3], yg[:, a, :], ALU.mult, ALU.add), tick=False)
                pr.op("dve", lambda v, a=a: v.tensor_scalar(
                    yg[:, a, :], yg[:, a, :], dwcs[:, a, 3:4], None, ALU.add), tick=False)
                pr.op("dve", lambda v, a=a: v.tensor_copy(ygb[:, a, :], yg[:, a, :]), tick=(a == NCH - 1))
            hyg = pr.last("dve")
            hyd = pr.dma("sp", o_yl[:, t0:t0 + TW2].rearrange("(a p) w -> p a w", p=128), yg[:], deps=[hyg])
            hyg = hyd
            pr.stf["yg"] = hyg
            pr.stf["vt"] = hyg
            for a in range(NCH):
                pr.op("dve", lambda v, a=a: v.tensor_copy(xnb[:, a, :], xnt[:, a, :]),
                      deps=[hxn_], tick=(a == NCH - 1))
            hxnb = pr.last("dve")
            pr.stf["xn"] = hxnb
            for a in range(NCH):
                hl1 = pr.op("pe", lambda t, a=a: t.matmul(ps.t[6][0:16, 0:TW2], fab[:, a, :], xnb[:, a, :],
                                                          start=(a == 0), stop=(a == NCH - 1)),
                            deps=[hxnb, hc5, ps.free[6]] if a == 0 else (), tick=(a == NCH - 1))
            ps.data[6] = hl1
            pr.op("dve", lambda v: v.tensor_copy(lorab[0:16, :], ps.t[6][0:16, 0:TW2]), deps=[hl1], tick=False)
            hl2 = pr.op("dve", lambda v: v.tensor_scalar(lorab[0:16, :], lorab[0:16, :], ascl[0:16, :],
                                                         None, ALU.mult))
            ps.free[6] = hl2
            for m in range(NCH):
                bs = m % 2
                bf = 2 + m % 2
                for a in range(NCH):
                    hpm = pr.op("pe", lambda t, m=m, a=a, bs=bs: t.matmul(
                        ps.t[bs][:, 0:TW2], PwTb[:, a, 128 * m:128 * (m + 1)], ygb[:, a, :],
                        start=(a == 0), stop=(a == NCH - 1)),
                        deps=[hyg, hc1, ps.free[bs]] if a == 0 else (), tick=(a == NCH - 1))
                ps.data[bs] = hpm
                for a in range(NCH):
                    hfm = pr.op("pe", lambda t, m=m, a=a, bf=bf, nolora=nolora: t.matmul(
                        ps.t[bf][:, 0:TW2], Fkq[:, a, 128 * m:128 * (m + 1)], xnb[:, a, :],
                        start=(a == 0), stop=(nolora and a == NCH - 1)),
                        deps=[hqdone, hxnb, ps.free[bf]] if a == 0 else (), tick=(nolora and a == NCH - 1))
                if not nolora:
                    hfm = pr.op("pe", lambda t, m=m, bf=bf: t.matmul(
                        ps.t[bf][:, 0:TW2], fbbb[:, 128 * m:128 * (m + 1)], lorab[:, :],
                        start=False, stop=True), deps=[hl2, hc6])
                ps.data[bf] = hfm
                sl = m % 2
                hsev = pr.op("dve", lambda v, m=m, bs=bs, sl=sl: v.tensor_scalar(
                    sev[:, sl, :], ps.t[bs][:, 0:TW2], pbs[:, m, 0:1], None, ALU.add),
                    deps=[hpm, hc3, pr.stf.get(("sev", sl))])
                pr.stf[("sev", sl)] = pr.dma("sp", o_s1[128 * m:128 * (m + 1), t0:t0 + TW2], sev[:, sl, :],
                                             deps=[hsev])
                ps.free[bs] = hsev
                pr.stf[("sev", sl)] = hsev
                hfev = pr.op("dve", lambda v, m=m, bf=bf, sl=sl: v.tensor_scalar(
                    fev[:, sl, :], ps.t[bf][:, 0:TW2], sscl[:, 0:1], fbs[:, m, 0:1], ALU.mult, ALU.add),
                    deps=[hfm, hc4, hq6, pr.stf.get(("fev", sl))])
                hfd = pr.dma("sp", o_f[128 * m:128 * (m + 1), t0:t0 + TW2], fev[:, sl, :], deps=[hfev])
                ps.free[bf] = hfev
                pr.stf[("fev", sl)] = hfev
                hx1_ = pr.dma("sp", x1t[:, sl, :], x1T[128 * m:128 * (m + 1), t0:t0 + TW2],
                              deps=[pr.stf.get(("x1", sl))])
                hsl_ = pr.op("act", lambda s, sl=sl: s.activation(fev[:, sl, :], fev[:, sl, :], AF.Silu,
                                                                  bias=0.0, scale=1.0), deps=[hfev, hfd],
                             selfsync=False)
                pr.op("dve", lambda v, sl=sl: v.tensor_tensor(fev[:, sl, :], fev[:, sl, :], sev[:, sl, :],
                                                              ALU.mult), deps=[hsl_], tick=False)
                hout = pr.op("dve", lambda v, sl=sl: v.tensor_tensor(fev[:, sl, :], fev[:, sl, :], x1t[:, sl, :],
                                                                     ALU.add), deps=[hx1_])
                hos = pr.dma("sp", o_out[128 * m:128 * (m + 1), t0:t0 + TW2], fev[:, sl, :], deps=[hout])
                pr.stf[("x1", sl)] = hos
                pr.stf[("fev", sl)] = hos
        if standalone:
            pr.finalize([pr.lastd("sp"), pr.lastd("pool"), pr.last("dve"), pr.last("pe"), pr.last("act")])
    return pr



def build_launch5(h0, h1, NJT=32, NI=None, pr=None):
    standalone = pr is None
    if standalone:
        pr = Prog("l5")
    _scope5 = f"l5_{h0}_"
    qTb = pr.din("qTb", [C, T // 2], U32)
    Ktm = pr.din("Ktm", [H, T, DH // 2], U32)
    Qtm = pr.din("Qtm", [H, T, DH // 2], U32)
    Vtm = pr.din("Vtm", [H, T, DH // 2], U32)
    cid = pr.din("cid", [128, 128])
    o_yg = pr.dout("ygT", [C, T])
    FI = 1024
    NI = NI or (T // FI)
    LAM = 4.0
    ACT_D = 14

    with (pr.es if standalone else pr.scope(_scope5)):
        onesrow_f = pr.sbuf("onesrow_f", [1, 128], F32)
        onesrow_b = pr.sbuf("onesrow_b", [1, 128], BF16)
        idb = pr.sbuf("idb", [128, 128], BF16)
        idb32 = pr.sbuf("idb32", [128, 128], F32)
        qrow = pr.sbuf("qrow", [1, 2, FI], BF16)
        qreps = pr.sbuf("qreps", [128, 64, FI], BF16)
        Km = pr.sbuf("Km", [128, NJT, DH], BF16)
        K32 = pr.sbuf("K32", [128, NJT, DH], F32)
        K32n = pr.sbuf("K32n", [128, NJT, DH], F32)
        Qm = pr.sbuf("Qm", [128, NJT, DH], BF16)
        Vm = pr.sbuf("Vm", [128, NJT, DH + 1], BF16)
        Sk = pr.sbuf("Sk", [128, NJT, 32], F32)
        nSk = pr.sbuf("nSk", [128, NJT], F32)
        Sq = pr.sbuf("Sq", [128, NJT, 32], F32)
        epsb = pr.sbuf("epsb", [128, NJT], F32)
        relu_t = pr.sbuf("relu_t", [128, 4, FI], BF16)
        kernb = pr.sbuf("kernb", [128, 2, FI], BF16)
        outi = pr.sbuf("outi", [128, 8, DH + 1], F32)
        den = pr.sbuf("den", [128, 1], F32)
        lnepsb = pr.sbuf("lnepsb", [128, 1], F32)
        ytile = pr.sbuf("ytile", [128, 2, DH], F32)
        tev = pr.sbuf("tev", [64, 2, 128], F32)
        ps = Banks([pr.psum(f"ps{i}", [128, 512]) for i in range(4)])
        psA = pr.psum("psA", [128, 2048])

        hcc = pr.dma("pool", idb[:], cid)
        hc2 = pr.dma("sp", idb32[:], cid)
        h0_ = pr.op("dve", lambda v: v.memset(onesrow_f[:], 1.0), deps=[hcc, hc2])
        h0b = pr.op("dve", lambda v: v.tensor_copy(onesrow_b[:], onesrow_f[:]))
        pr.op("dve", lambda v: v.memset(lnepsb[:], float(np.log(1e-6))), tick=False)
        pr.stf = {}
        hjoin = h0b

        for h in range(h0, h1):
            hK = pr.dma("pool", Km[:], Ktm[h].bitcast(BF16).rearrange("(a p) d -> p a d", p=128), deps=[hjoin])
            hK2 = pr.op("dve", lambda v: v.tensor_copy(K32[:].rearrange("p a d -> p (a d)"),
                                                       Km[:].rearrange("p a d -> p (a d)")), deps=[hK, hjoin])
            hK3 = pr.op("dve", lambda v: v.tensor_scalar(K32n[:].rearrange("p a d -> p (a d)"),
                                                         K32[:].rearrange("p a d -> p (a d)"),
                                                         -1.0, None, ALU.mult), deps=[hK2])
            hQm = pr.dma("pool", Qm[:], Qtm[h].bitcast(BF16).rearrange("(a p) d -> p a d", p=128), deps=[hjoin])
            hV = pr.dma("pool", Vm[:, :, 0:DH], Vtm[h].bitcast(BF16).rearrange("(a p) d -> p a d", p=128),
                        deps=[hjoin])
            hV1 = pr.op("dve", lambda v: v.memset(Vm[:, :, DH:DH + 1], 1.0), deps=[hV])
            hs = pr.op("dve", lambda v: v.tensor_tensor(Sk[:], K32[:, :, 0:32], K32[:, :, 32:64], ALU.add),
                       deps=[hK2])
            w = 32
            while w > 1:
                w //= 2
                hs = pr.op("dve", lambda v, w=w: v.tensor_tensor(Sk[:, :, 0:w], Sk[:, :, 0:w], Sk[:, :, w:2 * w],
                                                                 ALU.add), deps=[hs])
            hsk = pr.op("dve", lambda v: v.tensor_scalar(nSk[:], Sk[:, :, 0], -LAM / DH, None, ALU.mult),
                        deps=[hs])
            hq32 = pr.op("dve", lambda v: v.tensor_copy(Sq[:, :, 0:32], Qm[:, :, 0:32]), deps=[hQm])
            hq3 = pr.op("dve", lambda v: v.tensor_tensor(Sq[:, :, 0:32], Sq[:, :, 0:32], Qm[:, :, 32:64], ALU.add),
                        deps=[hq32])
            w = 32
            while w > 1:
                w //= 2
                hq3 = pr.op("dve", lambda v, w=w: v.tensor_tensor(Sq[:, :, 0:w], Sq[:, :, 0:w], Sq[:, :, w:2 * w],
                                                                  ALU.add), deps=[hq3])
            heps = pr.op("act", lambda s: s.activation(epsb[:], Sq[:, :, 0], AF.Exp,
                                                       bias=lnepsb[:], scale=LAM / DH), deps=[hq3])
            for ic in range(NI):
                i0 = ic * FI
                for d in range(DH):
                    sl = d % 2
                    bq = 2 + sl
                    hr = pr.dma("sp", qrow[:, sl, :],
                                qTb[64 * h + d:64 * h + d + 1, i0 // 2:(i0 + FI) // 2].bitcast(BF16),
                                deps=[pr.stf.get(("qrow", sl))])
                    for half in range(2):
                        hm = pr.op("pe", lambda t, sl=sl, bq=bq, half=half: t.matmul(
                            ps.t[bq][:, 0:512], onesrow_b[:], qrow[0:1, sl, 512 * half:512 * (half + 1)],
                            start=True, stop=True),
                            deps=[hr, h0b, ps.free[bq], pr.stf.get("qrepG")])
                        ps.data[bq] = hm
                        hev = pr.op("dve", lambda v, d=d, bq=bq, half=half: v.tensor_copy(
                            qreps[:, d, 512 * half:512 * (half + 1)], ps.t[bq][:, 0:512]),
                            deps=[hm])
                        ps.free[bq] = hev
                    pr.stf[("qrow", sl)] = hev
                hqrall = pr.last("dve")
                for jt in range(NJT):
                    accslot = jt % 2
                    for d in range(DH):
                        rs = d % 4
                        if d % 64 < ACT_D:
                            hrl = pr.op("act", lambda s, d=d, jt=jt, rs=rs: s.activation(
                                relu_t[:, rs, :], qreps[:, d, :], AF.Relu,
                                bias=K32n[:, jt, d:d + 1], scale=1.0),
                                deps=[hqrall, hK3, pr.stf.get(("rt", rs))], selfsync=False)
                        else:
                            hrl = pr.op("dve", lambda v, d=d, jt=jt, rs=rs: v.tensor_scalar(
                                relu_t[:, rs, :], qreps[:, d, :], K32[:, jt, d:d + 1], 0.0,
                                ALU.subtract, ALU.max),
                                deps=[hqrall, pr.stf.get(("rt", rs))], selfsync=False)
                        for half in range(2):
                            hax = pr.op("pe", lambda t, d=d, rs=rs, half=half, accslot=accslot: t.matmul(
                                psA[:, 512 * (2 * accslot + half):512 * (2 * accslot + half) + 512],
                                idb[:], relu_t[:, rs, 512 * half:512 * (half + 1)],
                                start=(d == 0), stop=(d == DH - 1)),
                                deps=([hrl] + ([pr.stf.get(("acc", accslot))] if d == 0 else [])),
                                tick=(half == 1), selfsync=False)
                        pr.stf[("rt", rs)] = hax
                    hacc = pr.last("pe")
                    ks = jt % 2
                    hkn = []
                    for half in range(2):
                        hkn.append(pr.op("act", lambda s, jt=jt, ks=ks, half=half, accslot=accslot: s.activation(
                            kernb[:, ks, 512 * half:512 * (half + 1)],
                            psA[:, 512 * (2 * accslot + half):512 * (2 * accslot + half) + 512],
                            AF.Exp, bias=nSk[:, jt:jt + 1], scale=-2.0 * LAM / DH),
                            deps=[hacc, hsk, pr.stf.get(("kern", ks))], selfsync=False))
                    pr.stf[("acc", accslot)] = hkn[-1]
                    for isl in range(8):
                        bo = isl % 2
                        him = pr.op("pe", lambda t, jt=jt, ks=ks, isl=isl, bo=bo: t.matmul(
                            ps.t[bo][:, 0:DH + 1], kernb[:, ks, 128 * isl:128 * (isl + 1)],
                            Vm[:, jt, :], start=True, stop=True),
                            deps=[hkn[isl // 4], hV1, ps.free[bo]], selfsync=False)
                        ps.data[bo] = him
                        if jt == 0:
                            hadd = pr.op("dve", lambda v, isl=isl, bo=bo: v.tensor_copy(
                                outi[:, isl, :], ps.t[bo][:, 0:DH + 1]),
                                deps=[him, pr.stf.get(("outi", isl))])
                        else:
                            hadd = pr.op("dve", lambda v, isl=isl, bo=bo: v.tensor_tensor(
                                outi[:, isl, :], outi[:, isl, :], ps.t[bo][:, 0:DH + 1], ALU.add),
                                deps=[him])
                        ps.free[bo] = hadd
                        if isl == 7:
                            pr.stf[("kern", ks)] = hadd
                for isl in range(8):
                    gi = ic * 8 + isl
                    hd1 = pr.op("dve", lambda v, isl=isl, gi=gi: v.tensor_tensor(
                        den[:], outi[:, isl, DH:DH + 1], epsb[:, gi:gi + 1], ALU.add), deps=[heps])
                    hd2 = pr.op("dve", lambda v: v.reciprocal(den[:], den[:]), deps=[hd1])
                    hd3 = pr.op("dve", lambda v, isl=isl: v.tensor_scalar(
                        ytile[:, isl % 2, :], outi[:, isl, 0:DH], den[:], None, ALU.mult),
                        deps=[hd2, pr.stf.get(("yt", isl % 2))])
                    pr.stf[("outi", isl)] = hd3
                    hT = pr.op("pe", lambda t, isl=isl: t.transpose(ps.t[0][0:64, 0:128], ytile[:, isl % 2, :],
                                                                   idb32[:]),
                               deps=[hd3, ps.free[0], hc2])
                    ps.data[0] = hT
                    hE = pr.op("dve", lambda v, isl=isl: v.tensor_copy(tev[:, isl % 2, :], ps.t[0][0:64, 0:128]),
                               deps=[hT, pr.stf.get(("tev", isl % 2))])
                    ps.free[0] = hE
                    pr.stf[("yt", isl % 2)] = hE
                    hD = pr.dma("sp", o_yg[64 * h:64 * (h + 1), i0 + 128 * isl:i0 + 128 * (isl + 1)],
                                tev[:, isl % 2, :], deps=[hE])
                    pr.stf[("tev", isl % 2)] = hD
                pr.stf["qrepG"] = pr.tick("dve", deps=[pr.last("pe"), pr.last("act")])
            hjoin = pr.tick("dve", deps=[pr.last("pe"), pr.last("act"), pr.lastd("sp"), pr.lastd("pool")])

        if standalone:
            pr.finalize([pr.lastd("sp"), pr.lastd("pool"), pr.last("dve"), pr.last("pe"), pr.last("act")])
    return pr



_PROGS = {}


def build_fused():
    pr = Prog("fused")
    pr.io = {}
    build_launch1(pr=pr)
    pr.barrier()
    build_launch2(pr=pr)
    pr.barrier()
    build_launch3(pr=pr)
    pr.barrier()
    build_launch4(pr=pr)
    pr.barrier()
    build_launch5(0, 8, pr=pr)
    pr.barrier()
    build_launch5(8, 16, pr=pr)
    pr.barrier()
    build_launch6(pr=pr)
    pr.finalize([pr.last(q) for q in pr.QUEUES] +
                [pr.lastd(q) for q in pr.QUEUES if (q + "d") in pr.cnt])
    return pr


def _progs():
    if "fused" not in _PROGS:
        pr = build_fused()
        _PROGS["fused"] = pr.build()
    return _PROGS


def kernel(**inp):
    f32 = np.float32
    nc = _progs()["fused"]
    NCORES = 8
    x = np.asarray(inp["L0_attn_x"], f32)

    eye = np.eye(128, dtype=f32)
    ones = np.ones((128, 1), f32)
    maa = np.ascontiguousarray(np.stack([np.asarray(inp[f"L0_attn_maa_{c}"], f32).reshape(-1)
                                         for c in "xkvr"], 1))
    w2s = np.ascontiguousarray(np.asarray(inp["L0_attn_w2"], f32).reshape(96, 1024))
    cmask = np.ones((128, 1024), f32)
    cmask[:, ::64] = 0.0
    tri = (np.arange(64)[:, None] < np.arange(64)[None, :]).astype(f32)
    ctri = np.ascontiguousarray(np.concatenate([tri, tri], 0))
    cb2 = np.zeros((128, 128), f32)
    cb2[0:64, 0] = 1.0
    cb2[64:128, 64] = 1.0
    csel = np.ascontiguousarray(np.stack(
        [np.tile((np.arange(8) == e)[:, None].astype(f32), (1, 128)) for e in range(8)], 1))
    cones2 = np.ones((128, 64), f32)
    WqT = np.ascontiguousarray(np.asarray(inp["L1_attn_qkv_w"], f32).T)
    PwT = np.ascontiguousarray(np.asarray(inp["L1_attn_proj_w"], f32).T)
    dwcw = np.asarray(inp["L1_attn_dwc_w"], f32)
    dwcc = np.ascontiguousarray(np.concatenate(
        [dwcw[:, 0, :], np.asarray(inp["L1_attn_dwc_b"], f32)[:, None]], 1))
    fbbp = np.ascontiguousarray(np.concatenate(
        [np.asarray(inp["L1_ffn_b"], f32), np.zeros((112, 1024), f32)], 0))

    shared = {
        "maa": maa, "w1": np.asarray(inp["L0_attn_w1"], f32), "w2s": w2s,
        "Wr": np.asarray(inp["L0_attn_Wr"], f32), "Wk": np.asarray(inp["L0_attn_Wk"], f32),
        "Wvg": np.asarray(inp["L0_attn_Wvg"], f32), "cones": ones,
        "Wo": np.asarray(inp["L0_attn_Wo"], f32),
        "uvec": np.asarray(inp["L0_attn_u"], f32).reshape(1024, 1),
        "lng": np.asarray(inp["L0_attn_ln_g"], f32).reshape(1024, 1),
        "lnb": np.asarray(inp["L0_attn_ln_b"], f32).reshape(1024, 1),
        "cmask": cmask, "ctri": ctri, "cb2": cb2, "cones2": cones2, "cid": eye,
        "rg": np.asarray(inp["L0_ffn_router_gate"], f32),
        "gw": np.asarray(inp["L0_ffn_gate_w"], f32), "uw": np.asarray(inp["L0_ffn_up_w"], f32),
        "dw": np.asarray(inp["L0_ffn_down_w"], f32), "csel": csel,
        "WqT": WqT, "qkvb": np.asarray(inp["L1_attn_qkv_b"], f32).reshape(-1, 1),
        "dwc": dwcc, "PwT": PwT,
        "pb": np.asarray(inp["L1_attn_proj_b"], f32).reshape(-1, 1),
        "Fk": np.asarray(inp["L1_ffn_kernel"], f32),
        "fb": np.asarray(inp["L1_ffn_bias"], f32).reshape(-1, 1),
        "fa": np.asarray(inp["L1_ffn_a"], f32), "fbb": fbbp,
    }
    maps = [dict(shared, xT=np.ascontiguousarray(x[b].T)) for b in range(NCORES)]
    res = run_bass_kernel_spmd(nc, maps, core_ids=list(range(NCORES)))
    out = np.empty((NCORES, T, C), f32)
    for b in range(NCORES):
        out[b] = np.asarray(res.results[b]["outT"], f32).T
    return out
